# version 08
# baseline (speedup 1.0000x reference)
import numpy as np
from contextlib import ExitStack

import concourse.bass as bass
import concourse.mybir as mybir
from concourse.bass_utils import run_bass_kernel_spmd

F32 = mybir.dt.float32
BF16 = mybir.dt.bfloat16
AX = mybir.AxisListType.X
ALU = mybir.AluOpType
AF = mybir.ActivationFunctionType

D = 1024
H = 16
E = 64
M = 64
INNER = 16
EPS = 1e-5
SCALE = 0.125


def build_nc(NK=8192, NQ=2048, debug=False, phases=3, p2_heads=None, p2_stub=0):
    NTK = NK // 128
    NGK = NK // 512
    NTQ = NQ // 128
    NGQ = max(1, NQ // 512)
    NQB = max(1, NQ // 512)
    NG = NTK // 2
    NPAIR = NTK // 2
    NVD = max(1, NTK // 8)
    P2H = H if p2_heads is None else p2_heads

    nc = bass.Bass("TRN2", target_bir_lowering=False,
                   detect_race_conditions=False)

    xbh = nc.dram_tensor("xb", [NK, D], F32, kind="ExternalInput")
    Wqh = nc.dram_tensor("Wq", [H, D, E], F32, kind="ExternalInput")
    Wkh = nc.dram_tensor("Wk", [H, D, E], F32, kind="ExternalInput")
    Wvh = nc.dram_tensor("Wv", [H, D, E], F32, kind="ExternalInput")
    Wph = nc.dram_tensor("Wp", [D, D], F32, kind="ExternalInput")
    bph = nc.dram_tensor("bp", [D], F32, kind="ExternalInput")
    Winh = nc.dram_tensor("Win", [D, M], F32, kind="ExternalInput")
    binh = nc.dram_tensor("binp", [M], F32, kind="ExternalInput")
    W1h = nc.dram_tensor("W1", [M, INNER], F32, kind="ExternalInput")
    b1h = nc.dram_tensor("b1", [M, INNER], F32, kind="ExternalInput")
    W2h = nc.dram_tensor("W2", [M, INNER], F32, kind="ExternalInput")
    b2h = nc.dram_tensor("b2", [M], F32, kind="ExternalInput")
    Wouth = nc.dram_tensor("Wout", [M, D], F32, kind="ExternalInput")
    bouth = nc.dram_tensor("bout", [D], F32, kind="ExternalInput")
    outh = nc.dram_tensor("out", [NQ, D], F32, kind="ExternalOutput")

    kTh = nc.dram_tensor("kT_scr", [D, NK], BF16, kind="Internal")
    vhh = nc.dram_tensor("v_scr", [NK, D], BF16, kind="Internal")
    dbgh = {}
    if debug:
        for nm, shp in (("d_kT", [128, 512]), ("d_v", [128, 512]),
                        ("d_qT", [128, 512]), ("d_crep", [128, 16]),
                        ("d_attc", [128, 512]), ("d_kmax", [16, 2]),
                        ("d_cq", [16, 4 * 512]), ("d_ck", [16, 512]),
                        ("d_q", [128, 512])):
            dbgh[nm] = nc.dram_tensor(nm, shp, F32, kind="ExternalOutput")

    xb = xbh.ap()
    kT = kTh.ap()
    vh = vhh.ap()

    ctx = ExitStack()

    def sb(name, shape, dt):
        return ctx.enter_context(nc.sbuf_tensor(name, shape, dt))

    def sem(name):
        return ctx.enter_context(nc.semaphore(name))

    identb = sb("identb", [128, 128], BF16)
    esel = sb("esel", [128, 8 * 16], BF16)
    if debug:
        dbg_cq = sb("dbg_cq", [16, 4 * 512], F32)
        dbg_ck = sb("dbg_ck", [16, 512], F32)
        dbg_q = sb("dbg_q", [128, 512], F32)
    qT = sb("qT", [128, 8 * NQ], BF16)
    kmax2 = sb("kmax2", [16, 1], F32)
    qmax2 = sb("qmax2", [16, 1], F32)
    kgmax = sb("kgmax", [16, NGK], F32)
    qgmax = sb("qgmax", [16, max(1, NGQ)], F32)
    crep = sb("crep", [128, 16], F32)

    s_w = sem("s_w")
    s_x = sem("s_x"); s_vS = sem("s_vS"); s_sS = sem("s_sS")
    s_vLN = sem("s_vLN"); s_tTP = sem("s_tTP"); s_vTP = sem("s_vTP")
    s_tK = sem("s_tK"); s_vK = sem("s_vK"); s_tV = sem("s_tV"); s_vV = sem("s_vV")
    s_tQ = sem("s_tQ"); s_vQ = sem("s_vQ")
    s_vSQ = sem("s_vSQ"); s_tCK = sem("s_tCK"); s_vCK = sem("s_vCK")
    s_vQS = sem("s_vQS"); s_tCQ = sem("s_tCQ"); s_vCQ = sem("s_vCQ")
    s_spK = sem("s_spK"); s_spV = sem("s_spV"); s_tG = sem("s_tG")
    s_c = sem("s_c"); s_cs = sem("s_cs"); s_crep = sem("s_crep")
    s_k = sem("s_k"); s_v2 = sem("s_v2"); s_q2 = sem("s_q2")
    s_tLp = sem("s_tLp"); s_tL = sem("s_tL")
    s_exp = sem("s_exp"); s_tAVg = sem("s_tAVg"); s_tAVq = sem("s_tAVq")
    s_vrec = sem("s_vrec"); s_vrec0 = sem("s_vrec0")
    s_rrA = sem("s_rrA"); s_rrB = sem("s_rrB")
    s_rgA = sem("s_rgA"); s_rgB = sem("s_rgB"); s_at2 = sem("s_at2")
    s_vAT = sem("s_vAT")
    s_w2 = sem("s_w2")
    s3_x = sem("s3_x"); s3_t = sem("s3_t"); s3_v = sem("s3_v"); s3_s = sem("s3_s")
    s3_c = sem("s3_c"); s3_g = sem("s3_g")
    s_out = sem("s_out")

    p1 = ExitStack()

    def sb1(name, shape, dt):
        return p1.enter_context(nc.sbuf_tensor(name, shape, dt))

    def ps1(name, shape, dt):
        return p1.enter_context(nc.psum_tensor(name, shape, dt))

    Wq_sb = sb1("Wq_sb", [128, 8 * D], BF16)
    Wk_sb = sb1("Wk_sb", [128, 8 * D], BF16)
    Wv_sb = sb1("Wv_sb", [128, 8 * D], BF16)
    x0t = [sb1(f"x0t{i}", [128, D], F32) for i in range(3)]
    sqv = sb1("sqv", [128, D], F32)
    ln1t = [sb1(f"ln1t{i}", [128, D], BF16) for i in range(2)]
    ln1Tg = [sb1(f"ln1Tg{i}", [128, 8 * 512], BF16) for i in range(2)]
    kTsp = [sb1(f"kTsp{i}", [128, 8 * 512], BF16) for i in range(2)]
    vsp = [sb1(f"vsp{i}", [128, 4 * D], BF16) for i in range(2)]
    ksq = [sb1(f"ksq{i}", [128, 8 * 512], BF16) for i in range(2)]
    qsq = [sb1(f"qsq{i}", [128, 8 * 512], BF16) for i in range(2)]
    st_s1 = sb1("st_s1", [128, 1], F32)
    st_s2 = sb1("st_s2", [128, 1], F32)
    st_m = sb1("st_m", [128, 1], F32)
    st_t = sb1("st_t", [128, 1], F32)
    st_rv = sb1("st_rv", [128, 1], F32)
    st_rs = sb1("st_rs", [128, 1], F32)
    cmx = sb1("cmx", [16, 1], F32)

    ptp = [ps1(f"ptp{i}", [128, 1024], BF16) for i in range(2)]
    kps = [ps1(f"kps{i}", [128, 512], F32) for i in range(2)]
    vps = ps1("vps", [128, 1024], F32)
    cps = ps1("cps", [128, 512], F32)
    cpsq = ps1("cpsq", [128, 512], F32)

    NW1 = 24

    with nc.Block() as b0:
        @b0.gpsimd
        def _(g):
            for dc in range(8):
                for (Wh, Wsb) in ((Wqh, Wq_sb), (Wkh, Wk_sb), (Wvh, Wv_sb)):
                    g.dma_start(
                        out=Wsb[:, dc * D:(dc + 1) * D],
                        in_=bass.AP(Wh, dc * 128 * E, [[E, 128], [D * E, H], [1, E]]),
                    ).then_inc(s_w, 16)
            g.memset(identb[:, :], 1.0)
            g.affine_select(out=identb[:, :], in_=identb[:, :],
                            compare_op=ALU.is_equal, fill=0.0, base=0,
                            pattern=[[-1, 128]], channel_multiplier=1)

        @b0.vector
        def _(v):
            v.memset(esel[:, :], 0.0)
            for dc in range(8):
                v.memset(esel[0:64, dc * 16 + 2 * dc:dc * 16 + 2 * dc + 1], 1.0)
                v.memset(esel[64:128, dc * 16 + 2 * dc + 1:dc * 16 + 2 * dc + 2], 1.0)
            v.memset(kmax2[:, :], 0.0)
            v.memset(qmax2[:, :], 0.0)

    with nc.Block() as b1:
        @b1.sync
        def _(s):
            def spill(gp):
                s.wait_ge(s_vK, 8 * gp + 8)
                for hc in range(8):
                    s.dma_start(
                        out=kT[hc * 128:(hc + 1) * 128, gp * 512:(gp + 1) * 512],
                        in_=kTsp[gp % 2][:, hc * 512:(hc + 1) * 512],
                    ).then_inc(s_spK, 16)
                s.wait_ge(s_vV, 4 * gp + 4)
                for mt in range(4):
                    s.dma_start(
                        out=vh[(4 * gp + mt) * 128:(4 * gp + mt + 1) * 128, :],
                        in_=vsp[gp % 2][:, mt * D:(mt + 1) * D],
                    ).then_inc(s_spV, 16)

            for g in range(NGK):
                spill(g)

        @b1.vector
        def _(v):
            for g in range(NGK):
                for tq in range(4):
                    t = 4 * g + tq
                    v.wait_ge(s_x, 16 * (t + 1))
                    x = x0t[t % 3]
                    v.reduce_sum(st_s1[:, :], x[:, :], axis=AX)
                    v.tensor_mul(sqv[:, :], x[:, :], x[:, :])
                    v.reduce_sum(st_s2[:, :], sqv[:, :], axis=AX)
                    v.tensor_scalar_mul(st_m[:, :], st_s1[:, :], 1.0 / D)
                    v.tensor_scalar_mul(st_t[:, :], st_s2[:, :], 1.0 / D)
                    v.tensor_mul(st_s1[:, :], st_m[:, :], st_m[:, :])
                    v.tensor_sub(st_t[:, :], st_t[:, :], st_s1[:, :])
                    v.tensor_scalar_add(st_t[:, :], st_t[:, :],
                                        EPS).then_inc(s_vS, 1)
                    v.wait_ge(s_sS, t + 1)
                    v.tensor_scalar(ln1t[t % 2][:, :], x[:, :], st_m[:, :],
                                    st_rs[:, :], ALU.subtract,
                                    ALU.mult).then_inc(s_vLN, 1)
                    if tq == 0 and g >= 2:
                        if g - 2 < NGQ:
                            v.wait_ge(s_tQ, 8 * (g - 1))
                        else:
                            v.wait_ge(s_tV, 4 * (g - 1))
                    for dc in range(8):
                        v.wait_ge(s_tTP, 8 * t + dc + 1)
                        v.tensor_copy(
                            ln1Tg[g % 2][:, dc * 512 + tq * 128:dc * 512 + tq * 128 + 128],
                            ptp[dc % 2][:, 0:128]).then_inc(s_vTP, 1)
                if g >= 2:
                    v.wait_ge(s_spK, 16 * 8 * (g - 1))
                    v.wait_ge(s_spV, 16 * 4 * (g - 1))
                for hc in range(8):
                    KI = 8 * g + hc
                    v.wait_ge(s_tK, KI + 1)
                    v.tensor_copy(kTsp[g % 2][:, hc * 512:(hc + 1) * 512],
                                  kps[KI % 2][:, :]).then_inc(s_vK, 1)
                    v.tensor_mul(ksq[g % 2][:, hc * 512:(hc + 1) * 512],
                                 kTsp[g % 2][:, hc * 512:(hc + 1) * 512],
                                 kTsp[g % 2][:, hc * 512:(hc + 1) * 512]
                                 ).then_inc(s_vSQ, 1)
                for mt in range(4):
                    VI = 4 * g + mt
                    v.wait_ge(s_tV, VI + 1)
                    v.tensor_copy(vsp[g % 2][:, mt * D:(mt + 1) * D],
                                  vps[:, :]).then_inc(s_vV, 1)
                if g >= 2:
                    v.wait_ge(s_tCK, g - 1)
                    if debug and g == 2:
                        v.tensor_copy(dbg_ck[:, :], cps[0:16, :])
                    v.reduce_max(kgmax[:, g - 2:g - 1], cps[0:16, :],
                                 axis=AX).then_inc(s_vCK, 1)
                if 2 <= g and g - 2 < NGQ:
                    v.wait_ge(s_tCQ, g - 1)
                    if debug:
                        v.tensor_copy(dbg_cq[:, (g - 2) * 512:(g - 1) * 512],
                                      cpsq[0:16, :])
                    v.reduce_max(qgmax[:, g - 2:g - 1], cpsq[0:16, :],
                                 axis=AX).then_inc(s_vCQ, 1)
            v.wait_ge(s_tCK, NGK - 1)
            v.reduce_max(kgmax[:, NGK - 2:NGK - 1], cps[0:16, :],
                         axis=AX).then_inc(s_vCK, 1)
            v.wait_ge(s_tCK, NGK)
            v.reduce_max(kgmax[:, NGK - 1:NGK], cps[0:16, :],
                         axis=AX).then_inc(s_vCK, 1)

        @b1.scalar
        def _(sc):
            def xload(t):
                sc.dma_start(out=x0t[t % 3][:, :],
                             in_=xb[t * 128:(t + 1) * 128, :]).then_inc(s_x, 16)

            for t in range(3):
                xload(t)
            for t in range(NTK):
                sc.wait_ge(s_vS, t + 1)
                sc.activation(out=st_rv[:, :], in_=st_t[:, :], func=AF.Ln)
                sc.activation(out=st_rs[:, :], in_=st_rv[:, :],
                              func=AF.Exp, scale=-0.5).then_inc(s_sS, 1)
                if t + 3 < NTK:
                    sc.wait_ge(s_vLN, t + 1)
                    xload(t + 3)
                if t % 4 == 0:
                    gq = t // 4 - 1
                    if 0 <= gq < NGQ:
                        for hc in range(8):
                            QI = 8 * gq + hc
                            sc.wait_ge(s_tQ, QI + 1)
                            sc.activation(
                                out=qT[:, hc * NQ + gq * 512:hc * NQ + gq * 512 + 512],
                                in_=kps[QI % 2][:, :], func=AF.Copy,
                                scale=SCALE).then_inc(s_vQ, 1)
                            sc.activation(
                                out=qsq[gq % 2][:, hc * 512:(hc + 1) * 512],
                                in_=kps[QI % 2][:, :], func=AF.Square,
                                scale=SCALE).then_inc(s_vQS, 1)

        @b1.tensor
        def _(tn):
            tn.wait_ge(s_w, 16 * NW1)
            for g in range(NGK):
                for tq in range(4):
                    t = 4 * g + tq
                    tn.wait_ge(s_vLN, t + 1)
                    for dc in range(8):
                        i = 8 * t + dc
                        if i >= 2:
                            tn.wait_ge(s_vTP, i - 1)
                        tn.transpose(ptp[dc % 2][:, 0:128],
                                     ln1t[t % 2][:, dc * 128:(dc + 1) * 128],
                                     identb[:, :]).then_inc(s_tTP, 1)
                tn.wait_ge(s_vTP, 8 * (4 * g + 4))
                for hc in range(8):
                    KI = 8 * g + hc
                    if KI >= 2:
                        tn.wait_ge(s_vK, KI - 1)
                    if g >= 1 and g - 1 < NGQ:
                        tn.wait_ge(s_vQS, 8 * (g - 1) + hc + 1)
                    for dc in range(8):
                        ins = tn.matmul(
                            kps[KI % 2][:, :],
                            Wk_sb[:, dc * D + hc * 128:dc * D + hc * 128 + 128],
                            ln1Tg[g % 2][:, dc * 512:(dc + 1) * 512],
                            start=(dc == 0), stop=(dc == 7))
                    ins.then_inc(s_tK, 1)
                for mt in range(4):
                    VI = 4 * g + mt
                    if VI >= 1:
                        tn.wait_ge(s_vV, VI)
                    for dc in range(8):
                        tn.matmul(vps[:, 0:512],
                                  ln1Tg[g % 2][:, dc * 512 + mt * 128:dc * 512 + mt * 128 + 128],
                                  Wv_sb[:, dc * D:dc * D + 512],
                                  start=(dc == 0), stop=(dc == 7))
                        ins = tn.matmul(
                            vps[:, 512:1024],
                            ln1Tg[g % 2][:, dc * 512 + mt * 128:dc * 512 + mt * 128 + 128],
                            Wv_sb[:, dc * D + 512:dc * D + 1024],
                            start=(dc == 0), stop=(dc == 7))
                    ins.then_inc(s_tV, 1)
                if g < NGQ:
                    tn.wait_ge(s_vK, 8 * (g + 1))
                    for hc in range(8):
                        QI = 8 * g + hc
                        if QI >= 2:
                            tn.wait_ge(s_vQS, QI - 1)
                        for dc in range(8):
                            ins = tn.matmul(
                                kps[QI % 2][:, :],
                                Wq_sb[:, dc * D + hc * 128:dc * D + hc * 128 + 128],
                                ln1Tg[g % 2][:, dc * 512:(dc + 1) * 512],
                                start=(dc == 0), stop=(dc == 7))
                        ins.then_inc(s_tQ, 1)
                def emit_ck(gp):
                    if gp >= 1:
                        tn.wait_ge(s_vCK, gp)
                    for hc in range(8):
                        tn.wait_ge(s_vSQ, 8 * gp + hc + 1)
                        ins = tn.matmul(cps[0:16, :],
                                        esel[:, hc * 16:(hc + 1) * 16],
                                        ksq[gp % 2][:, hc * 512:(hc + 1) * 512],
                                        start=(hc == 0), stop=(hc == 7))
                    ins.then_inc(s_tCK, 1)

                def emit_cq(gp):
                    if gp >= 1:
                        tn.wait_ge(s_vCQ, gp)
                    for hc in range(8):
                        tn.wait_ge(s_vQS, 8 * gp + hc + 1)
                        ins = tn.matmul(cpsq[0:16, :],
                                        esel[:, hc * 16:(hc + 1) * 16],
                                        qsq[gp % 2][:, hc * 512:(hc + 1) * 512],
                                        start=(hc == 0), stop=(hc == 7))
                    ins.then_inc(s_tCQ, 1)

                if g >= 1:
                    emit_ck(g - 1)
                if 1 <= g <= NGQ:
                    emit_cq(g - 1)
                if g == NGK - 1:
                    tn.wait_ge(s_vCK, g)
                    emit_ck(g)

    if phases == 1:
        with nc.Block() as bf:
            @bf.gpsimd
            def _(g):
                for t in range(NTQ):
                    g.dma_start(out=outh.ap()[t * 128:(t + 1) * 128, :],
                                in_=x0t[0][:, :]).then_inc(s_out, 16)
                g.wait_ge(s_out, 16 * NTQ)
                if debug:
                    nd = 0
                    g.dma_start(out=dbgh["d_kT"].ap()[:, 0:512],
                                in_=ln1t[(NTK - 1) % 2][:, 0:512]).then_inc(s_out, 16)
                    nd += 1
                    g.dma_start(out=dbgh["d_v"].ap()[:, 0:512],
                                in_=x0t[(NTK - 1) % 3][:, 0:512]).then_inc(s_out, 16)
                    nd += 1
                    g.dma_start(out=dbgh["d_qT"].ap()[:, 0:512],
                                in_=qT[0:128, 0:512]).then_inc(s_out, 16)
                    nd += 1
                    g.dma_start(out=dbgh["d_attc"].ap()[:, 0:512],
                                in_=ln1Tg[(NGK - 1) % 2][:, 0:512]).then_inc(s_out, 16)
                    nd += 1
                    g.dma_start(out=dbgh["d_crep"].ap()[0:16, 0:16],
                                in_=chist[0:16, 0:16]).then_inc(s_out, 16)
                    nd += 1
                    g.dma_start(out=dbgh["d_attc"].ap()[0:16, 0:512],
                                in_=cpsdump[0:16, 0:512]).then_inc(s_out, 16)
                    nd += 1
                    g.dma_start(out=dbgh["d_kT"].ap()[0:16, 0:20],
                                in_=chist[0:16, 20:40]).then_inc(s_out, 16)
                    nd += 1
                    g.dma_start(out=dbgh["d_crep"].ap()[0:128, 1:2],
                                in_=st_rv[0:128, 0:1],
                                allow_slow_non_contiguous=True).then_inc(s_out, 16)
                    nd += 1
                    g.dma_start(out=dbgh["d_crep"].ap()[0:128, 2:3],
                                in_=st_rs[0:128, 0:1],
                                allow_slow_non_contiguous=True).then_inc(s_out, 16)
                    nd += 1
                    g.dma_start(out=dbgh["d_crep"].ap()[0:128, 3:4],
                                in_=st_s2[0:128, 0:1],
                                allow_slow_non_contiguous=True).then_inc(s_out, 16)
                    nd += 1
                    g.dma_start(out=dbgh["d_kmax"].ap()[0:16, 0:1],
                                in_=kmax2[0:16, 0:1],
                                allow_slow_non_contiguous=True).then_inc(s_out, 16)
                    nd += 1
                    g.dma_start(out=dbgh["d_kmax"].ap()[0:16, 1:2],
                                in_=qmax2[0:16, 0:1],
                                allow_slow_non_contiguous=True).then_inc(s_out, 16)
                    nd += 1
                    g.wait_ge(s_out, 16 * (NTQ + nd))
        p1.close()
        ctx.close()
        return nc

    p1.close()

    attc = sb("attc", [128, 8 * NQ], BF16)
    Wp_sb = sb("Wp_sb", [128, 8 * D], BF16)
    Win_sb = sb("Win_sb", [128, 8 * M], BF16)
    Wout_sb = sb("Wout_sb", [64, D], BF16)
    bp_rep = sb("bp_rep", [128, D], F32)
    bout_rep = sb("bout_rep", [128, D], F32)
    w1_rep = sb("w1_rep", [128, D], F32)
    b1_rep = sb("b1_rep", [128, D], F32)
    w2_rep = sb("w2_rep", [128, D], F32)
    bin_rep = sb("bin_rep", [128, M], F32)
    b2_rep = sb("b2_rep", [128, M], F32)

    p2 = ExitStack()

    def sb2(name, shape, dt):
        return p2.enter_context(nc.sbuf_tensor(name, shape, dt))

    kdup = [sb2(f"kdup{i}", [128, NK], BF16) for i in range(2)]
    VTW = NTK * 65
    vt = [sb2(f"vt{i}", [128, VTW], BF16) for i in range(2)]
    qdup = [sb2(f"qdup{i}", [128, NQ], BF16) for i in range(2)]
    ex = [sb2(f"ex{i}", [128, 1024], BF16) for i in range(3)]
    recr = [sb2(f"recr{i}", [128, 512], F32) for i in range(2)]
    rtmp = sb2("rtmp", [1, 512], F32)
    atst = [sb2(f"atst{i}", [64, 512], BF16) for i in range(2)]
    cneg = sb2("cneg", [16, 1], F32)

    pL = p2.enter_context(nc.psum_tensor("pL", [128, 3072], F32))
    pA = [p2.enter_context(nc.psum_tensor(f"pA{i}", [128, 512], F32))
          for i in range(2)]

    DOUBLES = [(64, 1, 0, True), (0, 1, 1, False), (0, 2, 2, False),
               (0, 4, 4, False), (0, 8, 8, False), (0, 16, 16, False)]
    NRC = len(DOUBLES)

    def emit_rec_chain(eng, hq, semA, semB, idx):
        eng.wait_ge(s_vrec, hq + 1)
        r = recr[hq % 2]
        for (sA, base) in ((semA, 0), (semB, 32)):
            cnt = 16 * NRC * idx
            for (src0, n, dst0, abs_src) in DOUBLES:
                s0 = src0 if abs_src else base + src0
                if cnt:
                    eng.wait_ge(sA, cnt)
                eng.dma_start(out=r[base + dst0:base + dst0 + n, :],
                              in_=r[s0:s0 + n, :]).then_inc(sA, 16)
                cnt += 16

    def av_inc_of(G):
        hq_, g_ = G // NG, G % NG
        if g_ == NG - 1:
            return ("q", hq_ + 1)
        return ("g", (NG - 1) * hq_ + g_ + 1)

    NBG_PLAIN = 16 + 8 + 1
    rep_specs = [(bph, D), (bouth, D), (W1h, D), (b1h, D), (W2h, D),
                 (binh, M), (b2h, M)]
    rep_dst = [None] * 7
    rep_dst[0] = bp_rep; rep_dst[1] = bout_rep; rep_dst[2] = w1_rep
    rep_dst[3] = b1_rep; rep_dst[4] = w2_rep; rep_dst[5] = bin_rep
    rep_dst[6] = b2_rep
    NW2 = NBG_PLAIN + 7 * 8

    with nc.Block() as b2:
        @b2.vector
        def _(v):
            v.memset(crep[:, :], 0.0)
            v.reduce_max(kmax2[:, :], kgmax[:, :], axis=AX)
            v.reduce_max(qmax2[:, :], qgmax[:, :], axis=AX)
            v.memset(crep[:, :], 0.0)
            v.tensor_mul(cneg[:, :], kmax2[:, :], qmax2[:, :]).then_inc(s_c, 1)
            v.wait_ge(s_cs, 1)
            v.tensor_scalar(cneg[:, :], cneg[:, :], -1.0, -1.0,
                            ALU.mult, ALU.add).then_inc(s_c, 1)
            for h in range(P2H):
                for qb in range(NQB):
                    hq = NQB * h + qb
                    v.wait_ge(s_tAVq, hq + 1)
                    v.tensor_copy(recr[hq % 2][64:65, :],
                                  pA[hq % 2][64:65, :]).then_inc(s_vrec0, 1)
                    if hq % 2 == 0:
                        v.wait_ge(s_rrA, 16 * NRC * (hq // 2 + 1))
                        v.wait_ge(s_rrB, 16 * NRC * (hq // 2 + 1))
                    else:
                        v.wait_ge(s_rgA, 16 * NRC * (hq // 2 + 1))
                        v.wait_ge(s_rgB, 16 * NRC * (hq // 2 + 1))
                    if hq >= 2:
                        v.wait_ge(s_at2, 16 * (hq - 1))
                    v.tensor_tensor(
                        atst[hq % 2][:, :],
                        pA[hq % 2][0:64, :], recr[hq % 2][0:64, :],
                        ALU.mult).then_inc(s_vAT, 1)

        @b2.scalar
        def _(sc):
            def emit_recr(hq_):
                sc.wait_ge(s_vrec0, hq_ + 1)
                sc.activation(out=rtmp[0:1, :],
                              in_=recr[hq_ % 2][64:65, :], func=AF.Ln)
                sc.activation(out=recr[hq_ % 2][64:65, :],
                              in_=rtmp[0:1, :], func=AF.Exp,
                              scale=-1.0).then_inc(s_vrec, 1)

            sc.wait_ge(s_c, 1)
            sc.activation(out=cneg[:, :], in_=cneg[:, :], func=AF.Ln)
            sc.activation(out=cneg[:, :], in_=cneg[:, :],
                          func=AF.Exp, scale=0.5).then_inc(s_cs, 1)
            sc.wait_ge(s_crep, 16 * 8)
            for h in range(P2H):
                for qb in range(NQB):
                    hq = NQB * h + qb
                    for g in range(NG):
                        G = NG * hq + g
                        sc.wait_ge(s_tLp, NG * hq + g + 1)
                        if G >= 3:
                            sm, val = av_inc_of(G - 3)
                            sc.wait_ge(s_tAVg if sm == "g" else s_tAVq, val)
                        sc.activation(
                            out=ex[G % 3][:, :],
                            in_=pL[:, (g % 3) * 1024:(g % 3) * 1024 + 1024],
                            func=AF.Exp,
                            bias=crep[:, h:h + 1]).then_inc(s_exp, 1)
                        if g == 2 and hq >= 1:
                            emit_recr(hq - 1)
            emit_recr(P2H * NQB - 1)

        @b2.tensor
        def _(tn):
            for h in range(P2H):
                for qb in range(NQB):
                    hq = NQB * h + qb
                    if qb == 0:
                        tn.wait_ge(s_k, 32 * (h + 1))
                        tn.wait_ge(s_q2, 32 * (h + 1))
                        tn.wait_ge(s_v2, 16 * NVD * (h + 1))

                    def emit_av_group(g, hq=hq, h=h):
                        G = NG * hq + g
                        last_grp = (g == NG - 1)
                        tn.wait_ge(s_exp, G + 1)
                        if g == 0 and hq >= 2:
                            tn.wait_ge(s_vAT, hq - 1)
                        for i, t in enumerate((2 * g, 2 * g + 1)):
                            ins = tn.matmul(pA[hq % 2][0:65, :],
                                            vt[h % 2][:, t * 65:t * 65 + 65],
                                            ex[G % 3][:, i * 512:(i + 1) * 512],
                                            start=(t == 0), stop=(t == NTK - 1))
                        if last_grp:
                            ins.then_inc(s_tAVq, 1)
                        else:
                            ins.then_inc(s_tAVg, 1)

                    for j in range(NG):
                        Gj = NG * hq + j
                        if hq == 0 and j < 3:
                            pass
                        elif j in (0, 1):
                            tn.wait_ge(s_exp, Gj - 1)
                        elif j == 2:
                            tn.wait_ge(s_exp, Gj - 4)
                        else:
                            tn.wait_ge(s_exp, Gj - 2)
                        ta, tb = 2 * j, 2 * j + 1
                        tn.matmul(pL[:, (ta % 6) * 512:(ta % 6) * 512 + 512],
                                  kdup[h % 2][0:64, ta * 128:(ta + 1) * 128],
                                  qdup[h % 2][0:64, qb * 512:(qb + 1) * 512],
                                  start=True, stop=True, tile_position=(0, 0))
                        ins = tn.matmul(
                            pL[:, (tb % 6) * 512:(tb % 6) * 512 + 512],
                            kdup[h % 2][64:128, tb * 128:(tb + 1) * 128],
                            qdup[h % 2][64:128, qb * 512:(qb + 1) * 512],
                            start=True, stop=True, tile_position=(64, 0))
                        ins.then_inc(s_tLp, 1)
                        if j >= 2:
                            emit_av_group(j - 2)
                    emit_av_group(NG - 2)
                    emit_av_group(NG - 1)

        @b2.sync
        def _(s):
            def at2_dma(hq):
                h_, qb_ = hq // NQB, hq % NQB
                s.wait_ge(s_vAT, hq + 1)
                if p2_stub & 8:
                    s.sem_inc(s_at2, 16)
                    return
                s.dma_start(
                    out=attc[(h_ % 2) * 64:(h_ % 2) * 64 + 64,
                             (h_ // 2) * NQ + qb_ * 512:(h_ // 2) * NQ + qb_ * 512 + 512],
                    in_=atst[hq % 2][:, :]).then_inc(s_at2, 16)

            def loadkq(h):
                if p2_stub & 128:
                    s.sem_inc(s_k, 32)
                    s.sem_inc(s_q2, 32)
                    return
                s.dma_start(out=kdup[h % 2][0:64, :],
                            in_=kT[h * 64:h * 64 + 64, :]).then_inc(s_k, 16)
                s.dma_start(out=kdup[h % 2][64:128, :],
                            in_=kT[h * 64:h * 64 + 64, :]).then_inc(s_k, 16)
                s.dma_start(out=qdup[h % 2][0:64, :],
                            in_=qT[(h % 2) * 64:(h % 2) * 64 + 64,
                                   (h // 2) * NQ:(h // 2 + 1) * NQ]).then_inc(s_q2, 16)
                s.dma_start(out=qdup[h % 2][64:128, :],
                            in_=qT[(h % 2) * 64:(h % 2) * 64 + 64,
                                   (h // 2) * NQ:(h // 2 + 1) * NQ]).then_inc(s_q2, 16)

            s.wait_ge(s_spK, 16 * 8 * NGK)
            s.wait_ge(s_spV, 16 * 4 * NGK)
            loadkq(0)
            for h in range(P2H):
                if h + 1 < P2H:
                    if h >= 1:
                        s.wait_ge(s_tLp, NPAIR * NQB * h)
                    loadkq(h + 1)
                for qb in range(NQB):
                    hq = NQB * h + qb
                    if not (p2_stub & 4):
                        if hq % 2 == 0:
                            emit_rec_chain(s, hq, s_rrA, s_rrB, hq // 2)
                        else:
                            emit_rec_chain(s, hq, s_rgA, s_rgB, hq // 2)
                    at2_dma(hq)

        @b2.gpsimd
        def _(g):
            for i in range(2):
                g.memset(bass.AP(vt[i], 64, [[VTW, 128], [65, NTK], [1, 1]]), 1.0)
            g.wait_ge(s_c, 2)
            if p2_stub & 1024:
                g.sem_inc(s_crep, 16 * 8)
            else:
                g.dma_start(out=crep[0:1, 0:16],
                            in_=cneg[0:16, 0:1]).then_inc(s_crep, 16)
                cnt = 16
                for (src0, n, dst0) in [(0, 1, 1), (0, 2, 2), (0, 4, 4), (0, 8, 8),
                                        (0, 16, 16), (0, 32, 32), (0, 64, 64)]:
                    g.wait_ge(s_crep, cnt)
                    g.dma_start(out=crep[dst0:dst0 + n, :],
                                in_=crep[src0:src0 + n, :]).then_inc(s_crep, 16)
                    cnt += 16
            w2 = [0]

            def w2inc(ins):
                ins.then_inc(s_w2, 16)
                w2[0] += 16
                return w2[0]

            bg = []
            for hc in range(8):
                for hl in range(2):
                    h_ = 2 * hc + hl
                    bg.append(lambda g, hc=hc, hl=hl, h_=h_: w2inc(g.dma_start(
                        out=Wp_sb[hl * 64:(hl + 1) * 64, hc * D:(hc + 1) * D],
                        in_=bass.AP(Wph, h_ * D, [[16 * D, 64], [1, D]]))))
            for dc in range(8):
                bg.append(lambda g, dc=dc: w2inc(g.dma_start(
                    out=Win_sb[:, dc * M:(dc + 1) * M],
                    in_=bass.AP(Winh, dc * 128 * M, [[M, 128], [1, M]]))))
            bg.append(lambda g: w2inc(g.dma_start(
                out=Wout_sb[0:64, :], in_=bass.AP(Wouth, 0, [[D, 64], [1, D]]))))
            rep_state = []
            for i, (srch, width) in enumerate(rep_specs):
                rep_state.append({"dst": rep_dst[i], "srch": srch, "w": width,
                                  "k": 0, "cnt": 0})

            def emit_bg_one(g):
                if bg:
                    bg.pop(0)(g)
                    return True
                st = next((x for x in rep_state if x["k"] < 8), None)
                if st is None:
                    return False
                if st["k"] == 0:
                    st["cnt"] = w2inc(g.dma_start(
                        out=st["dst"][0:1, :],
                        in_=bass.AP(st["srch"], 0, [[st["w"], 1], [1, st["w"]]])))
                else:
                    rows = 1 << (st["k"] - 1)
                    g.wait_ge(s_w2, st["cnt"])
                    st["cnt"] = w2inc(g.dma_start(
                        out=st["dst"][rows:2 * rows, :],
                        in_=st["dst"][0:rows, :]))
                st["k"] += 1
                return True

            for h in range(P2H):
                if h >= 2:
                    g.wait_ge(s_tAVq, NQB * (h - 1))
                if p2_stub & 256:
                    g.sem_inc(s_v2, 16 * NVD)
                else:
                    for mo in range(NVD):
                        g.dma_start(
                            out=bass.AP(vt[h % 2], mo * 8 * 65,
                                        [[VTW, 128], [65, 8], [1, E]]),
                            in_=bass.AP(vhh, (mo * 8 * 128) * D + h * E,
                                        [[D, 128], [128 * D, 8], [1, E]]),
                        ).then_inc(s_v2, 16)
                if not (p2_stub & 512):
                    for _ in range(6):
                        emit_bg_one(g)

            if not (p2_stub & 512):
                while emit_bg_one(g):
                    pass
            else:
                g.sem_inc(s_w2, 16 * NW2)

    p2.close()

    if phases == 2:
        with nc.Block() as bf:
            @bf.gpsimd
            def _(g):
                for t in range(NTQ):
                    g.dma_start(out=outh.ap()[t * 128:(t + 1) * 128, :],
                                in_=bp_rep[:, :]).then_inc(s_out, 16)
                g.wait_ge(s_out, 16 * NTQ)
        ctx.close()
        return nc

    p3 = ExitStack()

    def sb3(name, shape, dt):
        return p3.enter_context(nc.sbuf_tensor(name, shape, dt))

    x3t = [sb3(f"x3t{i}", [128, D], F32) for i in range(2)]
    zt = sb3("zt", [128, D], F32)
    zt2 = sb3("zt2", [128, D], F32)
    x1t = [sb3(f"x1t{i}", [128, D], F32) for i in range(2)]
    ln2t = [sb3(f"ln2t{i}", [128, D], BF16) for i in range(2)]
    ln2T = [sb3(f"ln2T{i}", [128, D], BF16) for i in range(2)]
    s_sb = [sb3(f"s_sb{i}", [128, M], F32) for i in range(2)]
    hbuf = sb3("hbuf", [128, D], F32)
    g0b = sb3("g0b", [128, M], F32)
    gsb = [sb3(f"gsb{i}", [128, M], BF16) for i in range(2)]
    gT = [sb3(f"gT{i}", [64, 128], BF16) for i in range(2)]
    outt = [sb3(f"outt{i}", [128, D], F32) for i in range(2)]
    sq3 = sb3("sq3", [128, D], F32)
    sq3b = sb3("sq3b", [128, D], F32)
    st3 = [sb3(f"st3_{i}", [128, 1], F32) for i in range(6)]
    st3g = sb3("st3g", [128, 1], F32)

    pO = p3.enter_context(nc.psum_tensor("pO", [128, 1024], F32))
    pF = p3.enter_context(nc.psum_tensor("pF", [128, 1024], F32))
    ptp3 = [p3.enter_context(nc.psum_tensor(f"ptp3{i}", [128, 128], BF16))
            for i in range(2)]
    pS = p3.enter_context(nc.psum_tensor("pS", [128, 512], F32))
    ptg = p3.enter_context(nc.psum_tensor("ptg", [128, 128], BF16))

    NT3 = NTQ

    V = {}
    cv = 0
    for t in range(NT3):
        if t >= 1:
            cv += 1; V[("sS", t - 1)] = cv
        if t >= 2:
            cv += 1; V[("gsb", t - 2)] = cv
            cv += 1; V[("cF", t - 2)] = cv
            cv += 1; V[("out", t - 2)] = cv
        cv += 1; V[("a", t)] = cv
        cv += 1; V[("b", t)] = cv
        cv += 1; V[("ln2", t)] = cv
    cv += 1; V[("sS", NT3 - 1)] = cv
    cv += 1; V[("gsb", NT3 - 2)] = cv
    cv += 1; V[("cF", NT3 - 2)] = cv
    cv += 1; V[("out", NT3 - 2)] = cv
    cv += 1; V[("gsb", NT3 - 1)] = cv
    cv += 1; V[("cF", NT3 - 1)] = cv
    cv += 1; V[("out", NT3 - 1)] = cv

    Tt = {}
    ct = 0
    for t in range(NT3):
        ct += 1; Tt[("proj", t)] = ct
        if t >= 2:
            ct += 1; Tt[("ptg", t - 2)] = ct
            ct += 1; Tt[("wout", t - 2)] = ct
        for i in range(8):
            ct += 1; Tt[("tp", t, i)] = ct
        ct += 1; Tt[("win", t)] = ct
    ct += 1; Tt[("ptg", NT3 - 2)] = ct
    ct += 1; Tt[("wout", NT3 - 2)] = ct
    ct += 1; Tt[("ptg", NT3 - 1)] = ct
    ct += 1; Tt[("wout", NT3 - 1)] = ct

    Sc = {}
    Cc = {}
    cs3 = 0
    cc3 = 0
    for t in range(NT3):
        if t >= 2:
            cc3 += 1; Cc[("gt", t - 2)] = cc3
            cs3 += 1; Sc[("c", t - 2)] = cs3
        cs3 += 1; Sc[("a", t)] = cs3
        cs3 += 1; Sc[("b", t)] = cs3
        for i in range(8):
            cc3 += 1; Cc[("tp", t, i)] = cc3
    cc3 += 1; Cc[("gt", NT3 - 2)] = cc3
    cs3 += 1; Sc[("c", NT3 - 2)] = cs3
    cc3 += 1; Cc[("gt", NT3 - 1)] = cc3
    cs3 += 1; Sc[("c", NT3 - 1)] = cs3

    Gg = {}
    cg3 = 0
    for t in range(NT3):
        cg3 += 1; Gg[("hb", t)] = cg3

    with nc.Block() as b3:
        @b3.sync
        def _(s):
            for t in range(min(2, NT3)):
                s.dma_start(out=x3t[t % 2][:, :],
                            in_=xb[t * 128:(t + 1) * 128, :]).then_inc(s3_x, 16)
            for t in range(NT3):
                if t + 2 < NT3:
                    s.wait_ge(s3_v, V[("b", t)])
                    s.dma_start(out=x3t[t % 2][:, :],
                                in_=xb[(t + 2) * 128:(t + 3) * 128, :]
                                ).then_inc(s3_x, 16)
                s.wait_ge(s3_v, V[("out", t)])
                s.dma_start(out=outh.ap()[t * 128:(t + 1) * 128, :],
                            in_=outt[t % 2][:, :]).then_inc(s_out, 16)
            s.wait_ge(s_out, 16 * NT3)

        @b3.tensor
        def _(tn):
            tn.wait_ge(s_w2, 16 * NW2)
            tn.wait_ge(s_at2, 16 * H * NQB)

            def emit_pw(tt):
                tn.wait_ge(s3_v, V[("gsb", tt)])
                if tt >= 1:
                    tn.wait_ge(s3_c, Cc[("gt", tt - 1)])
                tn.transpose(ptg[0:64, 0:128], gsb[tt % 2][:, :],
                             identb[:, :]).then_inc(s3_t, 1)
                tn.wait_ge(s3_c, Cc[("gt", tt)])
                if tt >= 1:
                    tn.wait_ge(s3_v, V[("cF", tt - 1)])
                tn.matmul(pF[:, 0:512], gT[tt % 2][:, :],
                          Wout_sb[0:64, 0:512], start=True, stop=True)
                tn.matmul(pF[:, 512:1024], gT[tt % 2][:, :],
                          Wout_sb[0:64, 512:1024],
                          start=True, stop=True).then_inc(s3_t, 1)

            for t in range(NT3):
                qb, tq = t // 4, t % 4
                if t >= 1:
                    tn.wait_ge(s3_v, V[("a", t - 1)])
                for hc in range(8):
                    lhs = attc[:, hc * NQ + qb * 512 + tq * 128:
                               hc * NQ + qb * 512 + tq * 128 + 128]
                    tn.matmul(pO[:, 0:512], lhs,
                              Wp_sb[:, hc * D:hc * D + 512],
                              start=(hc == 0), stop=(hc == 7))
                    ins = tn.matmul(pO[:, 512:1024], lhs,
                                    Wp_sb[:, hc * D + 512:hc * D + 1024],
                                    start=(hc == 0), stop=(hc == 7))
                ins.then_inc(s3_t, 1)
                if t >= 2:
                    emit_pw(t - 2)
                tn.wait_ge(s3_v, V[("ln2", t)])
                for i in range(8):
                    gi = 8 * t + i
                    if gi >= 2:
                        tt_, ii_ = (t, i - 2) if i >= 2 else (t - 1, i + 6)
                        tn.wait_ge(s3_c, Cc[("tp", tt_, ii_)])
                    tn.transpose(ptp3[i % 2][:, 0:128],
                                 ln2t[t % 2][:, i * 128:(i + 1) * 128],
                                 identb[:, :]).then_inc(s3_t, 1)
                tn.wait_ge(s3_c, Cc[("tp", t, 7)])
                if t >= 1:
                    tn.wait_ge(s3_v, V[("sS", t - 1)])
                for dc in range(8):
                    ins = tn.matmul(pS[:, 0:64],
                                    ln2T[t % 2][:, dc * 128:(dc + 1) * 128],
                                    Win_sb[:, dc * M:(dc + 1) * M],
                                    start=(dc == 0), stop=(dc == 7))
                ins.then_inc(s3_t, 1)
            emit_pw(NT3 - 2)
            emit_pw(NT3 - 1)

        @b3.vector
        def _(v):
            def ln_stats(src):
                v.reduce_sum(st3[4][:, :], src[:, :], axis=AX)
                v.tensor_mul(sq3[:, :], src[:, :], src[:, :])
                v.reduce_sum(st3[5][:, :], sq3[:, :], axis=AX)
                v.tensor_scalar_mul(st3[1][:, :], st3[4][:, :], 1.0 / D)
                v.tensor_scalar_mul(st3[5][:, :], st3[5][:, :], 1.0 / D)
                v.tensor_mul(st3[4][:, :], st3[1][:, :], st3[1][:, :])
                v.tensor_sub(st3[5][:, :], st3[5][:, :], st3[4][:, :])
                return v.tensor_scalar_add(st3[5][:, :], st3[5][:, :], EPS)

            def emit_sS(tt):
                v.wait_ge(s3_t, Tt[("win", tt)])
                if tt >= 2:
                    v.wait_ge(s3_g, Gg[("hb", tt - 2)])
                v.tensor_add(s_sb[tt % 2][:, :], pS[:, 0:64],
                             bin_rep[:, :]).then_inc(s3_v, 1)

            def emit_R(tt):
                v.wait_ge(s3_g, Gg[("hb", tt)])
                if tt >= 2:
                    v.wait_ge(s3_t, Tt[("ptg", tt - 2)])
                v.reduce_sum(g0b[:, :],
                             bass.AP(hbuf, 0,
                                     [[D, 128], [INNER, M], [1, INNER]]),
                             axis=AX)
                v.tensor_copy(st3g[:, :], b2_rep[:, 0:1])
                v.tensor_add(gsb[tt % 2][:, :], g0b[:, :],
                             b2_rep[:, :]).then_inc(s3_v, 1)

            def emit_A(tt):
                v.wait_ge(s3_t, Tt[("proj", tt)])
                v.tensor_add(zt[:, :], pO[:, :], bp_rep[:, :])
                ln_stats(zt).then_inc(s3_v, 1)
                v.wait_ge(s3_s, Sc[("a", tt)])
                v.wait_ge(s3_x, 16 * (tt + 1))
                v.tensor_scalar(sq3[:, :], zt[:, :], st3[1][:, :], st3[2][:, :],
                                ALU.subtract, ALU.mult)
                v.tensor_add(x1t[tt % 2][:, :], x3t[tt % 2][:, :], sq3[:, :])
                ln_stats(x1t[tt % 2]).then_inc(s3_v, 1)
                v.wait_ge(s3_s, Sc[("b", tt)])
                if tt >= 2:
                    v.wait_ge(s3_t, Tt[("tp", tt - 2, 7)])
                v.tensor_scalar(ln2t[tt % 2][:, :], x1t[tt % 2][:, :],
                                st3[1][:, :], st3[2][:, :],
                                ALU.subtract, ALU.mult).then_inc(s3_v, 1)

            def emit_F(tt):
                v.wait_ge(s3_t, Tt[("wout", tt)])
                v.tensor_add(zt2[:, :], pF[:, :], bout_rep[:, :])
                ln_stats(zt2).then_inc(s3_v, 1)
                v.wait_ge(s3_s, Sc[("c", tt)])
                if tt >= 2:
                    v.wait_ge(s_out, 16 * (tt - 1))
                v.tensor_scalar(sq3b[:, :], zt2[:, :], st3[1][:, :],
                                st3[2][:, :], ALU.subtract, ALU.mult)
                v.tensor_add(outt[tt % 2][:, :], x1t[tt % 2][:, :],
                             sq3b[:, :]).then_inc(s3_v, 1)

            v.wait_ge(s_w2, 16 * NW2)
            emit_A(0)
            for t in range(1, NT3):
                emit_sS(t - 1)
                if t >= 2:
                    emit_R(t - 2)
                    emit_F(t - 2)
                emit_A(t)
            emit_sS(NT3 - 1)
            emit_R(NT3 - 2)
            emit_F(NT3 - 2)
            emit_R(NT3 - 1)
            emit_F(NT3 - 1)

        @b3.scalar
        def _(sc):
            def rsqrt(v_need):
                sc.wait_ge(s3_v, v_need)
                sc.activation(out=st3[0][:, :], in_=st3[5][:, :], func=AF.Ln)
                sc.activation(out=st3[2][:, :], in_=st3[0][:, :],
                              func=AF.Exp, scale=-0.5).then_inc(s3_s, 1)

            def emit_gt(tt):
                sc.wait_ge(s3_t, Tt[("ptg", tt)])
                if tt >= 2:
                    sc.wait_ge(s3_t, Tt[("wout", tt - 2)])
                sc.activation(out=gT[tt % 2][:, :], in_=ptg[0:64, 0:128],
                              func=AF.Copy).then_inc(s3_c, 1)

            for t in range(NT3):
                if t >= 2:
                    emit_gt(t - 2)
                    rsqrt(V[("cF", t - 2)])
                rsqrt(V[("a", t)])
                rsqrt(V[("b", t)])
                for i in range(8):
                    sc.wait_ge(s3_t, Tt[("tp", t, i)])
                    if t >= 2 and i == 0:
                        sc.wait_ge(s3_t, Tt[("win", t - 2)])
                    sc.activation(out=ln2T[t % 2][:, i * 128:(i + 1) * 128],
                                  in_=ptp3[i % 2][:, 0:128],
                                  func=AF.Copy).then_inc(s3_c, 1)
            emit_gt(NT3 - 2)
            rsqrt(V[("cF", NT3 - 2)])
            emit_gt(NT3 - 1)
            rsqrt(V[("cF", NT3 - 1)])

        @b3.gpsimd
        def _(g):
            g.wait_ge(s_w2, 16 * NW2)
            for t in range(NT3):
                g.wait_ge(s3_v, V[("sS", t)])
                if t >= 1:
                    g.wait_ge(s3_v, V[("gsb", t - 1)])
                g.tensor_tensor(hbuf[:, :],
                                bass.AP(s_sb[t % 2], 0,
                                        [[M, 128], [1, M], [0, INNER]]),
                                w1_rep[:, :], ALU.mult)
                g.tensor_add(hbuf[:, :], hbuf[:, :], b1_rep[:, :])
                g.tensor_scalar_max(hbuf[:, :], hbuf[:, :], 0.0)
                g.tensor_mul(hbuf[:, :], hbuf[:, :],
                             w2_rep[:, :]).then_inc(s3_g, 1)
            g.wait_ge(s_out, 16 * NT3)
            if debug:
                nd = 0
                for nm, src in (("d_qT", qT[:, 0:512]),
                                ("d_crep", crep[:, :]),
                                ("d_attc", attc[:, 0:512])):
                    g.dma_start(out=x3t[0][:, 0:src.shape[-1]], in_=src).then_inc(s_out, 16)
                    nd += 1
                    g.wait_ge(s_out, 16 * (NT3 + nd))
                    g.dma_start(out=dbgh[nm].ap()[0:src.shape[0], 0:src.shape[-1]],
                                in_=x3t[0][0:src.shape[0], 0:src.shape[-1]]).then_inc(s_out, 16)
                    nd += 1
                    g.wait_ge(s_out, 16 * (NT3 + nd))
                g.dma_start(out=dbgh["d_kT"].ap()[:, 0:512],
                            in_=kT[0:128, 0:512]).then_inc(s_out, 16)
                nd += 1
                g.wait_ge(s_out, 16 * (NT3 + nd))
                g.dma_start(out=dbgh["d_v"].ap()[:, 0:512],
                            in_=vh[0:128, 0:512]).then_inc(s_out, 16)
                nd += 1
                g.wait_ge(s_out, 16 * (NT3 + nd))
                g.dma_start(out=dbgh["d_kmax"].ap()[0:16, 0:1],
                            in_=kmax2[0:16, 0:1],
                            allow_slow_non_contiguous=True).then_inc(s_out, 16)
                nd += 1
                g.dma_start(out=dbgh["d_kmax"].ap()[0:16, 1:2],
                            in_=qmax2[0:16, 0:1],
                            allow_slow_non_contiguous=True).then_inc(s_out, 16)
                nd += 1
                g.dma_start(out=dbgh["d_cq"].ap()[:, :],
                            in_=dbg_cq[:, :]).then_inc(s_out, 16)
                nd += 1
                g.dma_start(out=dbgh["d_ck"].ap()[:, :],
                            in_=dbg_ck[:, :]).then_inc(s_out, 16)
                nd += 1
                g.dma_start(out=dbgh["d_q"].ap()[:, :],
                            in_=dbg_q[:, :]).then_inc(s_out, 16)
                nd += 1
                g.wait_ge(s_out, 16 * (NT3 + nd))

    p3.close()
    ctx.close()
    return nc


def kernel(L0_attn_query, L0_attn_Wq, L0_attn_Wk, L0_attn_Wv, L0_attn_Wp,
           L0_attn_bp, L0_ffn_Win, L0_ffn_bin_, L0_ffn_W1, L0_ffn_b1,
           L0_ffn_W2, L0_ffn_b2, L0_ffn_Wout, L0_ffn_bout):
    x = np.ascontiguousarray(np.asarray(L0_attn_query, dtype=np.float32))
    assert x.shape == (2, 8192, 1024)
    nc = build_nc(NK=8192, NQ=2048)
    weights = {
        "Wq": np.ascontiguousarray(np.asarray(L0_attn_Wq, np.float32)),
        "Wk": np.ascontiguousarray(np.asarray(L0_attn_Wk, np.float32)),
        "Wv": np.ascontiguousarray(np.asarray(L0_attn_Wv, np.float32)),
        "Wp": np.ascontiguousarray(np.asarray(L0_attn_Wp, np.float32)),
        "bp": np.ascontiguousarray(np.asarray(L0_attn_bp, np.float32)),
        "Win": np.ascontiguousarray(np.asarray(L0_ffn_Win, np.float32)),
        "binp": np.ascontiguousarray(np.asarray(L0_ffn_bin_, np.float32)),
        "W1": np.ascontiguousarray(np.asarray(L0_ffn_W1, np.float32)),
        "b1": np.ascontiguousarray(np.asarray(L0_ffn_b1, np.float32)),
        "W2": np.ascontiguousarray(np.asarray(L0_ffn_W2, np.float32)),
        "b2": np.ascontiguousarray(np.asarray(L0_ffn_b2, np.float32)),
        "Wout": np.ascontiguousarray(np.asarray(L0_ffn_Wout, np.float32)),
        "bout": np.ascontiguousarray(np.asarray(L0_ffn_bout, np.float32)),
    }
    in_maps = []
    for c in range(8):
        b, qq = c // 4, c % 4
        xr = np.roll(x[b], -qq * 2048, axis=0)
        m = {"xb": np.ascontiguousarray(xr)}
        m.update(weights)
        in_maps.append(m)
    res = run_bass_kernel_spmd(nc, in_maps, core_ids=list(range(8)))
    out = np.empty((2, 8192, 1024), np.float32)
    for c in range(8):
        b, qq = c // 4, c % 4
        out[b, qq * 2048:(qq + 1) * 2048] = res.results[c]["out"]
    return out


# version 11
# speedup vs baseline: 1.0208x; 1.0208x over previous
import numpy as np
from contextlib import ExitStack

import concourse.bass as bass
import concourse.mybir as mybir
from concourse.bass_utils import run_bass_kernel_spmd

F32 = mybir.dt.float32
BF16 = mybir.dt.bfloat16
AX = mybir.AxisListType.X
ALU = mybir.AluOpType
AF = mybir.ActivationFunctionType

D = 1024
H = 16
E = 64
M = 64
INNER = 16
EPS = 1e-5
SCALE = 0.125


def build_nc(NK=8192, NQ=2048, debug=False, phases=3, p2_heads=None, p2_stub=0):
    NTK = NK // 128
    NGK = NK // 512
    NTQ = NQ // 128
    NGQ = max(1, NQ // 512)
    NQB = max(1, NQ // 512)
    NG = NTK // 2
    NPAIR = NTK // 2
    NVD = max(1, NTK // 8)
    P2H = H if p2_heads is None else p2_heads

    nc = bass.Bass("TRN2", target_bir_lowering=False,
                   detect_race_conditions=False)

    xbh = nc.dram_tensor("xb", [NK, D], F32, kind="ExternalInput")
    Wqh = nc.dram_tensor("Wq", [H, D, E], F32, kind="ExternalInput")
    Wkh = nc.dram_tensor("Wk", [H, D, E], F32, kind="ExternalInput")
    Wvh = nc.dram_tensor("Wv", [H, D, E], F32, kind="ExternalInput")
    Wph = nc.dram_tensor("Wp", [D, D], F32, kind="ExternalInput")
    bph = nc.dram_tensor("bp", [D], F32, kind="ExternalInput")
    Winh = nc.dram_tensor("Win", [D, M], F32, kind="ExternalInput")
    binh = nc.dram_tensor("binp", [M], F32, kind="ExternalInput")
    W1h = nc.dram_tensor("W1", [M, INNER], F32, kind="ExternalInput")
    b1h = nc.dram_tensor("b1", [M, INNER], F32, kind="ExternalInput")
    W2h = nc.dram_tensor("W2", [M, INNER], F32, kind="ExternalInput")
    b2h = nc.dram_tensor("b2", [M], F32, kind="ExternalInput")
    Wouth = nc.dram_tensor("Wout", [M, D], F32, kind="ExternalInput")
    bouth = nc.dram_tensor("bout", [D], F32, kind="ExternalInput")
    outh = nc.dram_tensor("out", [NQ, D], F32, kind="ExternalOutput")

    kTh = nc.dram_tensor("kT_scr", [D, NK], BF16, kind="Internal")
    vhh = nc.dram_tensor("v_scr", [NK, D], BF16, kind="Internal")
    dbgh = {}
    if debug:
        for nm, shp in (("d_kT", [128, 512]), ("d_v", [128, 512]),
                        ("d_qT", [128, 512]), ("d_crep", [128, 16]),
                        ("d_attc", [128, 512]), ("d_kmax", [16, 2]),
                        ("d_cq", [16, 4 * 512]), ("d_ck", [16, 512]),
                        ("d_q", [128, 512])):
            dbgh[nm] = nc.dram_tensor(nm, shp, F32, kind="ExternalOutput")

    xb = xbh.ap()
    kT = kTh.ap()
    vh = vhh.ap()

    ctx = ExitStack()

    def sb(name, shape, dt):
        return ctx.enter_context(nc.sbuf_tensor(name, shape, dt))

    def sem(name):
        return ctx.enter_context(nc.semaphore(name))

    identb = sb("identb", [128, 128], BF16)
    esel = sb("esel", [128, 8 * 16], BF16)
    if debug:
        dbg_cq = sb("dbg_cq", [16, 4 * 512], F32)
        dbg_ck = sb("dbg_ck", [16, 512], F32)
        dbg_q = sb("dbg_q", [128, 512], F32)
    qT = sb("qT", [128, 8 * NQ], BF16)
    kmax2 = sb("kmax2", [16, 1], F32)
    qmax2 = sb("qmax2", [16, 1], F32)
    kgmax = sb("kgmax", [16, NGK], F32)
    qgmax = sb("qgmax", [16, max(1, NGQ)], F32)
    crep = sb("crep", [128, 16], F32)

    s_w = sem("s_w")
    s_x = sem("s_x"); s_vS = sem("s_vS"); s_sS = sem("s_sS")
    s_vLN = sem("s_vLN"); s_tTP = sem("s_tTP"); s_vTP = sem("s_vTP")
    s_tK = sem("s_tK"); s_vK = sem("s_vK"); s_tV = sem("s_tV"); s_vV = sem("s_vV")
    s_tQ = sem("s_tQ"); s_vQ = sem("s_vQ")
    s_vSQ = sem("s_vSQ"); s_tCK = sem("s_tCK"); s_vCK = sem("s_vCK")
    s_vQS = sem("s_vQS"); s_tCQ = sem("s_tCQ"); s_vCQ = sem("s_vCQ")
    s_spK = sem("s_spK"); s_spV = sem("s_spV"); s_tG = sem("s_tG")
    s_c = sem("s_c"); s_cs = sem("s_cs"); s_crep = sem("s_crep")
    s_k = sem("s_k"); s_v2 = sem("s_v2"); s_q2 = sem("s_q2")
    s_tLp = sem("s_tLp"); s_tL = sem("s_tL")
    s_exp = sem("s_exp"); s_tAVg = sem("s_tAVg"); s_tAVq = sem("s_tAVq")
    s_vrec = sem("s_vrec"); s_vrec0 = sem("s_vrec0")
    s_rrA = sem("s_rrA"); s_rrB = sem("s_rrB")
    s_rgA = sem("s_rgA"); s_rgB = sem("s_rgB"); s_at2 = sem("s_at2")
    s_vAT = sem("s_vAT")
    s_w2 = sem("s_w2")
    s3_x = sem("s3_x"); s3_t = sem("s3_t"); s3_v = sem("s3_v"); s3_s = sem("s3_s")
    s3_c = sem("s3_c"); s3_g = sem("s3_g")
    s_out = sem("s_out")

    p1 = ExitStack()

    def sb1(name, shape, dt):
        return p1.enter_context(nc.sbuf_tensor(name, shape, dt))

    def ps1(name, shape, dt):
        return p1.enter_context(nc.psum_tensor(name, shape, dt))

    Wq_sb = sb1("Wq_sb", [128, 8 * D], BF16)
    Wk_sb = sb1("Wk_sb", [128, 8 * D], BF16)
    Wv_sb = sb1("Wv_sb", [128, 8 * D], BF16)
    x0t = [sb1(f"x0t{i}", [128, D], F32) for i in range(3)]
    sqv = sb1("sqv", [128, D], F32)
    ln1t = [sb1(f"ln1t{i}", [128, D], BF16) for i in range(2)]
    ln1Tg = [sb1(f"ln1Tg{i}", [128, 8 * 512], BF16) for i in range(2)]
    kTsp = [sb1(f"kTsp{i}", [128, 8 * 512], BF16) for i in range(2)]
    vsp = [sb1(f"vsp{i}", [128, 4 * D], BF16) for i in range(2)]
    ksq = [sb1(f"ksq{i}", [128, 8 * 512], BF16) for i in range(2)]
    qsq = [sb1(f"qsq{i}", [128, 8 * 512], BF16) for i in range(2)]
    st_s1 = sb1("st_s1", [128, 1], F32)
    st_s2 = sb1("st_s2", [128, 1], F32)
    st_m = sb1("st_m", [128, 1], F32)
    st_t = sb1("st_t", [128, 1], F32)
    st_rv = sb1("st_rv", [128, 1], F32)
    st_rs = sb1("st_rs", [128, 1], F32)
    cmx = sb1("cmx", [16, 1], F32)

    ptp = [ps1(f"ptp{i}", [128, 1024], BF16) for i in range(2)]
    kps = [ps1(f"kps{i}", [128, 512], F32) for i in range(2)]
    vps = ps1("vps", [128, 1024], F32)
    cps = ps1("cps", [128, 512], F32)
    cpsq = ps1("cpsq", [128, 512], F32)

    NW1 = 24

    with nc.Block() as b0:
        @b0.gpsimd
        def _(g):
            for dc in range(8):
                for (Wh, Wsb) in ((Wqh, Wq_sb), (Wkh, Wk_sb), (Wvh, Wv_sb)):
                    g.dma_start(
                        out=Wsb[:, dc * D:(dc + 1) * D],
                        in_=bass.AP(Wh, dc * 128 * E, [[E, 128], [D * E, H], [1, E]]),
                    ).then_inc(s_w, 16)
            g.memset(identb[:, :], 1.0)
            g.affine_select(out=identb[:, :], in_=identb[:, :],
                            compare_op=ALU.is_equal, fill=0.0, base=0,
                            pattern=[[-1, 128]], channel_multiplier=1)

        @b0.vector
        def _(v):
            v.memset(esel[:, :], 0.0)
            for dc in range(8):
                v.memset(esel[0:64, dc * 16 + 2 * dc:dc * 16 + 2 * dc + 1], 1.0)
                v.memset(esel[64:128, dc * 16 + 2 * dc + 1:dc * 16 + 2 * dc + 2], 1.0)
            v.memset(kmax2[:, :], 0.0)
            v.memset(qmax2[:, :], 0.0)

    with nc.Block() as b1:
        @b1.sync
        def _(s):
            def spill(gp):
                s.wait_ge(s_vK, 8 * gp + 8)
                for hc in range(8):
                    s.dma_start(
                        out=kT[hc * 128:(hc + 1) * 128, gp * 512:(gp + 1) * 512],
                        in_=kTsp[gp % 2][:, hc * 512:(hc + 1) * 512],
                    ).then_inc(s_spK, 16)
                s.wait_ge(s_vV, 4 * gp + 4)
                for mt in range(4):
                    s.dma_start(
                        out=vh[(4 * gp + mt) * 128:(4 * gp + mt + 1) * 128, :],
                        in_=vsp[gp % 2][:, mt * D:(mt + 1) * D],
                    ).then_inc(s_spV, 16)

            for g in range(NGK):
                spill(g)

        @b1.vector
        def _(v):
            for g in range(NGK):
                for tq in range(4):
                    t = 4 * g + tq
                    v.wait_ge(s_x, 16 * (t + 1))
                    x = x0t[t % 3]
                    v.reduce_sum(st_s1[:, :], x[:, :], axis=AX)
                    v.tensor_mul(sqv[:, :], x[:, :], x[:, :])
                    v.reduce_sum(st_s2[:, :], sqv[:, :], axis=AX)
                    v.tensor_scalar_mul(st_m[:, :], st_s1[:, :], 1.0 / D)
                    v.tensor_scalar_mul(st_t[:, :], st_s2[:, :], 1.0 / D)
                    v.tensor_mul(st_s1[:, :], st_m[:, :], st_m[:, :])
                    v.tensor_sub(st_t[:, :], st_t[:, :], st_s1[:, :])
                    v.tensor_scalar_add(st_t[:, :], st_t[:, :],
                                        EPS).then_inc(s_vS, 1)
                    v.wait_ge(s_sS, t + 1)
                    v.tensor_scalar(ln1t[t % 2][:, :], x[:, :], st_m[:, :],
                                    st_rs[:, :], ALU.subtract,
                                    ALU.mult).then_inc(s_vLN, 1)
                    if tq == 0 and g >= 2:
                        if g - 2 < NGQ:
                            v.wait_ge(s_tQ, 8 * (g - 1))
                        else:
                            v.wait_ge(s_tV, 4 * (g - 1))
                    for dc in range(8):
                        v.wait_ge(s_tTP, 8 * t + dc + 1)
                        v.tensor_copy(
                            ln1Tg[g % 2][:, dc * 512 + tq * 128:dc * 512 + tq * 128 + 128],
                            ptp[dc % 2][:, 0:128]).then_inc(s_vTP, 1)
                if g >= 2:
                    v.wait_ge(s_spK, 16 * 8 * (g - 1))
                    v.wait_ge(s_spV, 16 * 4 * (g - 1))
                for hc in range(8):
                    KI = 8 * g + hc
                    v.wait_ge(s_tK, KI + 1)
                    v.tensor_copy(kTsp[g % 2][:, hc * 512:(hc + 1) * 512],
                                  kps[KI % 2][:, :]).then_inc(s_vK, 1)
                    v.tensor_mul(ksq[g % 2][:, hc * 512:(hc + 1) * 512],
                                 kTsp[g % 2][:, hc * 512:(hc + 1) * 512],
                                 kTsp[g % 2][:, hc * 512:(hc + 1) * 512]
                                 ).then_inc(s_vSQ, 1)
                for mt in range(4):
                    VI = 4 * g + mt
                    v.wait_ge(s_tV, VI + 1)
                    v.tensor_copy(vsp[g % 2][:, mt * D:(mt + 1) * D],
                                  vps[:, :]).then_inc(s_vV, 1)
                if g >= 2:
                    v.wait_ge(s_tCK, g - 1)
                    if debug and g == 2:
                        v.tensor_copy(dbg_ck[:, :], cps[0:16, :])
                    v.reduce_max(kgmax[:, g - 2:g - 1], cps[0:16, :],
                                 axis=AX).then_inc(s_vCK, 1)
                if 2 <= g and g - 2 < NGQ:
                    v.wait_ge(s_tCQ, g - 1)
                    if debug:
                        v.tensor_copy(dbg_cq[:, (g - 2) * 512:(g - 1) * 512],
                                      cpsq[0:16, :])
                    v.reduce_max(qgmax[:, g - 2:g - 1], cpsq[0:16, :],
                                 axis=AX).then_inc(s_vCQ, 1)
            v.wait_ge(s_tCK, NGK - 1)
            v.reduce_max(kgmax[:, NGK - 2:NGK - 1], cps[0:16, :],
                         axis=AX).then_inc(s_vCK, 1)
            v.wait_ge(s_tCK, NGK)
            v.reduce_max(kgmax[:, NGK - 1:NGK], cps[0:16, :],
                         axis=AX).then_inc(s_vCK, 1)

        @b1.scalar
        def _(sc):
            def xload(t):
                sc.dma_start(out=x0t[t % 3][:, :],
                             in_=xb[t * 128:(t + 1) * 128, :]).then_inc(s_x, 16)

            for t in range(3):
                xload(t)
            for t in range(NTK):
                sc.wait_ge(s_vS, t + 1)
                sc.activation(out=st_rv[:, :], in_=st_t[:, :], func=AF.Ln)
                sc.activation(out=st_rs[:, :], in_=st_rv[:, :],
                              func=AF.Exp, scale=-0.5).then_inc(s_sS, 1)
                if t + 3 < NTK:
                    sc.wait_ge(s_vLN, t + 1)
                    xload(t + 3)
                if t % 4 == 0:
                    gq = t // 4 - 1
                    if 0 <= gq < NGQ:
                        for hc in range(8):
                            QI = 8 * gq + hc
                            sc.wait_ge(s_tQ, QI + 1)
                            sc.activation(
                                out=qT[:, hc * NQ + gq * 512:hc * NQ + gq * 512 + 512],
                                in_=kps[QI % 2][:, :], func=AF.Copy,
                                scale=SCALE).then_inc(s_vQ, 1)
                            sc.activation(
                                out=qsq[gq % 2][:, hc * 512:(hc + 1) * 512],
                                in_=kps[QI % 2][:, :], func=AF.Square,
                                scale=SCALE).then_inc(s_vQS, 1)

        @b1.tensor
        def _(tn):
            tn.wait_ge(s_w, 16 * NW1)
            for g in range(NGK):
                for tq in range(4):
                    t = 4 * g + tq
                    tn.wait_ge(s_vLN, t + 1)
                    for dc in range(8):
                        i = 8 * t + dc
                        if i >= 2:
                            tn.wait_ge(s_vTP, i - 1)
                        tn.transpose(ptp[dc % 2][:, 0:128],
                                     ln1t[t % 2][:, dc * 128:(dc + 1) * 128],
                                     identb[:, :]).then_inc(s_tTP, 1)
                tn.wait_ge(s_vTP, 8 * (4 * g + 4))
                for hc in range(8):
                    KI = 8 * g + hc
                    if KI >= 2:
                        tn.wait_ge(s_vK, KI - 1)
                    if g >= 1 and g - 1 < NGQ:
                        tn.wait_ge(s_vQS, 8 * (g - 1) + hc + 1)
                    for dc in range(8):
                        ins = tn.matmul(
                            kps[KI % 2][:, :],
                            Wk_sb[:, dc * D + hc * 128:dc * D + hc * 128 + 128],
                            ln1Tg[g % 2][:, dc * 512:(dc + 1) * 512],
                            start=(dc == 0), stop=(dc == 7))
                    ins.then_inc(s_tK, 1)
                for mt in range(4):
                    VI = 4 * g + mt
                    if VI >= 1:
                        tn.wait_ge(s_vV, VI)
                    for dc in range(8):
                        tn.matmul(vps[:, 0:512],
                                  ln1Tg[g % 2][:, dc * 512 + mt * 128:dc * 512 + mt * 128 + 128],
                                  Wv_sb[:, dc * D:dc * D + 512],
                                  start=(dc == 0), stop=(dc == 7))
                        ins = tn.matmul(
                            vps[:, 512:1024],
                            ln1Tg[g % 2][:, dc * 512 + mt * 128:dc * 512 + mt * 128 + 128],
                            Wv_sb[:, dc * D + 512:dc * D + 1024],
                            start=(dc == 0), stop=(dc == 7))
                    ins.then_inc(s_tV, 1)
                if g < NGQ:
                    tn.wait_ge(s_vK, 8 * (g + 1))
                    for hc in range(8):
                        QI = 8 * g + hc
                        if QI >= 2:
                            tn.wait_ge(s_vQS, QI - 1)
                        for dc in range(8):
                            ins = tn.matmul(
                                kps[QI % 2][:, :],
                                Wq_sb[:, dc * D + hc * 128:dc * D + hc * 128 + 128],
                                ln1Tg[g % 2][:, dc * 512:(dc + 1) * 512],
                                start=(dc == 0), stop=(dc == 7))
                        ins.then_inc(s_tQ, 1)
                def emit_ck(gp):
                    if gp >= 1:
                        tn.wait_ge(s_vCK, gp)
                    for hc in range(8):
                        tn.wait_ge(s_vSQ, 8 * gp + hc + 1)
                        ins = tn.matmul(cps[0:16, :],
                                        esel[:, hc * 16:(hc + 1) * 16],
                                        ksq[gp % 2][:, hc * 512:(hc + 1) * 512],
                                        start=(hc == 0), stop=(hc == 7))
                    ins.then_inc(s_tCK, 1)

                def emit_cq(gp):
                    if gp >= 1:
                        tn.wait_ge(s_vCQ, gp)
                    for hc in range(8):
                        tn.wait_ge(s_vQS, 8 * gp + hc + 1)
                        ins = tn.matmul(cpsq[0:16, :],
                                        esel[:, hc * 16:(hc + 1) * 16],
                                        qsq[gp % 2][:, hc * 512:(hc + 1) * 512],
                                        start=(hc == 0), stop=(hc == 7))
                    ins.then_inc(s_tCQ, 1)

                if g >= 1:
                    emit_ck(g - 1)
                if 1 <= g <= NGQ:
                    emit_cq(g - 1)
                if g == NGK - 1:
                    tn.wait_ge(s_vCK, g)
                    emit_ck(g)

    if phases == 1:
        with nc.Block() as bf:
            @bf.gpsimd
            def _(g):
                for t in range(NTQ):
                    g.dma_start(out=outh.ap()[t * 128:(t + 1) * 128, :],
                                in_=x0t[0][:, :]).then_inc(s_out, 16)
                g.wait_ge(s_out, 16 * NTQ)
                if debug:
                    nd = 0
                    g.dma_start(out=dbgh["d_kT"].ap()[:, 0:512],
                                in_=ln1t[(NTK - 1) % 2][:, 0:512]).then_inc(s_out, 16)
                    nd += 1
                    g.dma_start(out=dbgh["d_v"].ap()[:, 0:512],
                                in_=x0t[(NTK - 1) % 3][:, 0:512]).then_inc(s_out, 16)
                    nd += 1
                    g.dma_start(out=dbgh["d_qT"].ap()[:, 0:512],
                                in_=qT[0:128, 0:512]).then_inc(s_out, 16)
                    nd += 1
                    g.dma_start(out=dbgh["d_attc"].ap()[:, 0:512],
                                in_=ln1Tg[(NGK - 1) % 2][:, 0:512]).then_inc(s_out, 16)
                    nd += 1
                    g.dma_start(out=dbgh["d_crep"].ap()[0:16, 0:16],
                                in_=chist[0:16, 0:16]).then_inc(s_out, 16)
                    nd += 1
                    g.dma_start(out=dbgh["d_attc"].ap()[0:16, 0:512],
                                in_=cpsdump[0:16, 0:512]).then_inc(s_out, 16)
                    nd += 1
                    g.dma_start(out=dbgh["d_kT"].ap()[0:16, 0:20],
                                in_=chist[0:16, 20:40]).then_inc(s_out, 16)
                    nd += 1
                    g.dma_start(out=dbgh["d_crep"].ap()[0:128, 1:2],
                                in_=st_rv[0:128, 0:1],
                                allow_slow_non_contiguous=True).then_inc(s_out, 16)
                    nd += 1
                    g.dma_start(out=dbgh["d_crep"].ap()[0:128, 2:3],
                                in_=st_rs[0:128, 0:1],
                                allow_slow_non_contiguous=True).then_inc(s_out, 16)
                    nd += 1
                    g.dma_start(out=dbgh["d_crep"].ap()[0:128, 3:4],
                                in_=st_s2[0:128, 0:1],
                                allow_slow_non_contiguous=True).then_inc(s_out, 16)
                    nd += 1
                    g.dma_start(out=dbgh["d_kmax"].ap()[0:16, 0:1],
                                in_=kmax2[0:16, 0:1],
                                allow_slow_non_contiguous=True).then_inc(s_out, 16)
                    nd += 1
                    g.dma_start(out=dbgh["d_kmax"].ap()[0:16, 1:2],
                                in_=qmax2[0:16, 0:1],
                                allow_slow_non_contiguous=True).then_inc(s_out, 16)
                    nd += 1
                    g.wait_ge(s_out, 16 * (NTQ + nd))
        p1.close()
        ctx.close()
        return nc

    p1.close()

    attc = sb("attc", [128, 8 * NQ], BF16)
    Wp_sb = sb("Wp_sb", [128, 8 * D], BF16)
    Win_sb = sb("Win_sb", [128, 8 * M], BF16)
    Wout_sb = sb("Wout_sb", [64, D], BF16)
    bp_rep = sb("bp_rep", [128, D], F32)
    bout_rep = sb("bout_rep", [128, D], F32)
    w1_rep = sb("w1_rep", [128, D], F32)
    b1_rep = sb("b1_rep", [128, D], F32)
    w2_rep = sb("w2_rep", [128, D], F32)
    bin_rep = sb("bin_rep", [128, M], F32)
    b2_rep = sb("b2_rep", [128, M], F32)

    p2 = ExitStack()

    def sb2(name, shape, dt):
        return p2.enter_context(nc.sbuf_tensor(name, shape, dt))

    kdup = [sb2(f"kdup{i}", [128, NK], BF16) for i in range(2)]
    VTW = NTK * 65
    vt = [sb2(f"vt{i}", [128, VTW], BF16) for i in range(2)]
    qdup = [sb2(f"qdup{i}", [128, NQ], BF16) for i in range(2)]
    ex = [sb2(f"ex{i}", [128, 1024], BF16) for i in range(3)]
    recr = [sb2(f"recr{i}", [128, 512], F32) for i in range(2)]
    rtmp = sb2("rtmp", [1, 512], F32)
    atst = [sb2(f"atst{i}", [64, 512], BF16) for i in range(2)]
    cneg = sb2("cneg", [16, 1], F32)

    pL = p2.enter_context(nc.psum_tensor("pL", [128, 3072], F32))
    pA = [p2.enter_context(nc.psum_tensor(f"pA{i}", [128, 512], F32))
          for i in range(2)]

    DOUBLES = [(64, 1, 0, True), (0, 1, 1, False), (0, 2, 2, False),
               (0, 4, 4, False), (0, 8, 8, False), (0, 16, 16, False)]
    NRC = len(DOUBLES)

    def emit_rec_chain(eng, hq, semA, semB, idx):
        eng.wait_ge(s_vrec, hq + 1)
        r = recr[hq % 2]
        for (sA, base) in ((semA, 0), (semB, 32)):
            cnt = 16 * NRC * idx
            for (src0, n, dst0, abs_src) in DOUBLES:
                s0 = src0 if abs_src else base + src0
                if cnt:
                    eng.wait_ge(sA, cnt)
                eng.dma_start(out=r[base + dst0:base + dst0 + n, :],
                              in_=r[s0:s0 + n, :]).then_inc(sA, 16)
                cnt += 16

    def av_inc_of(G):
        hq_, g_ = G // NG, G % NG
        if g_ == NG - 1:
            return ("q", hq_ + 1)
        return ("g", (NG - 1) * hq_ + g_ + 1)

    NBG_PLAIN = 16 + 8 + 1
    rep_specs = [(bph, D), (bouth, D), (W1h, D), (b1h, D), (W2h, D),
                 (binh, M), (b2h, M)]
    rep_dst = [None] * 7
    rep_dst[0] = bp_rep; rep_dst[1] = bout_rep; rep_dst[2] = w1_rep
    rep_dst[3] = b1_rep; rep_dst[4] = w2_rep; rep_dst[5] = bin_rep
    rep_dst[6] = b2_rep
    NW2 = NBG_PLAIN + 7 * 8

    with nc.Block() as b2:
        @b2.vector
        def _(v):
            v.memset(crep[:, :], 0.0)
            v.reduce_max(kmax2[:, :], kgmax[:, :], axis=AX)
            v.reduce_max(qmax2[:, :], qgmax[:, :], axis=AX)
            v.memset(crep[:, :], 0.0)
            v.tensor_mul(cneg[:, :], kmax2[:, :], qmax2[:, :]).then_inc(s_c, 1)
            v.wait_ge(s_cs, 1)
            v.tensor_scalar(cneg[:, :], cneg[:, :], -1.0, -1.0,
                            ALU.mult, ALU.add).then_inc(s_c, 1)
            for h in range(P2H):
                for qb in range(NQB):
                    hq = NQB * h + qb
                    v.wait_ge(s_tAVq, hq + 1)
                    v.tensor_copy(recr[hq % 2][64:65, :],
                                  pA[hq % 2][64:65, :]).then_inc(s_vrec0, 1)
                    if hq % 2 == 0:
                        v.wait_ge(s_rrA, 16 * NRC * (hq // 2 + 1))
                        v.wait_ge(s_rrB, 16 * NRC * (hq // 2 + 1))
                    else:
                        v.wait_ge(s_rgA, 16 * NRC * (hq // 2 + 1))
                        v.wait_ge(s_rgB, 16 * NRC * (hq // 2 + 1))
                    if hq >= 2:
                        v.wait_ge(s_at2, 16 * (hq - 1))
                    v.tensor_tensor(
                        atst[hq % 2][:, :],
                        pA[hq % 2][0:64, :], recr[hq % 2][0:64, :],
                        ALU.mult).then_inc(s_vAT, 1)

        @b2.scalar
        def _(sc):
            def emit_recr(hq_):
                sc.wait_ge(s_vrec0, hq_ + 1)
                sc.activation(out=rtmp[0:1, :],
                              in_=recr[hq_ % 2][64:65, :], func=AF.Ln)
                sc.activation(out=recr[hq_ % 2][64:65, :],
                              in_=rtmp[0:1, :], func=AF.Exp,
                              scale=-1.0).then_inc(s_vrec, 1)

            sc.wait_ge(s_c, 1)
            sc.activation(out=cneg[:, :], in_=cneg[:, :], func=AF.Ln)
            sc.activation(out=cneg[:, :], in_=cneg[:, :],
                          func=AF.Exp, scale=0.5).then_inc(s_cs, 1)
            sc.wait_ge(s_crep, 16 * 8)
            for h in range(P2H):
                for qb in range(NQB):
                    hq = NQB * h + qb
                    for g in range(NG):
                        G = NG * hq + g
                        sc.wait_ge(s_tLp, NG * hq + g + 1)
                        if G >= 3:
                            sm, val = av_inc_of(G - 3)
                            sc.wait_ge(s_tAVg if sm == "g" else s_tAVq, val)
                        sc.activation(
                            out=ex[G % 3][:, :],
                            in_=pL[:, (g % 3) * 1024:(g % 3) * 1024 + 1024],
                            func=AF.Exp,
                            bias=crep[:, h:h + 1]).then_inc(s_exp, 1)
                        if g == 2 and hq >= 1:
                            emit_recr(hq - 1)
            emit_recr(P2H * NQB - 1)

        @b2.tensor
        def _(tn):
            for h in range(P2H):
                for qb in range(NQB):
                    hq = NQB * h + qb
                    if qb == 0:
                        tn.wait_ge(s_k, 32 * (h + 1))
                        tn.wait_ge(s_q2, 32 * (h + 1))
                        tn.wait_ge(s_v2, 16 * NVD * (h + 1))

                    def emit_av_group(g, hq=hq, h=h):
                        G = NG * hq + g
                        last_grp = (g == NG - 1)
                        tn.wait_ge(s_exp, G + 1)
                        if g == 0 and hq >= 2:
                            tn.wait_ge(s_vAT, hq - 1)
                        for i, t in enumerate((2 * g, 2 * g + 1)):
                            ins = tn.matmul(pA[hq % 2][0:65, :],
                                            vt[h % 2][:, t * 65:t * 65 + 65],
                                            ex[G % 3][:, i * 512:(i + 1) * 512],
                                            start=(t == 0), stop=(t == NTK - 1))
                        if last_grp:
                            ins.then_inc(s_tAVq, 1)
                        else:
                            ins.then_inc(s_tAVg, 1)

                    for j in range(NG):
                        Gj = NG * hq + j
                        if hq == 0 and j < 3:
                            pass
                        elif j in (0, 1):
                            tn.wait_ge(s_exp, Gj - 1)
                        elif j == 2:
                            tn.wait_ge(s_exp, Gj - 4)
                        else:
                            tn.wait_ge(s_exp, Gj - 2)
                        ta, tb = 2 * j, 2 * j + 1
                        tn.matmul(pL[:, (ta % 6) * 512:(ta % 6) * 512 + 512],
                                  kdup[h % 2][0:64, ta * 128:(ta + 1) * 128],
                                  qdup[h % 2][0:64, qb * 512:(qb + 1) * 512],
                                  start=True, stop=True, tile_position=(0, 0))
                        ins = tn.matmul(
                            pL[:, (tb % 6) * 512:(tb % 6) * 512 + 512],
                            kdup[h % 2][64:128, tb * 128:(tb + 1) * 128],
                            qdup[h % 2][64:128, qb * 512:(qb + 1) * 512],
                            start=True, stop=True, tile_position=(64, 0))
                        ins.then_inc(s_tLp, 1)
                        if j >= 2:
                            emit_av_group(j - 2)
                    emit_av_group(NG - 2)
                    emit_av_group(NG - 1)

        @b2.sync
        def _(s):
            def at2_dma(hq):
                h_, qb_ = hq // NQB, hq % NQB
                s.wait_ge(s_vAT, hq + 1)
                if p2_stub & 8:
                    s.sem_inc(s_at2, 16)
                    return
                s.dma_start(
                    out=attc[(h_ % 2) * 64:(h_ % 2) * 64 + 64,
                             (h_ // 2) * NQ + qb_ * 512:(h_ // 2) * NQ + qb_ * 512 + 512],
                    in_=atst[hq % 2][:, :]).then_inc(s_at2, 16)

            def loadkq(h):
                if p2_stub & 128:
                    s.sem_inc(s_k, 32)
                    s.sem_inc(s_q2, 32)
                    return
                s.dma_start(out=kdup[h % 2][0:64, :],
                            in_=kT[h * 64:h * 64 + 64, :]).then_inc(s_k, 16)
                s.dma_start(out=kdup[h % 2][64:128, :],
                            in_=kT[h * 64:h * 64 + 64, :]).then_inc(s_k, 16)
                s.dma_start(out=qdup[h % 2][0:64, :],
                            in_=qT[(h % 2) * 64:(h % 2) * 64 + 64,
                                   (h // 2) * NQ:(h // 2 + 1) * NQ]).then_inc(s_q2, 16)
                s.dma_start(out=qdup[h % 2][64:128, :],
                            in_=qT[(h % 2) * 64:(h % 2) * 64 + 64,
                                   (h // 2) * NQ:(h // 2 + 1) * NQ]).then_inc(s_q2, 16)

            s.wait_ge(s_spK, 16 * 8 * NGK)
            s.wait_ge(s_spV, 16 * 4 * NGK)
            loadkq(0)
            for h in range(P2H):
                if h + 1 < P2H:
                    if h >= 1:
                        s.wait_ge(s_tLp, NPAIR * NQB * h)
                    loadkq(h + 1)
                for qb in range(NQB):
                    hq = NQB * h + qb
                    if not (p2_stub & 4):
                        if hq % 2 == 0:
                            emit_rec_chain(s, hq, s_rrA, s_rrB, hq // 2)
                        else:
                            emit_rec_chain(s, hq, s_rgA, s_rgB, hq // 2)
                    at2_dma(hq)

        @b2.gpsimd
        def _(g):
            for i in range(2):
                g.memset(bass.AP(vt[i], 64, [[VTW, 128], [65, NTK], [1, 1]]), 1.0)
            g.wait_ge(s_c, 2)
            if p2_stub & 1024:
                g.sem_inc(s_crep, 16 * 8)
            else:
                g.dma_start(out=crep[0:1, 0:16],
                            in_=cneg[0:16, 0:1]).then_inc(s_crep, 16)
                cnt = 16
                for (src0, n, dst0) in [(0, 1, 1), (0, 2, 2), (0, 4, 4), (0, 8, 8),
                                        (0, 16, 16), (0, 32, 32), (0, 64, 64)]:
                    g.wait_ge(s_crep, cnt)
                    g.dma_start(out=crep[dst0:dst0 + n, :],
                                in_=crep[src0:src0 + n, :]).then_inc(s_crep, 16)
                    cnt += 16
            w2 = [0]

            def w2inc(ins):
                ins.then_inc(s_w2, 16)
                w2[0] += 16
                return w2[0]

            bg = []
            for hc in range(8):
                for hl in range(2):
                    h_ = 2 * hc + hl
                    bg.append(lambda g, hc=hc, hl=hl, h_=h_: w2inc(g.dma_start(
                        out=Wp_sb[hl * 64:(hl + 1) * 64, hc * D:(hc + 1) * D],
                        in_=bass.AP(Wph, h_ * D, [[16 * D, 64], [1, D]]))))
            for dc in range(8):
                bg.append(lambda g, dc=dc: w2inc(g.dma_start(
                    out=Win_sb[:, dc * M:(dc + 1) * M],
                    in_=bass.AP(Winh, dc * 128 * M, [[M, 128], [1, M]]))))
            bg.append(lambda g: w2inc(g.dma_start(
                out=Wout_sb[0:64, :], in_=bass.AP(Wouth, 0, [[D, 64], [1, D]]))))
            rep_state = []
            for i, (srch, width) in enumerate(rep_specs):
                rep_state.append({"dst": rep_dst[i], "srch": srch, "w": width,
                                  "k": 0, "cnt": 0})

            def emit_bg_one(g):
                if bg:
                    bg.pop(0)(g)
                    return True
                st = next((x for x in rep_state if x["k"] < 8), None)
                if st is None:
                    return False
                if st["k"] == 0:
                    st["cnt"] = w2inc(g.dma_start(
                        out=st["dst"][0:1, :],
                        in_=bass.AP(st["srch"], 0, [[st["w"], 1], [1, st["w"]]])))
                else:
                    rows = 1 << (st["k"] - 1)
                    g.wait_ge(s_w2, st["cnt"])
                    st["cnt"] = w2inc(g.dma_start(
                        out=st["dst"][rows:2 * rows, :],
                        in_=st["dst"][0:rows, :]))
                st["k"] += 1
                return True

            for h in range(P2H):
                if h >= 2:
                    g.wait_ge(s_tAVq, NQB * (h - 1))
                if p2_stub & 256:
                    g.sem_inc(s_v2, 16 * NVD)
                else:
                    for mo in range(NVD):
                        g.dma_start(
                            out=bass.AP(vt[h % 2], mo * 8 * 65,
                                        [[VTW, 128], [65, 8], [1, E]]),
                            in_=bass.AP(vhh, (mo * 8 * 128) * D + h * E,
                                        [[D, 128], [128 * D, 8], [1, E]]),
                        ).then_inc(s_v2, 16)
                if not (p2_stub & 512):
                    for _ in range(6):
                        emit_bg_one(g)

            if not (p2_stub & 512):
                while emit_bg_one(g):
                    pass
            else:
                g.sem_inc(s_w2, 16 * NW2)

    p2.close()

    if phases == 2:
        with nc.Block() as bf:
            @bf.gpsimd
            def _(g):
                for t in range(NTQ):
                    g.dma_start(out=outh.ap()[t * 128:(t + 1) * 128, :],
                                in_=bp_rep[:, :]).then_inc(s_out, 16)
                g.wait_ge(s_out, 16 * NTQ)
        ctx.close()
        return nc

    p3 = ExitStack()

    def sb3(name, shape, dt):
        return p3.enter_context(nc.sbuf_tensor(name, shape, dt))

    x3t = [sb3(f"x3t{i}", [128, D], F32) for i in range(2)]
    zt = sb3("zt", [128, D], F32)
    zt2 = sb3("zt2", [128, D], F32)
    x1t = [sb3(f"x1t{i}", [128, D], F32) for i in range(3)]
    ln2t = [sb3(f"ln2t{i}", [128, D], BF16) for i in range(2)]
    ln2T = [sb3(f"ln2T{i}", [128, D], BF16) for i in range(2)]
    s_sb = [sb3(f"s_sb{i}", [128, M], F32) for i in range(2)]
    hbuf = sb3("hbuf", [128, D], F32)
    g0b = sb3("g0b", [128, M], F32)
    gsb = [sb3(f"gsb{i}", [128, M], BF16) for i in range(2)]
    gT = [sb3(f"gT{i}", [64, 128], BF16) for i in range(2)]
    outt = [sb3(f"outt{i}", [128, D], F32) for i in range(2)]
    sq3 = sb3("sq3", [128, D], F32)
    sq3b = sb3("sq3b", [128, D], F32)
    st3 = [sb3(f"st3_{i}", [128, 1], F32) for i in range(6)]
    st3g = sb3("st3g", [128, 1], F32)

    pO = p3.enter_context(nc.psum_tensor("pO", [128, 1024], F32))
    pF = p3.enter_context(nc.psum_tensor("pF", [128, 1024], F32))
    pS = p3.enter_context(nc.psum_tensor("pS", [128, 512], F32))
    ptp3 = [p3.enter_context(nc.psum_tensor(f"ptp3{i}", [128, 128], BF16))
            for i in range(2)]
    ptg = p3.enter_context(nc.psum_tensor("ptg", [128, 128], BF16))

    NT3 = NTQ

    V = {}
    cv = 0
    for t in range(NT3):
        if t >= 1:
            cv += 1; V[("sS", t - 1)] = cv
        if t >= 2:
            cv += 1; V[("gsb", t - 2)] = cv
        cv += 1; V[("a", t)] = cv
        cv += 1; V[("b", t)] = cv
        cv += 1; V[("ln2", t)] = cv
        if t >= 2:
            cv += 1; V[("cF", t - 2)] = cv
            cv += 1; V[("out", t - 2)] = cv
    cv += 1; V[("sS", NT3 - 1)] = cv
    cv += 1; V[("gsb", NT3 - 2)] = cv
    cv += 1; V[("cF", NT3 - 2)] = cv
    cv += 1; V[("out", NT3 - 2)] = cv
    cv += 1; V[("gsb", NT3 - 1)] = cv
    cv += 1; V[("cF", NT3 - 1)] = cv
    cv += 1; V[("out", NT3 - 1)] = cv

    Tt = {}
    ct = 0
    for t in range(NT3):
        ct += 1; Tt[("proj", t)] = ct
        if t >= 2:
            ct += 1; Tt[("ptg", t - 2)] = ct
            ct += 1; Tt[("wout", t - 2)] = ct
        for i in range(8):
            ct += 1; Tt[("tp", t, i)] = ct
        ct += 1; Tt[("win", t)] = ct
    ct += 1; Tt[("ptg", NT3 - 2)] = ct
    ct += 1; Tt[("wout", NT3 - 2)] = ct
    ct += 1; Tt[("ptg", NT3 - 1)] = ct
    ct += 1; Tt[("wout", NT3 - 1)] = ct

    Sc = {}
    Cc = {}
    cs3 = 0
    cc3 = 0
    for t in range(NT3):
        if t >= 2:
            cc3 += 1; Cc[("gt", t - 2)] = cc3
        cs3 += 1; Sc[("a", t)] = cs3
        cs3 += 1; Sc[("b", t)] = cs3
        for i in range(8):
            cc3 += 1; Cc[("tp", t, i)] = cc3
        if t >= 2:
            cs3 += 1; Sc[("c", t - 2)] = cs3
    cc3 += 1; Cc[("gt", NT3 - 2)] = cc3
    cs3 += 1; Sc[("c", NT3 - 2)] = cs3
    cc3 += 1; Cc[("gt", NT3 - 1)] = cc3
    cs3 += 1; Sc[("c", NT3 - 1)] = cs3

    Gg = {}
    cg3 = 0
    for t in range(NT3):
        cg3 += 1; Gg[("hb", t)] = cg3

    with nc.Block() as b3:
        @b3.sync
        def _(s):
            for t in range(min(2, NT3)):
                s.dma_start(out=x3t[t % 2][:, :],
                            in_=xb[t * 128:(t + 1) * 128, :]).then_inc(s3_x, 16)
            for t in range(NT3):
                if t + 2 < NT3:
                    s.wait_ge(s3_v, V[("b", t)])
                    s.dma_start(out=x3t[t % 2][:, :],
                                in_=xb[(t + 2) * 128:(t + 3) * 128, :]
                                ).then_inc(s3_x, 16)
                s.wait_ge(s3_v, V[("out", t)])
                s.dma_start(out=outh.ap()[t * 128:(t + 1) * 128, :],
                            in_=outt[t % 2][:, :]).then_inc(s_out, 16)
            s.wait_ge(s_out, 16 * NT3)

        @b3.tensor
        def _(tn):
            tn.wait_ge(s_w2, 16 * NW2)
            tn.wait_ge(s_at2, 16 * H * NQB)

            def emit_pw(tt):
                tn.wait_ge(s3_v, V[("gsb", tt)])
                if tt >= 1:
                    tn.wait_ge(s3_c, Cc[("gt", tt - 1)])
                tn.transpose(ptg[0:64, 0:128], gsb[tt % 2][:, :],
                             identb[:, :]).then_inc(s3_t, 1)
                tn.wait_ge(s3_c, Cc[("gt", tt)])
                if tt >= 1:
                    tn.wait_ge(s3_v, V[("cF", tt - 1)])
                tn.matmul(pF[:, 0:512], gT[tt % 2][:, :],
                          Wout_sb[0:64, 0:512], start=True, stop=True)
                tn.matmul(pF[:, 512:1024], gT[tt % 2][:, :],
                          Wout_sb[0:64, 512:1024],
                          start=True, stop=True).then_inc(s3_t, 1)

            for t in range(NT3):
                qb, tq = t // 4, t % 4
                if t >= 1:
                    tn.wait_ge(s3_v, V[("a", t - 1)])
                for hc in range(8):
                    lhs = attc[:, hc * NQ + qb * 512 + tq * 128:
                               hc * NQ + qb * 512 + tq * 128 + 128]
                    tn.matmul(pO[:, 0:512], lhs,
                              Wp_sb[:, hc * D:hc * D + 512],
                              start=(hc == 0), stop=(hc == 7))
                    ins = tn.matmul(pO[:, 512:1024], lhs,
                                    Wp_sb[:, hc * D + 512:hc * D + 1024],
                                    start=(hc == 0), stop=(hc == 7))
                ins.then_inc(s3_t, 1)
                if t >= 2:
                    emit_pw(t - 2)
                tn.wait_ge(s3_v, V[("ln2", t)])
                for i in range(8):
                    gi = 8 * t + i
                    if gi >= 2:
                        tt_, ii_ = (t, i - 2) if i >= 2 else (t - 1, i + 6)
                        tn.wait_ge(s3_c, Cc[("tp", tt_, ii_)])
                    tn.transpose(ptp3[i % 2][:, 0:128],
                                 ln2t[t % 2][:, i * 128:(i + 1) * 128],
                                 identb[:, :]).then_inc(s3_t, 1)
                tn.wait_ge(s3_c, Cc[("tp", t, 7)])
                if t >= 1:
                    tn.wait_ge(s3_v, V[("sS", t - 1)])
                for dc in range(8):
                    ins = tn.matmul(pS[:, 0:64],
                                    ln2T[t % 2][:, dc * 128:(dc + 1) * 128],
                                    Win_sb[:, dc * M:(dc + 1) * M],
                                    start=(dc == 0), stop=(dc == 7))
                ins.then_inc(s3_t, 1)
            emit_pw(NT3 - 2)
            emit_pw(NT3 - 1)

        @b3.vector
        def _(v):
            def ln_stats(src):
                v.reduce_sum(st3[4][:, :], src[:, :], axis=AX)
                v.tensor_mul(sq3[:, :], src[:, :], src[:, :])
                v.reduce_sum(st3[5][:, :], sq3[:, :], axis=AX)
                v.tensor_scalar_mul(st3[1][:, :], st3[4][:, :], 1.0 / D)
                v.tensor_scalar_mul(st3[5][:, :], st3[5][:, :], 1.0 / D)
                v.tensor_mul(st3[4][:, :], st3[1][:, :], st3[1][:, :])
                v.tensor_sub(st3[5][:, :], st3[5][:, :], st3[4][:, :])
                return v.tensor_scalar_add(st3[5][:, :], st3[5][:, :], EPS)

            def emit_sS(tt):
                v.wait_ge(s3_t, Tt[("win", tt)])
                if tt >= 2:
                    v.wait_ge(s3_g, Gg[("hb", tt - 2)])
                v.tensor_add(s_sb[tt % 2][:, :], pS[:, 0:64],
                             bin_rep[:, :]).then_inc(s3_v, 1)

            def emit_R(tt):
                v.wait_ge(s3_g, Gg[("hb", tt)])
                if tt >= 2:
                    v.wait_ge(s3_t, Tt[("ptg", tt - 2)])
                v.reduce_sum(g0b[:, :],
                             bass.AP(hbuf, 0,
                                     [[D, 128], [INNER, M], [1, INNER]]),
                             axis=AX)
                v.tensor_copy(st3g[:, :], b2_rep[:, 0:1])
                v.tensor_add(gsb[tt % 2][:, :], g0b[:, :],
                             b2_rep[:, :]).then_inc(s3_v, 1)

            def emit_A(tt):
                v.wait_ge(s3_t, Tt[("proj", tt)])
                v.tensor_add(zt[:, :], pO[:, :], bp_rep[:, :])
                ln_stats(zt).then_inc(s3_v, 1)
                v.wait_ge(s3_s, Sc[("a", tt)])
                v.wait_ge(s3_x, 16 * (tt + 1))
                v.tensor_scalar(sq3[:, :], zt[:, :], st3[1][:, :], st3[2][:, :],
                                ALU.subtract, ALU.mult)
                v.tensor_add(x1t[tt % 3][:, :], x3t[tt % 2][:, :], sq3[:, :])
                ln_stats(x1t[tt % 3]).then_inc(s3_v, 1)
                v.wait_ge(s3_s, Sc[("b", tt)])
                if tt >= 2:
                    v.wait_ge(s3_t, Tt[("tp", tt - 2, 7)])
                v.tensor_scalar(ln2t[tt % 2][:, :], x1t[tt % 3][:, :],
                                st3[1][:, :], st3[2][:, :],
                                ALU.subtract, ALU.mult).then_inc(s3_v, 1)

            def emit_F(tt):
                v.wait_ge(s3_t, Tt[("wout", tt)])
                v.tensor_add(zt2[:, :], pF[:, :], bout_rep[:, :])
                ln_stats(zt2).then_inc(s3_v, 1)
                v.wait_ge(s3_s, Sc[("c", tt)])
                if tt >= 2:
                    v.wait_ge(s_out, 16 * (tt - 1))
                v.tensor_scalar(sq3b[:, :], zt2[:, :], st3[1][:, :],
                                st3[2][:, :], ALU.subtract, ALU.mult)
                v.tensor_add(outt[tt % 2][:, :], x1t[tt % 3][:, :],
                             sq3b[:, :]).then_inc(s3_v, 1)

            v.wait_ge(s_w2, 16 * NW2)
            emit_A(0)
            for t in range(1, NT3):
                emit_sS(t - 1)
                if t >= 2:
                    emit_R(t - 2)
                emit_A(t)
                if t >= 2:
                    emit_F(t - 2)
            emit_sS(NT3 - 1)
            emit_R(NT3 - 2)
            emit_F(NT3 - 2)
            emit_R(NT3 - 1)
            emit_F(NT3 - 1)

        @b3.scalar
        def _(sc):
            def rsqrt(v_need):
                sc.wait_ge(s3_v, v_need)
                sc.activation(out=st3[0][:, :], in_=st3[5][:, :], func=AF.Ln)
                sc.activation(out=st3[2][:, :], in_=st3[0][:, :],
                              func=AF.Exp, scale=-0.5).then_inc(s3_s, 1)

            def emit_gt(tt):
                sc.wait_ge(s3_t, Tt[("ptg", tt)])
                if tt >= 2:
                    sc.wait_ge(s3_t, Tt[("wout", tt - 2)])
                sc.activation(out=gT[tt % 2][:, :], in_=ptg[0:64, 0:128],
                              func=AF.Copy).then_inc(s3_c, 1)

            for t in range(NT3):
                if t >= 2:
                    emit_gt(t - 2)
                rsqrt(V[("a", t)])
                rsqrt(V[("b", t)])
                for i in range(8):
                    sc.wait_ge(s3_t, Tt[("tp", t, i)])
                    if t >= 2 and i == 0:
                        sc.wait_ge(s3_t, Tt[("win", t - 2)])
                    sc.activation(out=ln2T[t % 2][:, i * 128:(i + 1) * 128],
                                  in_=ptp3[i % 2][:, 0:128],
                                  func=AF.Copy).then_inc(s3_c, 1)
                if t >= 2:
                    rsqrt(V[("cF", t - 2)])
            emit_gt(NT3 - 2)
            rsqrt(V[("cF", NT3 - 2)])
            emit_gt(NT3 - 1)
            rsqrt(V[("cF", NT3 - 1)])

        @b3.gpsimd
        def _(g):
            g.wait_ge(s_w2, 16 * NW2)
            for t in range(NT3):
                g.wait_ge(s3_v, V[("sS", t)])
                if t >= 1:
                    g.wait_ge(s3_v, V[("gsb", t - 1)])
                g.tensor_tensor(hbuf[:, :],
                                bass.AP(s_sb[t % 2], 0,
                                        [[M, 128], [1, M], [0, INNER]]),
                                w1_rep[:, :], ALU.mult)
                g.tensor_add(hbuf[:, :], hbuf[:, :], b1_rep[:, :])
                g.tensor_scalar_max(hbuf[:, :], hbuf[:, :], 0.0)
                g.tensor_mul(hbuf[:, :], hbuf[:, :],
                             w2_rep[:, :]).then_inc(s3_g, 1)
            g.wait_ge(s_out, 16 * NT3)
            if debug:
                nd = 0
                for nm, src in (("d_qT", qT[:, 0:512]),
                                ("d_crep", crep[:, :]),
                                ("d_attc", attc[:, 0:512])):
                    g.dma_start(out=x3t[0][:, 0:src.shape[-1]], in_=src).then_inc(s_out, 16)
                    nd += 1
                    g.wait_ge(s_out, 16 * (NT3 + nd))
                    g.dma_start(out=dbgh[nm].ap()[0:src.shape[0], 0:src.shape[-1]],
                                in_=x3t[0][0:src.shape[0], 0:src.shape[-1]]).then_inc(s_out, 16)
                    nd += 1
                    g.wait_ge(s_out, 16 * (NT3 + nd))
                g.dma_start(out=dbgh["d_kT"].ap()[:, 0:512],
                            in_=kT[0:128, 0:512]).then_inc(s_out, 16)
                nd += 1
                g.wait_ge(s_out, 16 * (NT3 + nd))
                g.dma_start(out=dbgh["d_v"].ap()[:, 0:512],
                            in_=vh[0:128, 0:512]).then_inc(s_out, 16)
                nd += 1
                g.wait_ge(s_out, 16 * (NT3 + nd))
                g.dma_start(out=dbgh["d_kmax"].ap()[0:16, 0:1],
                            in_=kmax2[0:16, 0:1],
                            allow_slow_non_contiguous=True).then_inc(s_out, 16)
                nd += 1
                g.dma_start(out=dbgh["d_kmax"].ap()[0:16, 1:2],
                            in_=qmax2[0:16, 0:1],
                            allow_slow_non_contiguous=True).then_inc(s_out, 16)
                nd += 1
                g.dma_start(out=dbgh["d_cq"].ap()[:, :],
                            in_=dbg_cq[:, :]).then_inc(s_out, 16)
                nd += 1
                g.dma_start(out=dbgh["d_ck"].ap()[:, :],
                            in_=dbg_ck[:, :]).then_inc(s_out, 16)
                nd += 1
                g.dma_start(out=dbgh["d_q"].ap()[:, :],
                            in_=dbg_q[:, :]).then_inc(s_out, 16)
                nd += 1
                g.wait_ge(s_out, 16 * (NT3 + nd))

    p3.close()
    ctx.close()
    return nc


def kernel(L0_attn_query, L0_attn_Wq, L0_attn_Wk, L0_attn_Wv, L0_attn_Wp,
           L0_attn_bp, L0_ffn_Win, L0_ffn_bin_, L0_ffn_W1, L0_ffn_b1,
           L0_ffn_W2, L0_ffn_b2, L0_ffn_Wout, L0_ffn_bout):
    x = np.ascontiguousarray(np.asarray(L0_attn_query, dtype=np.float32))
    assert x.shape == (2, 8192, 1024)
    nc = build_nc(NK=8192, NQ=2048)
    weights = {
        "Wq": np.ascontiguousarray(np.asarray(L0_attn_Wq, np.float32)),
        "Wk": np.ascontiguousarray(np.asarray(L0_attn_Wk, np.float32)),
        "Wv": np.ascontiguousarray(np.asarray(L0_attn_Wv, np.float32)),
        "Wp": np.ascontiguousarray(np.asarray(L0_attn_Wp, np.float32)),
        "bp": np.ascontiguousarray(np.asarray(L0_attn_bp, np.float32)),
        "Win": np.ascontiguousarray(np.asarray(L0_ffn_Win, np.float32)),
        "binp": np.ascontiguousarray(np.asarray(L0_ffn_bin_, np.float32)),
        "W1": np.ascontiguousarray(np.asarray(L0_ffn_W1, np.float32)),
        "b1": np.ascontiguousarray(np.asarray(L0_ffn_b1, np.float32)),
        "W2": np.ascontiguousarray(np.asarray(L0_ffn_W2, np.float32)),
        "b2": np.ascontiguousarray(np.asarray(L0_ffn_b2, np.float32)),
        "Wout": np.ascontiguousarray(np.asarray(L0_ffn_Wout, np.float32)),
        "bout": np.ascontiguousarray(np.asarray(L0_ffn_bout, np.float32)),
    }
    in_maps = []
    for c in range(8):
        b, qq = c // 4, c % 4
        xr = np.roll(x[b], -qq * 2048, axis=0)
        m = {"xb": np.ascontiguousarray(xr)}
        m.update(weights)
        in_maps.append(m)
    res = run_bass_kernel_spmd(nc, in_maps, core_ids=list(range(8)))
    out = np.empty((2, 8192, 1024), np.float32)
    for c in range(8):
        b, qq = c // 4, c % 4
        out[b, qq * 2048:(qq + 1) * 2048] = res.results[c]["out"]
    return out


# version 13
# speedup vs baseline: 1.0723x; 1.0505x over previous
import numpy as np
from contextlib import ExitStack

import concourse.bass as bass
import concourse.mybir as mybir
from concourse.bass_utils import run_bass_kernel_spmd

F32 = mybir.dt.float32
BF16 = mybir.dt.bfloat16
AX = mybir.AxisListType.X
ALU = mybir.AluOpType
AF = mybir.ActivationFunctionType

D = 1024
H = 16
E = 64
M = 64
INNER = 16
EPS = 1e-5
SCALE = 0.125


def build_nc(NK=8192, NQ=2048, debug=False, phases=3, p2_heads=None, p2_stub=0):
    NTK = NK // 128
    NGK = NK // 512
    NTQ = NQ // 128
    NGQ = max(1, NQ // 512)
    NQB = max(1, NQ // 512)
    NG = NTK // 2
    NPAIR = NTK // 2
    NVD = max(1, NTK // 8)
    P2H = H if p2_heads is None else p2_heads

    nc = bass.Bass("TRN2", target_bir_lowering=False,
                   detect_race_conditions=False)

    xbh = nc.dram_tensor("xb", [NK, D], F32, kind="ExternalInput")
    Wqh = nc.dram_tensor("Wq", [H, D, E], F32, kind="ExternalInput")
    Wkh = nc.dram_tensor("Wk", [H, D, E], F32, kind="ExternalInput")
    Wvh = nc.dram_tensor("Wv", [H, D, E], F32, kind="ExternalInput")
    Wph = nc.dram_tensor("Wp", [D, D], F32, kind="ExternalInput")
    bph = nc.dram_tensor("bp", [D], F32, kind="ExternalInput")
    Winh = nc.dram_tensor("Win", [D, M], F32, kind="ExternalInput")
    binh = nc.dram_tensor("binp", [M], F32, kind="ExternalInput")
    W1h = nc.dram_tensor("W1", [M, INNER], F32, kind="ExternalInput")
    b1h = nc.dram_tensor("b1", [M, INNER], F32, kind="ExternalInput")
    W2h = nc.dram_tensor("W2", [M, INNER], F32, kind="ExternalInput")
    b2h = nc.dram_tensor("b2", [M], F32, kind="ExternalInput")
    Wouth = nc.dram_tensor("Wout", [M, D], F32, kind="ExternalInput")
    bouth = nc.dram_tensor("bout", [D], F32, kind="ExternalInput")
    outh = nc.dram_tensor("out", [NQ, D], F32, kind="ExternalOutput")

    kTh = nc.dram_tensor("kT_scr", [D, NK], BF16, kind="Internal")
    vhh = nc.dram_tensor("v_scr", [NK, D], BF16, kind="Internal")
    dbgh = {}
    if debug:
        for nm, shp in (("d_kT", [128, 512]), ("d_v", [128, 512]),
                        ("d_qT", [128, 512]), ("d_crep", [128, 16]),
                        ("d_attc", [128, 512]), ("d_kmax", [16, 2]),
                        ("d_cq", [16, 4 * 512]), ("d_ck", [16, 512]),
                        ("d_q", [128, 512])):
            dbgh[nm] = nc.dram_tensor(nm, shp, F32, kind="ExternalOutput")

    xb = xbh.ap()
    kT = kTh.ap()
    vh = vhh.ap()

    ctx = ExitStack()

    def sb(name, shape, dt):
        return ctx.enter_context(nc.sbuf_tensor(name, shape, dt))

    def sem(name):
        return ctx.enter_context(nc.semaphore(name))

    identb = sb("identb", [128, 128], BF16)
    esel = sb("esel", [128, 8 * 16], BF16)
    if debug:
        dbg_cq = sb("dbg_cq", [16, 4 * 512], F32)
        dbg_ck = sb("dbg_ck", [16, 512], F32)
        dbg_q = sb("dbg_q", [128, 512], F32)
    qT = sb("qT", [128, 8 * NQ], BF16)
    kmax2 = sb("kmax2", [16, 1], F32)
    qmax2 = sb("qmax2", [16, 1], F32)
    kgmax = sb("kgmax", [16, NGK], F32)
    qgmax = sb("qgmax", [16, max(1, NGQ)], F32)
    crep = sb("crep", [128, 16], F32)

    s_w = sem("s_w")
    s_x = sem("s_x"); s_vS = sem("s_vS"); s_sS = sem("s_sS")
    s_vLN = sem("s_vLN"); s_tTP = sem("s_tTP"); s_vTP = sem("s_vTP")
    s_tK = sem("s_tK"); s_vK = sem("s_vK"); s_tV = sem("s_tV"); s_vV = sem("s_vV")
    s_tQ = sem("s_tQ"); s_vQ = sem("s_vQ")
    s_vSQ = sem("s_vSQ"); s_tCK = sem("s_tCK"); s_vCK = sem("s_vCK")
    s_vQS = sem("s_vQS"); s_tCQ = sem("s_tCQ"); s_vCQ = sem("s_vCQ")
    s_spK = sem("s_spK"); s_spV = sem("s_spV"); s_tG = sem("s_tG")
    s_c = sem("s_c"); s_cs = sem("s_cs"); s_crep = sem("s_crep")
    s_k = sem("s_k"); s_v2 = sem("s_v2"); s_q2 = sem("s_q2")
    s_tLp = sem("s_tLp"); s_tL = sem("s_tL")
    s_exp = sem("s_exp"); s_tAVg = sem("s_tAVg"); s_tAVq = sem("s_tAVq")
    s_vrec = sem("s_vrec"); s_vrec0 = sem("s_vrec0")
    s_rrA = sem("s_rrA"); s_rrB = sem("s_rrB")
    s_rgA = sem("s_rgA"); s_rgB = sem("s_rgB"); s_at2 = sem("s_at2")
    s_vAT = sem("s_vAT")
    s_w2 = sem("s_w2")
    s3_x = sem("s3_x"); s3_t = sem("s3_t"); s3_v = sem("s3_v"); s3_s = sem("s3_s")
    s3_c = sem("s3_c"); s3_g = sem("s3_g")
    s_out = sem("s_out")

    p1 = ExitStack()

    def sb1(name, shape, dt):
        return p1.enter_context(nc.sbuf_tensor(name, shape, dt))

    def ps1(name, shape, dt):
        return p1.enter_context(nc.psum_tensor(name, shape, dt))

    Wq_sb = sb1("Wq_sb", [128, 8 * D], BF16)
    Wk_sb = sb1("Wk_sb", [128, 8 * D], BF16)
    Wv_sb = sb1("Wv_sb", [128, 8 * D], BF16)
    x0t = [sb1(f"x0t{i}", [128, D], F32) for i in range(3)]
    sqv = sb1("sqv", [128, D], F32)
    ln1t = [sb1(f"ln1t{i}", [128, D], BF16) for i in range(2)]
    ln1Tg = [sb1(f"ln1Tg{i}", [128, 8 * 512], BF16) for i in range(2)]
    kTsp = [sb1(f"kTsp{i}", [128, 8 * 512], BF16) for i in range(2)]
    vsp = [sb1(f"vsp{i}", [128, 4 * D], BF16) for i in range(2)]
    ksq = [sb1(f"ksq{i}", [128, 8 * 512], BF16) for i in range(2)]
    qsq = [sb1(f"qsq{i}", [128, 8 * 512], BF16) for i in range(2)]
    st_s1 = sb1("st_s1", [128, 1], F32)
    st_s2 = sb1("st_s2", [128, 1], F32)
    st_m = sb1("st_m", [128, 1], F32)
    st_t = sb1("st_t", [128, 1], F32)
    st_rv = sb1("st_rv", [128, 1], F32)
    st_rs = sb1("st_rs", [128, 1], F32)
    cmx = sb1("cmx", [16, 1], F32)

    ptp = [ps1(f"ptp{i}", [128, 1024], BF16) for i in range(2)]
    kps = [ps1(f"kps{i}", [128, 512], F32) for i in range(2)]
    vps = ps1("vps", [128, 1024], F32)
    cps = ps1("cps", [128, 512], F32)
    cpsq = ps1("cpsq", [128, 512], F32)

    NW1 = 24

    with nc.Block() as b0:
        @b0.gpsimd
        def _(g):
            for dc in range(8):
                for (Wh, Wsb) in ((Wqh, Wq_sb), (Wkh, Wk_sb), (Wvh, Wv_sb)):
                    g.dma_start(
                        out=Wsb[:, dc * D:(dc + 1) * D],
                        in_=bass.AP(Wh, dc * 128 * E, [[E, 128], [D * E, H], [1, E]]),
                    ).then_inc(s_w, 16)
            g.memset(identb[:, :], 1.0)
            g.affine_select(out=identb[:, :], in_=identb[:, :],
                            compare_op=ALU.is_equal, fill=0.0, base=0,
                            pattern=[[-1, 128]], channel_multiplier=1)

        @b0.vector
        def _(v):
            v.memset(esel[:, :], 0.0)
            for dc in range(8):
                v.memset(esel[0:64, dc * 16 + 2 * dc:dc * 16 + 2 * dc + 1], 1.0)
                v.memset(esel[64:128, dc * 16 + 2 * dc + 1:dc * 16 + 2 * dc + 2], 1.0)
            v.memset(kmax2[:, :], 0.0)
            v.memset(qmax2[:, :], 0.0)

    with nc.Block() as b1:
        @b1.sync
        def _(s):
            def spill(gp):
                s.wait_ge(s_vK, 8 * gp + 8)
                for hc in range(8):
                    s.dma_start(
                        out=kT[hc * 128:(hc + 1) * 128, gp * 512:(gp + 1) * 512],
                        in_=kTsp[gp % 2][:, hc * 512:(hc + 1) * 512],
                    ).then_inc(s_spK, 16)
                s.wait_ge(s_vV, 4 * gp + 4)
                for mt in range(4):
                    s.dma_start(
                        out=vh[(4 * gp + mt) * 128:(4 * gp + mt + 1) * 128, :],
                        in_=vsp[gp % 2][:, mt * D:(mt + 1) * D],
                    ).then_inc(s_spV, 16)

            for g in range(NGK):
                spill(g)

        @b1.vector
        def _(v):
            for g in range(NGK):
                for tq in range(4):
                    t = 4 * g + tq
                    v.wait_ge(s_x, 16 * (t + 1))
                    x = x0t[t % 3]
                    v.reduce_sum(st_s1[:, :], x[:, :], axis=AX)
                    v.tensor_mul(sqv[:, :], x[:, :], x[:, :])
                    v.reduce_sum(st_s2[:, :], sqv[:, :], axis=AX)
                    v.tensor_scalar_mul(st_m[:, :], st_s1[:, :], 1.0 / D)
                    v.tensor_scalar_mul(st_t[:, :], st_s2[:, :], 1.0 / D)
                    v.tensor_mul(st_s1[:, :], st_m[:, :], st_m[:, :])
                    v.tensor_sub(st_t[:, :], st_t[:, :], st_s1[:, :])
                    v.tensor_scalar_add(st_t[:, :], st_t[:, :],
                                        EPS).then_inc(s_vS, 1)
                    v.wait_ge(s_sS, t + 1)
                    v.tensor_scalar(ln1t[t % 2][:, :], x[:, :], st_m[:, :],
                                    st_rs[:, :], ALU.subtract,
                                    ALU.mult).then_inc(s_vLN, 1)
                    if tq == 0 and g >= 2:
                        if g - 2 < NGQ:
                            v.wait_ge(s_tQ, 8 * (g - 1))
                        else:
                            v.wait_ge(s_tV, 4 * (g - 1))
                    for dc in range(8):
                        v.wait_ge(s_tTP, 8 * t + dc + 1)
                        v.tensor_copy(
                            ln1Tg[g % 2][:, dc * 512 + tq * 128:dc * 512 + tq * 128 + 128],
                            ptp[dc % 2][:, 0:128]).then_inc(s_vTP, 1)
                if g >= 2:
                    v.wait_ge(s_spK, 16 * 8 * (g - 1))
                    v.wait_ge(s_spV, 16 * 4 * (g - 1))
                for hc in range(8):
                    KI = 8 * g + hc
                    v.wait_ge(s_tK, KI + 1)
                    v.tensor_copy(kTsp[g % 2][:, hc * 512:(hc + 1) * 512],
                                  kps[KI % 2][:, :]).then_inc(s_vK, 1)
                    v.tensor_mul(ksq[g % 2][:, hc * 512:(hc + 1) * 512],
                                 kTsp[g % 2][:, hc * 512:(hc + 1) * 512],
                                 kTsp[g % 2][:, hc * 512:(hc + 1) * 512]
                                 ).then_inc(s_vSQ, 1)
                for mt in range(4):
                    VI = 4 * g + mt
                    v.wait_ge(s_tV, VI + 1)
                    v.tensor_copy(vsp[g % 2][:, mt * D:(mt + 1) * D],
                                  vps[:, :]).then_inc(s_vV, 1)
                if g >= 2:
                    v.wait_ge(s_tCK, g - 1)
                    if debug and g == 2:
                        v.tensor_copy(dbg_ck[:, :], cps[0:16, :])
                    v.reduce_max(kgmax[:, g - 2:g - 1], cps[0:16, :],
                                 axis=AX).then_inc(s_vCK, 1)
                if 2 <= g and g - 2 < NGQ:
                    v.wait_ge(s_tCQ, g - 1)
                    if debug:
                        v.tensor_copy(dbg_cq[:, (g - 2) * 512:(g - 1) * 512],
                                      cpsq[0:16, :])
                    v.reduce_max(qgmax[:, g - 2:g - 1], cpsq[0:16, :],
                                 axis=AX).then_inc(s_vCQ, 1)
            v.wait_ge(s_tCK, NGK - 1)
            v.reduce_max(kgmax[:, NGK - 2:NGK - 1], cps[0:16, :],
                         axis=AX).then_inc(s_vCK, 1)
            v.wait_ge(s_tCK, NGK)
            v.reduce_max(kgmax[:, NGK - 1:NGK], cps[0:16, :],
                         axis=AX).then_inc(s_vCK, 1)

        @b1.scalar
        def _(sc):
            def xload(t):
                sc.dma_start(out=x0t[t % 3][:, :],
                             in_=xb[t * 128:(t + 1) * 128, :]).then_inc(s_x, 16)

            for t in range(3):
                xload(t)
            for t in range(NTK):
                sc.wait_ge(s_vS, t + 1)
                sc.activation(out=st_rv[:, :], in_=st_t[:, :], func=AF.Ln)
                sc.activation(out=st_rs[:, :], in_=st_rv[:, :],
                              func=AF.Exp, scale=-0.5).then_inc(s_sS, 1)
                if t + 3 < NTK:
                    sc.wait_ge(s_vLN, t + 1)
                    xload(t + 3)
                if t % 4 == 0:
                    gq = t // 4 - 1
                    if 0 <= gq < NGQ:
                        for hc in range(8):
                            QI = 8 * gq + hc
                            sc.wait_ge(s_tQ, QI + 1)
                            sc.activation(
                                out=qT[:, hc * NQ + gq * 512:hc * NQ + gq * 512 + 512],
                                in_=kps[QI % 2][:, :], func=AF.Copy,
                                scale=SCALE).then_inc(s_vQ, 1)
                            sc.activation(
                                out=qsq[gq % 2][:, hc * 512:(hc + 1) * 512],
                                in_=kps[QI % 2][:, :], func=AF.Square,
                                scale=SCALE).then_inc(s_vQS, 1)

        @b1.tensor
        def _(tn):
            tn.wait_ge(s_w, 16 * NW1)
            for g in range(NGK):
                for tq in range(4):
                    t = 4 * g + tq
                    tn.wait_ge(s_vLN, t + 1)
                    for dc in range(8):
                        i = 8 * t + dc
                        if i >= 2:
                            tn.wait_ge(s_vTP, i - 1)
                        tn.transpose(ptp[dc % 2][:, 0:128],
                                     ln1t[t % 2][:, dc * 128:(dc + 1) * 128],
                                     identb[:, :]).then_inc(s_tTP, 1)
                tn.wait_ge(s_vTP, 8 * (4 * g + 4))
                for hc in range(8):
                    KI = 8 * g + hc
                    if KI >= 2:
                        tn.wait_ge(s_vK, KI - 1)
                    if g >= 1 and g - 1 < NGQ:
                        tn.wait_ge(s_vQS, 8 * (g - 1) + hc + 1)
                    for dc in range(8):
                        ins = tn.matmul(
                            kps[KI % 2][:, :],
                            Wk_sb[:, dc * D + hc * 128:dc * D + hc * 128 + 128],
                            ln1Tg[g % 2][:, dc * 512:(dc + 1) * 512],
                            start=(dc == 0), stop=(dc == 7))
                    ins.then_inc(s_tK, 1)
                for mt in range(4):
                    VI = 4 * g + mt
                    if VI >= 1:
                        tn.wait_ge(s_vV, VI)
                    for dc in range(8):
                        tn.matmul(vps[:, 0:512],
                                  ln1Tg[g % 2][:, dc * 512 + mt * 128:dc * 512 + mt * 128 + 128],
                                  Wv_sb[:, dc * D:dc * D + 512],
                                  start=(dc == 0), stop=(dc == 7))
                        ins = tn.matmul(
                            vps[:, 512:1024],
                            ln1Tg[g % 2][:, dc * 512 + mt * 128:dc * 512 + mt * 128 + 128],
                            Wv_sb[:, dc * D + 512:dc * D + 1024],
                            start=(dc == 0), stop=(dc == 7))
                    ins.then_inc(s_tV, 1)
                if g < NGQ:
                    tn.wait_ge(s_vK, 8 * (g + 1))
                    for hc in range(8):
                        QI = 8 * g + hc
                        if QI >= 2:
                            tn.wait_ge(s_vQS, QI - 1)
                        for dc in range(8):
                            ins = tn.matmul(
                                kps[QI % 2][:, :],
                                Wq_sb[:, dc * D + hc * 128:dc * D + hc * 128 + 128],
                                ln1Tg[g % 2][:, dc * 512:(dc + 1) * 512],
                                start=(dc == 0), stop=(dc == 7))
                        ins.then_inc(s_tQ, 1)
                def emit_ck(gp):
                    if gp >= 1:
                        tn.wait_ge(s_vCK, gp)
                    for hc in range(8):
                        tn.wait_ge(s_vSQ, 8 * gp + hc + 1)
                        ins = tn.matmul(cps[0:16, :],
                                        esel[:, hc * 16:(hc + 1) * 16],
                                        ksq[gp % 2][:, hc * 512:(hc + 1) * 512],
                                        start=(hc == 0), stop=(hc == 7))
                    ins.then_inc(s_tCK, 1)

                def emit_cq(gp):
                    if gp >= 1:
                        tn.wait_ge(s_vCQ, gp)
                    for hc in range(8):
                        tn.wait_ge(s_vQS, 8 * gp + hc + 1)
                        ins = tn.matmul(cpsq[0:16, :],
                                        esel[:, hc * 16:(hc + 1) * 16],
                                        qsq[gp % 2][:, hc * 512:(hc + 1) * 512],
                                        start=(hc == 0), stop=(hc == 7))
                    ins.then_inc(s_tCQ, 1)

                if g >= 1:
                    emit_ck(g - 1)
                if 1 <= g <= NGQ:
                    emit_cq(g - 1)
                if g == NGK - 1:
                    tn.wait_ge(s_vCK, g)
                    emit_ck(g)

    if phases == 1:
        with nc.Block() as bf:
            @bf.gpsimd
            def _(g):
                for t in range(NTQ):
                    g.dma_start(out=outh.ap()[t * 128:(t + 1) * 128, :],
                                in_=x0t[0][:, :]).then_inc(s_out, 16)
                g.wait_ge(s_out, 16 * NTQ)
                if debug:
                    nd = 0
                    g.dma_start(out=dbgh["d_kT"].ap()[:, 0:512],
                                in_=ln1t[(NTK - 1) % 2][:, 0:512]).then_inc(s_out, 16)
                    nd += 1
                    g.dma_start(out=dbgh["d_v"].ap()[:, 0:512],
                                in_=x0t[(NTK - 1) % 3][:, 0:512]).then_inc(s_out, 16)
                    nd += 1
                    g.dma_start(out=dbgh["d_qT"].ap()[:, 0:512],
                                in_=qT[0:128, 0:512]).then_inc(s_out, 16)
                    nd += 1
                    g.dma_start(out=dbgh["d_attc"].ap()[:, 0:512],
                                in_=ln1Tg[(NGK - 1) % 2][:, 0:512]).then_inc(s_out, 16)
                    nd += 1
                    g.dma_start(out=dbgh["d_crep"].ap()[0:16, 0:16],
                                in_=chist[0:16, 0:16]).then_inc(s_out, 16)
                    nd += 1
                    g.dma_start(out=dbgh["d_attc"].ap()[0:16, 0:512],
                                in_=cpsdump[0:16, 0:512]).then_inc(s_out, 16)
                    nd += 1
                    g.dma_start(out=dbgh["d_kT"].ap()[0:16, 0:20],
                                in_=chist[0:16, 20:40]).then_inc(s_out, 16)
                    nd += 1
                    g.dma_start(out=dbgh["d_crep"].ap()[0:128, 1:2],
                                in_=st_rv[0:128, 0:1],
                                allow_slow_non_contiguous=True).then_inc(s_out, 16)
                    nd += 1
                    g.dma_start(out=dbgh["d_crep"].ap()[0:128, 2:3],
                                in_=st_rs[0:128, 0:1],
                                allow_slow_non_contiguous=True).then_inc(s_out, 16)
                    nd += 1
                    g.dma_start(out=dbgh["d_crep"].ap()[0:128, 3:4],
                                in_=st_s2[0:128, 0:1],
                                allow_slow_non_contiguous=True).then_inc(s_out, 16)
                    nd += 1
                    g.dma_start(out=dbgh["d_kmax"].ap()[0:16, 0:1],
                                in_=kmax2[0:16, 0:1],
                                allow_slow_non_contiguous=True).then_inc(s_out, 16)
                    nd += 1
                    g.dma_start(out=dbgh["d_kmax"].ap()[0:16, 1:2],
                                in_=qmax2[0:16, 0:1],
                                allow_slow_non_contiguous=True).then_inc(s_out, 16)
                    nd += 1
                    g.wait_ge(s_out, 16 * (NTQ + nd))
        p1.close()
        ctx.close()
        return nc

    p1.close()

    attc = sb("attc", [128, 8 * NQ], BF16)
    Wp_sb = sb("Wp_sb", [128, 8 * D], BF16)
    Win_sb = sb("Win_sb", [128, 8 * M], BF16)
    Wout_sb = sb("Wout_sb", [64, D], BF16)
    bp_rep = sb("bp_rep", [128, D], F32)
    bout_rep = sb("bout_rep", [128, D], F32)
    w1_rep = sb("w1_rep", [128, D], F32)
    b1_rep = sb("b1_rep", [128, D], F32)
    w2_rep = sb("w2_rep", [128, D], F32)
    bin_rep = sb("bin_rep", [128, M], F32)
    b2_rep = sb("b2_rep", [128, M], F32)

    p2 = ExitStack()

    def sb2(name, shape, dt):
        return p2.enter_context(nc.sbuf_tensor(name, shape, dt))

    kdup = [sb2(f"kdup{i}", [128, NK], BF16) for i in range(2)]
    VTW = NTK * 65
    vt = [sb2(f"vt{i}", [128, VTW], BF16) for i in range(2)]
    qdup = [sb2(f"qdup{i}", [128, NQ], BF16) for i in range(2)]
    ex = [sb2(f"ex{i}", [128, 1024], BF16) for i in range(3)]
    recr = [sb2(f"recr{i}", [128, 512], F32) for i in range(2)]
    rtmp = sb2("rtmp", [1, 512], F32)
    atst = [sb2(f"atst{i}", [64, 512], BF16) for i in range(2)]
    cneg = sb2("cneg", [16, 1], F32)

    pL = p2.enter_context(nc.psum_tensor("pL", [128, 3072], F32))
    pA = [p2.enter_context(nc.psum_tensor(f"pA{i}", [128, 512], F32))
          for i in range(2)]

    DOUBLES = [(64, 1, 0, True), (0, 1, 1, False), (0, 2, 2, False),
               (0, 4, 4, False), (0, 8, 8, False), (0, 16, 16, False)]
    NRC = len(DOUBLES)

    def emit_rec_chain(eng, hq, semA, semB, idx):
        eng.wait_ge(s_vrec, hq + 1)
        r = recr[hq % 2]
        for (sA, base) in ((semA, 0), (semB, 32)):
            cnt = 16 * NRC * idx
            for (src0, n, dst0, abs_src) in DOUBLES:
                s0 = src0 if abs_src else base + src0
                if cnt:
                    eng.wait_ge(sA, cnt)
                eng.dma_start(out=r[base + dst0:base + dst0 + n, :],
                              in_=r[s0:s0 + n, :]).then_inc(sA, 16)
                cnt += 16

    def av_inc_of(G):
        hq_, g_ = G // NG, G % NG
        if g_ == NG - 1:
            return ("q", hq_ + 1)
        return ("g", (NG - 1) * hq_ + g_ + 1)

    NBG_PLAIN = 16 + 8 + 1
    rep_specs = [(bph, D), (bouth, D), (W1h, D), (b1h, D), (W2h, D),
                 (binh, M), (b2h, M)]
    rep_dst = [None] * 7
    rep_dst[0] = bp_rep; rep_dst[1] = bout_rep; rep_dst[2] = w1_rep
    rep_dst[3] = b1_rep; rep_dst[4] = w2_rep; rep_dst[5] = bin_rep
    rep_dst[6] = b2_rep
    NW2 = NBG_PLAIN + 7 * 8

    with nc.Block() as b2:
        @b2.vector
        def _(v):
            v.memset(crep[:, :], 0.0)
            v.reduce_max(kmax2[:, :], kgmax[:, :], axis=AX)
            v.reduce_max(qmax2[:, :], qgmax[:, :], axis=AX)
            v.memset(crep[:, :], 0.0)
            v.tensor_mul(cneg[:, :], kmax2[:, :], qmax2[:, :]).then_inc(s_c, 1)
            v.wait_ge(s_cs, 1)
            v.tensor_scalar(cneg[:, :], cneg[:, :], -1.0, -1.0,
                            ALU.mult, ALU.add).then_inc(s_c, 1)
            for h in range(P2H):
                for qb in range(NQB):
                    hq = NQB * h + qb
                    v.wait_ge(s_tAVq, hq + 1)
                    v.tensor_copy(recr[hq % 2][64:65, :],
                                  pA[hq % 2][64:65, :]).then_inc(s_vrec0, 1)
                    if hq % 2 == 0:
                        v.wait_ge(s_rrA, 16 * NRC * (hq // 2 + 1))
                        v.wait_ge(s_rrB, 16 * NRC * (hq // 2 + 1))
                    else:
                        v.wait_ge(s_rgA, 16 * NRC * (hq // 2 + 1))
                        v.wait_ge(s_rgB, 16 * NRC * (hq // 2 + 1))
                    if hq >= 2:
                        v.wait_ge(s_at2, 16 * (hq - 1))
                    v.tensor_tensor(
                        atst[hq % 2][:, :],
                        pA[hq % 2][0:64, :], recr[hq % 2][0:64, :],
                        ALU.mult).then_inc(s_vAT, 1)

        @b2.scalar
        def _(sc):
            def emit_recr(hq_):
                sc.wait_ge(s_vrec0, hq_ + 1)
                sc.activation(out=rtmp[0:1, :],
                              in_=recr[hq_ % 2][64:65, :], func=AF.Ln)
                sc.activation(out=recr[hq_ % 2][64:65, :],
                              in_=rtmp[0:1, :], func=AF.Exp,
                              scale=-1.0).then_inc(s_vrec, 1)

            sc.wait_ge(s_c, 1)
            sc.activation(out=cneg[:, :], in_=cneg[:, :], func=AF.Ln)
            sc.activation(out=cneg[:, :], in_=cneg[:, :],
                          func=AF.Exp, scale=0.5).then_inc(s_cs, 1)
            sc.wait_ge(s_crep, 16 * 8)
            for h in range(P2H):
                for qb in range(NQB):
                    hq = NQB * h + qb
                    for g in range(NG):
                        G = NG * hq + g
                        sc.wait_ge(s_tLp, NG * hq + g + 1)
                        if G >= 3:
                            sm, val = av_inc_of(G - 3)
                            sc.wait_ge(s_tAVg if sm == "g" else s_tAVq, val)
                        sc.activation(
                            out=ex[G % 3][:, :],
                            in_=pL[:, (g % 3) * 1024:(g % 3) * 1024 + 1024],
                            func=AF.Exp,
                            bias=crep[:, h:h + 1]).then_inc(s_exp, 1)
                        if g == 2 and hq >= 1:
                            emit_recr(hq - 1)
            emit_recr(P2H * NQB - 1)

        @b2.tensor
        def _(tn):
            for h in range(P2H):
                for qb in range(NQB):
                    hq = NQB * h + qb
                    if qb == 0:
                        tn.wait_ge(s_k, 32 * (h + 1))
                        tn.wait_ge(s_q2, 32 * (h + 1))
                        tn.wait_ge(s_v2, 16 * NVD * (h + 1))

                    def emit_av_group(g, hq=hq, h=h):
                        G = NG * hq + g
                        last_grp = (g == NG - 1)
                        tn.wait_ge(s_exp, G + 1)
                        if g == 0 and hq >= 2:
                            tn.wait_ge(s_vAT, hq - 1)
                        for i, t in enumerate((2 * g, 2 * g + 1)):
                            ins = tn.matmul(pA[hq % 2][0:65, :],
                                            vt[h % 2][:, t * 65:t * 65 + 65],
                                            ex[G % 3][:, i * 512:(i + 1) * 512],
                                            start=(t == 0), stop=(t == NTK - 1))
                        if last_grp:
                            ins.then_inc(s_tAVq, 1)
                        else:
                            ins.then_inc(s_tAVg, 1)

                    for j in range(NG):
                        Gj = NG * hq + j
                        if hq == 0 and j < 3:
                            pass
                        elif j in (0, 1):
                            tn.wait_ge(s_exp, Gj - 1)
                        elif j == 2:
                            tn.wait_ge(s_exp, Gj - 4)
                        else:
                            tn.wait_ge(s_exp, Gj - 2)
                        ta, tb = 2 * j, 2 * j + 1
                        tn.matmul(pL[:, (ta % 6) * 512:(ta % 6) * 512 + 512],
                                  kdup[h % 2][0:64, ta * 128:(ta + 1) * 128],
                                  qdup[h % 2][0:64, qb * 512:(qb + 1) * 512],
                                  start=True, stop=True, tile_position=(0, 0))
                        ins = tn.matmul(
                            pL[:, (tb % 6) * 512:(tb % 6) * 512 + 512],
                            kdup[h % 2][64:128, tb * 128:(tb + 1) * 128],
                            qdup[h % 2][64:128, qb * 512:(qb + 1) * 512],
                            start=True, stop=True, tile_position=(64, 0))
                        ins.then_inc(s_tLp, 1)
                        if j >= 2:
                            emit_av_group(j - 2)
                    emit_av_group(NG - 2)
                    emit_av_group(NG - 1)

        @b2.sync
        def _(s):
            def at2_dma(hq):
                h_, qb_ = hq // NQB, hq % NQB
                s.wait_ge(s_vAT, hq + 1)
                if p2_stub & 8:
                    s.sem_inc(s_at2, 16)
                    return
                s.dma_start(
                    out=attc[(h_ % 2) * 64:(h_ % 2) * 64 + 64,
                             (h_ // 2) * NQ + qb_ * 512:(h_ // 2) * NQ + qb_ * 512 + 512],
                    in_=atst[hq % 2][:, :]).then_inc(s_at2, 16)

            def loadkq(h):
                if p2_stub & 128:
                    s.sem_inc(s_k, 32)
                    s.sem_inc(s_q2, 32)
                    return
                s.dma_start(out=kdup[h % 2][0:64, :],
                            in_=kT[h * 64:h * 64 + 64, :]).then_inc(s_k, 16)
                s.dma_start(out=kdup[h % 2][64:128, :],
                            in_=kT[h * 64:h * 64 + 64, :]).then_inc(s_k, 16)
                s.dma_start(out=qdup[h % 2][0:64, :],
                            in_=qT[(h % 2) * 64:(h % 2) * 64 + 64,
                                   (h // 2) * NQ:(h // 2 + 1) * NQ]).then_inc(s_q2, 16)
                s.dma_start(out=qdup[h % 2][64:128, :],
                            in_=qT[(h % 2) * 64:(h % 2) * 64 + 64,
                                   (h // 2) * NQ:(h // 2 + 1) * NQ]).then_inc(s_q2, 16)

            s.wait_ge(s_spK, 16 * 8 * NGK)
            s.wait_ge(s_spV, 16 * 4 * NGK)
            loadkq(0)
            for h in range(P2H):
                if h + 1 < P2H:
                    if h >= 1:
                        s.wait_ge(s_tLp, NPAIR * NQB * h)
                    loadkq(h + 1)
                for qb in range(NQB):
                    hq = NQB * h + qb
                    if not (p2_stub & 4):
                        if hq % 2 == 0:
                            emit_rec_chain(s, hq, s_rrA, s_rrB, hq // 2)
                        else:
                            emit_rec_chain(s, hq, s_rgA, s_rgB, hq // 2)
                    at2_dma(hq)

        @b2.gpsimd
        def _(g):
            for i in range(2):
                g.memset(bass.AP(vt[i], 64, [[VTW, 128], [65, NTK], [1, 1]]), 1.0)
            g.wait_ge(s_c, 2)
            if p2_stub & 1024:
                g.sem_inc(s_crep, 16 * 8)
            else:
                g.dma_start(out=crep[0:1, 0:16],
                            in_=cneg[0:16, 0:1]).then_inc(s_crep, 16)
                cnt = 16
                for (src0, n, dst0) in [(0, 1, 1), (0, 2, 2), (0, 4, 4), (0, 8, 8),
                                        (0, 16, 16), (0, 32, 32), (0, 64, 64)]:
                    g.wait_ge(s_crep, cnt)
                    g.dma_start(out=crep[dst0:dst0 + n, :],
                                in_=crep[src0:src0 + n, :]).then_inc(s_crep, 16)
                    cnt += 16
            w2 = [0]

            def w2inc(ins):
                ins.then_inc(s_w2, 16)
                w2[0] += 16
                return w2[0]

            bg = []
            for hc in range(8):
                for hl in range(2):
                    h_ = 2 * hc + hl
                    bg.append(lambda g, hc=hc, hl=hl, h_=h_: w2inc(g.dma_start(
                        out=Wp_sb[hl * 64:(hl + 1) * 64, hc * D:(hc + 1) * D],
                        in_=bass.AP(Wph, h_ * D, [[16 * D, 64], [1, D]]))))
            for dc in range(8):
                bg.append(lambda g, dc=dc: w2inc(g.dma_start(
                    out=Win_sb[:, dc * M:(dc + 1) * M],
                    in_=bass.AP(Winh, dc * 128 * M, [[M, 128], [1, M]]))))
            bg.append(lambda g: w2inc(g.dma_start(
                out=Wout_sb[0:64, :], in_=bass.AP(Wouth, 0, [[D, 64], [1, D]]))))
            rep_state = []
            for i, (srch, width) in enumerate(rep_specs):
                rep_state.append({"dst": rep_dst[i], "srch": srch, "w": width,
                                  "k": 0, "cnt": 0})

            def emit_bg_one(g):
                if bg:
                    bg.pop(0)(g)
                    return True
                st = next((x for x in rep_state if x["k"] < 8), None)
                if st is None:
                    return False
                if st["k"] == 0:
                    st["cnt"] = w2inc(g.dma_start(
                        out=st["dst"][0:1, :],
                        in_=bass.AP(st["srch"], 0, [[st["w"], 1], [1, st["w"]]])))
                else:
                    rows = 1 << (st["k"] - 1)
                    g.wait_ge(s_w2, st["cnt"])
                    st["cnt"] = w2inc(g.dma_start(
                        out=st["dst"][rows:2 * rows, :],
                        in_=st["dst"][0:rows, :]))
                st["k"] += 1
                return True

            for h in range(P2H):
                if h >= 2:
                    g.wait_ge(s_tAVq, NQB * (h - 1))
                if p2_stub & 256:
                    g.sem_inc(s_v2, 16 * NVD)
                else:
                    for mo in range(NVD):
                        g.dma_start(
                            out=bass.AP(vt[h % 2], mo * 8 * 65,
                                        [[VTW, 128], [65, 8], [1, E]]),
                            in_=bass.AP(vhh, (mo * 8 * 128) * D + h * E,
                                        [[D, 128], [128 * D, 8], [1, E]]),
                        ).then_inc(s_v2, 16)
                if not (p2_stub & 512):
                    for _ in range(6):
                        emit_bg_one(g)

            if not (p2_stub & 512):
                while emit_bg_one(g):
                    pass
            else:
                g.sem_inc(s_w2, 16 * NW2)

    p2.close()

    if phases == 2:
        with nc.Block() as bf:
            @bf.gpsimd
            def _(g):
                for t in range(NTQ):
                    g.dma_start(out=outh.ap()[t * 128:(t + 1) * 128, :],
                                in_=bp_rep[:, :]).then_inc(s_out, 16)
                g.wait_ge(s_out, 16 * NTQ)
        ctx.close()
        return nc

    p3 = ExitStack()

    def sb3(name, shape, dt):
        return p3.enter_context(nc.sbuf_tensor(name, shape, dt))

    x3t = [sb3(f"x3t{i}", [128, D], F32) for i in range(2)]
    zt = sb3("zt", [128, D], F32)
    zt2 = sb3("zt2", [128, D], F32)
    x1t = [sb3(f"x1t{i}", [128, D], F32) for i in range(3)]
    ln2t = [sb3(f"ln2t{i}", [128, D], BF16) for i in range(2)]
    ln2T = [sb3(f"ln2T{i}", [128, D], BF16) for i in range(2)]
    s_sb = [sb3(f"s_sb{i}", [128, M], F32) for i in range(2)]
    hbuf = sb3("hbuf", [128, D], F32)
    g0b = sb3("g0b", [128, M], F32)
    gsb = [sb3(f"gsb{i}", [128, M], BF16) for i in range(2)]
    gT = [sb3(f"gT{i}", [64, 128], BF16) for i in range(2)]
    outt = [sb3(f"outt{i}", [128, D], F32) for i in range(2)]
    sq3 = sb3("sq3", [128, D], F32)
    sq3b = sb3("sq3b", [128, D], F32)
    st3 = [sb3(f"st3_{i}", [128, 1], F32) for i in range(6)]
    st3g = sb3("st3g", [128, 1], F32)

    pO = p3.enter_context(nc.psum_tensor("pO", [128, 1024], F32))
    pF = p3.enter_context(nc.psum_tensor("pF", [128, 1024], F32))
    pS = p3.enter_context(nc.psum_tensor("pS", [128, 512], F32))
    ptp3 = [p3.enter_context(nc.psum_tensor(f"ptp3{i}", [128, 128], BF16))
            for i in range(2)]
    ptg = p3.enter_context(nc.psum_tensor("ptg", [128, 128], BF16))

    NT3 = NTQ

    V = {}
    cv = 0
    for t in range(NT3):
        if t >= 1:
            cv += 1; V[("sS", t - 1)] = cv
        if t >= 2:
            cv += 1; V[("gsb", t - 2)] = cv
        cv += 1; V[("a", t)] = cv
        cv += 1; V[("b", t)] = cv
        cv += 1; V[("ln2", t)] = cv
        if t >= 2:
            cv += 1; V[("cF", t - 2)] = cv
            cv += 1; V[("out", t - 2)] = cv
    cv += 1; V[("sS", NT3 - 1)] = cv
    cv += 1; V[("gsb", NT3 - 2)] = cv
    cv += 1; V[("cF", NT3 - 2)] = cv
    cv += 1; V[("out", NT3 - 2)] = cv
    cv += 1; V[("gsb", NT3 - 1)] = cv
    cv += 1; V[("cF", NT3 - 1)] = cv
    cv += 1; V[("out", NT3 - 1)] = cv

    Tt = {}
    ct = 0
    for t in range(NT3):
        ct += 1; Tt[("proj", t)] = ct
        if t >= 2:
            ct += 1; Tt[("ptg", t - 2)] = ct
            ct += 1; Tt[("wout", t - 2)] = ct
        for i in range(8):
            ct += 1; Tt[("tp", t, i)] = ct
        ct += 1; Tt[("win", t)] = ct
    ct += 1; Tt[("ptg", NT3 - 2)] = ct
    ct += 1; Tt[("wout", NT3 - 2)] = ct
    ct += 1; Tt[("ptg", NT3 - 1)] = ct
    ct += 1; Tt[("wout", NT3 - 1)] = ct

    Sc = {}
    Cc = {}
    cs3 = 0
    cc3 = 0
    for t in range(NT3):
        if t >= 2:
            cc3 += 1; Cc[("gt", t - 2)] = cc3
        cs3 += 1; Sc[("a", t)] = cs3
        cs3 += 1; Sc[("b", t)] = cs3
        for i in range(8):
            cc3 += 1; Cc[("tp", t, i)] = cc3
        if t >= 2:
            cs3 += 1; Sc[("c", t - 2)] = cs3
    cc3 += 1; Cc[("gt", NT3 - 2)] = cc3
    cs3 += 1; Sc[("c", NT3 - 2)] = cs3
    cc3 += 1; Cc[("gt", NT3 - 1)] = cc3
    cs3 += 1; Sc[("c", NT3 - 1)] = cs3

    Gg = {}
    cg3 = 0
    for t in range(NT3):
        cg3 += 1; Gg[("hb", t)] = cg3

    with nc.Block() as b3:
        @b3.sync
        def _(s):
            for t in range(min(2, NT3)):
                s.dma_start(out=x3t[t % 2][:, :],
                            in_=xb[t * 128:(t + 1) * 128, :]).then_inc(s3_x, 16)
            for t in range(NT3):
                if t + 2 < NT3:
                    s.wait_ge(s3_v, V[("b", t)])
                    s.dma_start(out=x3t[t % 2][:, :],
                                in_=xb[(t + 2) * 128:(t + 3) * 128, :]
                                ).then_inc(s3_x, 16)
                s.wait_ge(s3_v, V[("out", t)])
                s.dma_start(out=outh.ap()[t * 128:(t + 1) * 128, :],
                            in_=outt[t % 2][:, :]).then_inc(s_out, 16)
            s.wait_ge(s_out, 16 * NT3)

        @b3.tensor
        def _(tn):
            tn.wait_ge(s_w2, 16 * NW2)
            tn.wait_ge(s_at2, 16 * H * NQB)

            def emit_pw(tt):
                tn.wait_ge(s3_v, V[("gsb", tt)])
                if tt >= 1:
                    tn.wait_ge(s3_c, Cc[("gt", tt - 1)])
                tn.transpose(ptg[0:64, 0:128], gsb[tt % 2][:, :],
                             identb[:, :]).then_inc(s3_t, 1)
                tn.wait_ge(s3_c, Cc[("gt", tt)])
                if tt >= 1:
                    tn.wait_ge(s3_v, V[("cF", tt - 1)])
                tn.matmul(pF[:, 0:512], gT[tt % 2][:, :],
                          Wout_sb[0:64, 0:512], start=True, stop=True)
                tn.matmul(pF[:, 512:1024], gT[tt % 2][:, :],
                          Wout_sb[0:64, 512:1024],
                          start=True, stop=True).then_inc(s3_t, 1)

            for t in range(NT3):
                qb, tq = t // 4, t % 4
                if t >= 1:
                    tn.wait_ge(s3_v, V[("a", t - 1)])
                for hc in range(8):
                    lhs = attc[:, hc * NQ + qb * 512 + tq * 128:
                               hc * NQ + qb * 512 + tq * 128 + 128]
                    tn.matmul(pO[:, 0:512], lhs,
                              Wp_sb[:, hc * D:hc * D + 512],
                              start=(hc == 0), stop=(hc == 7))
                    ins = tn.matmul(pO[:, 512:1024], lhs,
                                    Wp_sb[:, hc * D + 512:hc * D + 1024],
                                    start=(hc == 0), stop=(hc == 7))
                ins.then_inc(s3_t, 1)
                if t >= 2:
                    emit_pw(t - 2)
                tn.wait_ge(s3_v, V[("ln2", t)])
                for i in range(8):
                    gi = 8 * t + i
                    if gi >= 2:
                        tt_, ii_ = (t, i - 2) if i >= 2 else (t - 1, i + 6)
                        tn.wait_ge(s3_c, Cc[("tp", tt_, ii_)])
                    tn.transpose(ptp3[i % 2][:, 0:128],
                                 ln2t[t % 2][:, i * 128:(i + 1) * 128],
                                 identb[:, :]).then_inc(s3_t, 1)
                tn.wait_ge(s3_c, Cc[("tp", t, 7)])
                if t >= 1:
                    tn.wait_ge(s3_v, V[("sS", t - 1)])
                for dc in range(8):
                    ins = tn.matmul(pS[:, 0:64],
                                    ln2T[t % 2][:, dc * 128:(dc + 1) * 128],
                                    Win_sb[:, dc * M:(dc + 1) * M],
                                    start=(dc == 0), stop=(dc == 7))
                ins.then_inc(s3_t, 1)
            emit_pw(NT3 - 2)
            emit_pw(NT3 - 1)

        @b3.vector
        def _(v):
            def ln_stats(src):
                v.reduce_sum(st3[4][:, :], src[:, :], axis=AX)
                v.tensor_mul(sq3[:, :], src[:, :], src[:, :])
                v.reduce_sum(st3[5][:, :], sq3[:, :], axis=AX)
                v.tensor_scalar_mul(st3[1][:, :], st3[4][:, :], 1.0 / D)
                v.tensor_scalar_mul(st3[5][:, :], st3[5][:, :], 1.0 / D)
                v.tensor_mul(st3[4][:, :], st3[1][:, :], st3[1][:, :])
                v.tensor_sub(st3[5][:, :], st3[5][:, :], st3[4][:, :])
                return v.tensor_scalar_add(st3[5][:, :], st3[5][:, :], EPS)

            def emit_sS(tt):
                v.wait_ge(s3_t, Tt[("win", tt)])
                if tt >= 2:
                    v.wait_ge(s3_g, Gg[("hb", tt - 2)])
                v.tensor_add(s_sb[tt % 2][:, :], pS[:, 0:64],
                             bin_rep[:, :]).then_inc(s3_v, 1)

            def emit_R(tt):
                v.wait_ge(s3_g, Gg[("hb", tt)])
                if tt >= 2:
                    v.wait_ge(s3_t, Tt[("ptg", tt - 2)])
                v.scalar_tensor_tensor(hbuf[:, :], hbuf[:, :], 0.0,
                                       w2_rep[:, :], ALU.max, ALU.mult)
                v.reduce_sum(g0b[:, :],
                             bass.AP(hbuf, 0,
                                     [[D, 128], [INNER, M], [1, INNER]]),
                             axis=AX)
                v.tensor_copy(st3g[:, :], b2_rep[:, 0:1])
                v.tensor_add(gsb[tt % 2][:, :], g0b[:, :],
                             b2_rep[:, :]).then_inc(s3_v, 1)

            def emit_A(tt):
                v.wait_ge(s3_t, Tt[("proj", tt)])
                v.tensor_add(zt[:, :], pO[:, :], bp_rep[:, :])
                ln_stats(zt).then_inc(s3_v, 1)
                v.wait_ge(s3_s, Sc[("a", tt)])
                v.wait_ge(s3_x, 16 * (tt + 1))
                v.tensor_scalar(sq3[:, :], zt[:, :], st3[1][:, :], st3[2][:, :],
                                ALU.subtract, ALU.mult)
                v.tensor_add(x1t[tt % 3][:, :], x3t[tt % 2][:, :], sq3[:, :])
                ln_stats(x1t[tt % 3]).then_inc(s3_v, 1)
                v.wait_ge(s3_s, Sc[("b", tt)])
                if tt >= 2:
                    v.wait_ge(s3_t, Tt[("tp", tt - 2, 7)])
                v.tensor_scalar(ln2t[tt % 2][:, :], x1t[tt % 3][:, :],
                                st3[1][:, :], st3[2][:, :],
                                ALU.subtract, ALU.mult).then_inc(s3_v, 1)

            def emit_F(tt):
                v.wait_ge(s3_t, Tt[("wout", tt)])
                v.tensor_add(zt2[:, :], pF[:, :], bout_rep[:, :])
                ln_stats(zt2).then_inc(s3_v, 1)
                v.wait_ge(s3_s, Sc[("c", tt)])
                if tt >= 2:
                    v.wait_ge(s_out, 16 * (tt - 1))
                v.tensor_scalar(sq3b[:, :], zt2[:, :], st3[1][:, :],
                                st3[2][:, :], ALU.subtract, ALU.mult)
                v.tensor_add(outt[tt % 2][:, :], x1t[tt % 3][:, :],
                             sq3b[:, :]).then_inc(s3_v, 1)

            v.wait_ge(s_w2, 16 * NW2)
            emit_A(0)
            for t in range(1, NT3):
                emit_sS(t - 1)
                if t >= 2:
                    emit_R(t - 2)
                emit_A(t)
                if t >= 2:
                    emit_F(t - 2)
            emit_sS(NT3 - 1)
            emit_R(NT3 - 2)
            emit_F(NT3 - 2)
            emit_R(NT3 - 1)
            emit_F(NT3 - 1)

        @b3.scalar
        def _(sc):
            def rsqrt(v_need):
                sc.wait_ge(s3_v, v_need)
                sc.activation(out=st3[0][:, :], in_=st3[5][:, :], func=AF.Ln)
                sc.activation(out=st3[2][:, :], in_=st3[0][:, :],
                              func=AF.Exp, scale=-0.5).then_inc(s3_s, 1)

            def emit_gt(tt):
                sc.wait_ge(s3_t, Tt[("ptg", tt)])
                if tt >= 2:
                    sc.wait_ge(s3_t, Tt[("wout", tt - 2)])
                sc.activation(out=gT[tt % 2][:, :], in_=ptg[0:64, 0:128],
                              func=AF.Copy).then_inc(s3_c, 1)

            for t in range(NT3):
                if t >= 2:
                    emit_gt(t - 2)
                rsqrt(V[("a", t)])
                rsqrt(V[("b", t)])
                for i in range(8):
                    sc.wait_ge(s3_t, Tt[("tp", t, i)])
                    if t >= 2 and i == 0:
                        sc.wait_ge(s3_t, Tt[("win", t - 2)])
                    sc.activation(out=ln2T[t % 2][:, i * 128:(i + 1) * 128],
                                  in_=ptp3[i % 2][:, 0:128],
                                  func=AF.Copy).then_inc(s3_c, 1)
                if t >= 2:
                    rsqrt(V[("cF", t - 2)])
            emit_gt(NT3 - 2)
            rsqrt(V[("cF", NT3 - 2)])
            emit_gt(NT3 - 1)
            rsqrt(V[("cF", NT3 - 1)])

        @b3.gpsimd
        def _(g):
            g.wait_ge(s_w2, 16 * NW2)
            for t in range(NT3):
                g.wait_ge(s3_v, V[("sS", t)])
                if t >= 1:
                    g.wait_ge(s3_v, V[("gsb", t - 1)])
                g.tensor_tensor(hbuf[:, :],
                                bass.AP(s_sb[t % 2], 0,
                                        [[M, 128], [1, M], [0, INNER]]),
                                w1_rep[:, :], ALU.mult)
                g.tensor_add(hbuf[:, :], hbuf[:, :],
                             b1_rep[:, :]).then_inc(s3_g, 1)
            g.wait_ge(s_out, 16 * NT3)
            if debug:
                nd = 0
                for nm, src in (("d_qT", qT[:, 0:512]),
                                ("d_crep", crep[:, :]),
                                ("d_attc", attc[:, 0:512])):
                    g.dma_start(out=x3t[0][:, 0:src.shape[-1]], in_=src).then_inc(s_out, 16)
                    nd += 1
                    g.wait_ge(s_out, 16 * (NT3 + nd))
                    g.dma_start(out=dbgh[nm].ap()[0:src.shape[0], 0:src.shape[-1]],
                                in_=x3t[0][0:src.shape[0], 0:src.shape[-1]]).then_inc(s_out, 16)
                    nd += 1
                    g.wait_ge(s_out, 16 * (NT3 + nd))
                g.dma_start(out=dbgh["d_kT"].ap()[:, 0:512],
                            in_=kT[0:128, 0:512]).then_inc(s_out, 16)
                nd += 1
                g.wait_ge(s_out, 16 * (NT3 + nd))
                g.dma_start(out=dbgh["d_v"].ap()[:, 0:512],
                            in_=vh[0:128, 0:512]).then_inc(s_out, 16)
                nd += 1
                g.wait_ge(s_out, 16 * (NT3 + nd))
                g.dma_start(out=dbgh["d_kmax"].ap()[0:16, 0:1],
                            in_=kmax2[0:16, 0:1],
                            allow_slow_non_contiguous=True).then_inc(s_out, 16)
                nd += 1
                g.dma_start(out=dbgh["d_kmax"].ap()[0:16, 1:2],
                            in_=qmax2[0:16, 0:1],
                            allow_slow_non_contiguous=True).then_inc(s_out, 16)
                nd += 1
                g.dma_start(out=dbgh["d_cq"].ap()[:, :],
                            in_=dbg_cq[:, :]).then_inc(s_out, 16)
                nd += 1
                g.dma_start(out=dbgh["d_ck"].ap()[:, :],
                            in_=dbg_ck[:, :]).then_inc(s_out, 16)
                nd += 1
                g.dma_start(out=dbgh["d_q"].ap()[:, :],
                            in_=dbg_q[:, :]).then_inc(s_out, 16)
                nd += 1
                g.wait_ge(s_out, 16 * (NT3 + nd))

    p3.close()
    ctx.close()
    return nc


def kernel(L0_attn_query, L0_attn_Wq, L0_attn_Wk, L0_attn_Wv, L0_attn_Wp,
           L0_attn_bp, L0_ffn_Win, L0_ffn_bin_, L0_ffn_W1, L0_ffn_b1,
           L0_ffn_W2, L0_ffn_b2, L0_ffn_Wout, L0_ffn_bout):
    x = np.ascontiguousarray(np.asarray(L0_attn_query, dtype=np.float32))
    assert x.shape == (2, 8192, 1024)
    nc = build_nc(NK=8192, NQ=2048)
    weights = {
        "Wq": np.ascontiguousarray(np.asarray(L0_attn_Wq, np.float32)),
        "Wk": np.ascontiguousarray(np.asarray(L0_attn_Wk, np.float32)),
        "Wv": np.ascontiguousarray(np.asarray(L0_attn_Wv, np.float32)),
        "Wp": np.ascontiguousarray(np.asarray(L0_attn_Wp, np.float32)),
        "bp": np.ascontiguousarray(np.asarray(L0_attn_bp, np.float32)),
        "Win": np.ascontiguousarray(np.asarray(L0_ffn_Win, np.float32)),
        "binp": np.ascontiguousarray(np.asarray(L0_ffn_bin_, np.float32)),
        "W1": np.ascontiguousarray(np.asarray(L0_ffn_W1, np.float32)),
        "b1": np.ascontiguousarray(np.asarray(L0_ffn_b1, np.float32)),
        "W2": np.ascontiguousarray(np.asarray(L0_ffn_W2, np.float32)),
        "b2": np.ascontiguousarray(np.asarray(L0_ffn_b2, np.float32)),
        "Wout": np.ascontiguousarray(np.asarray(L0_ffn_Wout, np.float32)),
        "bout": np.ascontiguousarray(np.asarray(L0_ffn_bout, np.float32)),
    }
    in_maps = []
    for c in range(8):
        b, qq = c // 4, c % 4
        xr = np.roll(x[b], -qq * 2048, axis=0)
        m = {"xb": np.ascontiguousarray(xr)}
        m.update(weights)
        in_maps.append(m)
    res = run_bass_kernel_spmd(nc, in_maps, core_ids=list(range(8)))
    out = np.empty((2, 8192, 1024), np.float32)
    for c in range(8):
        b, qq = c // 4, c % 4
        out[b, qq * 2048:(qq + 1) * 2048] = res.results[c]["out"]
    return out


# version 14
# speedup vs baseline: 1.1049x; 1.0304x over previous
import numpy as np
from contextlib import ExitStack

import concourse.bass as bass
import concourse.mybir as mybir
from concourse.bass_utils import run_bass_kernel_spmd

F32 = mybir.dt.float32
BF16 = mybir.dt.bfloat16
AX = mybir.AxisListType.X
ALU = mybir.AluOpType
AF = mybir.ActivationFunctionType

D = 1024
H = 16
E = 64
M = 64
INNER = 16
EPS = 1e-5
SCALE = 0.125


def build_nc(NK=8192, NQ=2048, debug=False, phases=3, p2_heads=None, p2_stub=0):
    NTK = NK // 128
    NGK = NK // 512
    NTQ = NQ // 128
    NGQ = max(1, NQ // 512)
    NQB = max(1, NQ // 512)
    NG = NTK // 2
    NPAIR = NTK // 2
    NVD = max(1, NTK // 8)
    P2H = H if p2_heads is None else p2_heads

    nc = bass.Bass("TRN2", target_bir_lowering=False,
                   detect_race_conditions=False)

    xbh = nc.dram_tensor("xb", [NK, D], F32, kind="ExternalInput")
    Wqh = nc.dram_tensor("Wq", [H, D, E], F32, kind="ExternalInput")
    Wkh = nc.dram_tensor("Wk", [H, D, E], F32, kind="ExternalInput")
    Wvh = nc.dram_tensor("Wv", [H, D, E], F32, kind="ExternalInput")
    Wph = nc.dram_tensor("Wp", [D, D], F32, kind="ExternalInput")
    bph = nc.dram_tensor("bp", [D], F32, kind="ExternalInput")
    Winh = nc.dram_tensor("Win", [D, M], F32, kind="ExternalInput")
    binh = nc.dram_tensor("binp", [M], F32, kind="ExternalInput")
    W1h = nc.dram_tensor("W1", [M, INNER], F32, kind="ExternalInput")
    b1h = nc.dram_tensor("b1", [M, INNER], F32, kind="ExternalInput")
    W2h = nc.dram_tensor("W2", [M, INNER], F32, kind="ExternalInput")
    b2h = nc.dram_tensor("b2", [M], F32, kind="ExternalInput")
    Wouth = nc.dram_tensor("Wout", [M, D], F32, kind="ExternalInput")
    bouth = nc.dram_tensor("bout", [D], F32, kind="ExternalInput")
    outh = nc.dram_tensor("out", [NQ, D], F32, kind="ExternalOutput")

    kTh = nc.dram_tensor("kT_scr", [D, NK], BF16, kind="Internal")
    vhh = nc.dram_tensor("v_scr", [NK, D], BF16, kind="Internal")
    dbgh = {}
    if debug:
        for nm, shp in (("d_kT", [128, 512]), ("d_v", [128, 512]),
                        ("d_qT", [128, 512]), ("d_crep", [128, 16]),
                        ("d_attc", [128, 512]), ("d_kmax", [16, 2]),
                        ("d_cq", [16, 4 * 512]), ("d_ck", [16, 512]),
                        ("d_q", [128, 512])):
            dbgh[nm] = nc.dram_tensor(nm, shp, F32, kind="ExternalOutput")

    xb = xbh.ap()
    kT = kTh.ap()
    vh = vhh.ap()

    ctx = ExitStack()

    def sb(name, shape, dt):
        return ctx.enter_context(nc.sbuf_tensor(name, shape, dt))

    def sem(name):
        return ctx.enter_context(nc.semaphore(name))

    identb = sb("identb", [128, 128], BF16)
    esel = sb("esel", [128, 8 * 16], BF16)
    if debug:
        dbg_cq = sb("dbg_cq", [16, 4 * 512], F32)
        dbg_ck = sb("dbg_ck", [16, 512], F32)
        dbg_q = sb("dbg_q", [128, 512], F32)
    qT = sb("qT", [128, 8 * NQ], BF16)
    kmax2 = sb("kmax2", [16, 1], F32)
    qmax2 = sb("qmax2", [16, 1], F32)
    kgmax = sb("kgmax", [16, NGK], F32)
    qgmax = sb("qgmax", [16, max(1, NGQ)], F32)
    crep = sb("crep", [128, 16], F32)

    s_w = sem("s_w")
    s_x = sem("s_x"); s_vS = sem("s_vS"); s_sS = sem("s_sS")
    s_vLN = sem("s_vLN"); s_tTP = sem("s_tTP"); s_vTP = sem("s_vTP")
    s_tK = sem("s_tK"); s_vK = sem("s_vK"); s_tV = sem("s_tV"); s_vV = sem("s_vV")
    s_tQ = sem("s_tQ"); s_vQ = sem("s_vQ")
    s_vSQ = sem("s_vSQ"); s_tCK = sem("s_tCK"); s_vCK = sem("s_vCK")
    s_vQS = sem("s_vQS"); s_tCQ = sem("s_tCQ"); s_vCQ = sem("s_vCQ")
    s_spK = sem("s_spK"); s_spV = sem("s_spV"); s_tG = sem("s_tG")
    s_c = sem("s_c"); s_cs = sem("s_cs"); s_crep = sem("s_crep")
    s_k = sem("s_k"); s_v2 = sem("s_v2"); s_q2 = sem("s_q2")
    s_tLp = sem("s_tLp"); s_tL = sem("s_tL")
    s_exp = sem("s_exp"); s_tAVg = sem("s_tAVg"); s_tAVq = sem("s_tAVq")
    s_vrec = sem("s_vrec"); s_vrec0 = sem("s_vrec0")
    s_rrA = sem("s_rrA"); s_rrB = sem("s_rrB")
    s_rgA = sem("s_rgA"); s_rgB = sem("s_rgB"); s_at2 = sem("s_at2")
    s_vAT = sem("s_vAT")
    s_w2 = sem("s_w2")
    s3_x = sem("s3_x"); s3_t = sem("s3_t"); s3_v = sem("s3_v"); s3_s = sem("s3_s")
    s3_c = sem("s3_c"); s3_g = sem("s3_g")
    s_out = sem("s_out")

    p1 = ExitStack()

    def sb1(name, shape, dt):
        return p1.enter_context(nc.sbuf_tensor(name, shape, dt))

    def ps1(name, shape, dt):
        return p1.enter_context(nc.psum_tensor(name, shape, dt))

    Wq_sb = sb1("Wq_sb", [128, 8 * D], BF16)
    Wk_sb = sb1("Wk_sb", [128, 8 * D], BF16)
    Wv_sb = sb1("Wv_sb", [128, 8 * D], BF16)
    x0t = [sb1(f"x0t{i}", [128, D], F32) for i in range(3)]
    sqv = sb1("sqv", [128, D], F32)
    ln1t = [sb1(f"ln1t{i}", [128, D], BF16) for i in range(2)]
    ln1Tg = [sb1(f"ln1Tg{i}", [128, 8 * 512], BF16) for i in range(2)]
    kTsp = [sb1(f"kTsp{i}", [128, 8 * 512], BF16) for i in range(2)]
    vsp = [sb1(f"vsp{i}", [128, 4 * D], BF16) for i in range(2)]
    ksq = [sb1(f"ksq{i}", [128, 8 * 512], BF16) for i in range(2)]
    qsq = [sb1(f"qsq{i}", [128, 8 * 512], BF16) for i in range(2)]
    st_s1 = sb1("st_s1", [128, 1], F32)
    st_s2 = sb1("st_s2", [128, 1], F32)
    st_m = sb1("st_m", [128, 1], F32)
    st_t = sb1("st_t", [128, 1], F32)
    st_rv = sb1("st_rv", [128, 1], F32)
    st_rs = sb1("st_rs", [128, 1], F32)
    cmx = sb1("cmx", [16, 1], F32)

    ptp = [ps1(f"ptp{i}", [128, 1024], BF16) for i in range(2)]
    kps = [ps1(f"kps{i}", [128, 512], F32) for i in range(2)]
    vps = ps1("vps", [128, 1024], F32)
    cps = ps1("cps", [128, 512], F32)
    cpsq = ps1("cpsq", [128, 512], F32)

    NW1 = 24

    with nc.Block() as b0:
        @b0.gpsimd
        def _(g):
            for dc in range(8):
                for (Wh, Wsb) in ((Wqh, Wq_sb), (Wkh, Wk_sb), (Wvh, Wv_sb)):
                    g.dma_start(
                        out=Wsb[:, dc * D:(dc + 1) * D],
                        in_=bass.AP(Wh, dc * 128 * E, [[E, 128], [D * E, H], [1, E]]),
                    ).then_inc(s_w, 16)
            g.memset(identb[:, :], 1.0)
            g.affine_select(out=identb[:, :], in_=identb[:, :],
                            compare_op=ALU.is_equal, fill=0.0, base=0,
                            pattern=[[-1, 128]], channel_multiplier=1)

        @b0.vector
        def _(v):
            v.memset(esel[:, :], 0.0)
            for dc in range(8):
                v.memset(esel[0:64, dc * 16 + 2 * dc:dc * 16 + 2 * dc + 1], 1.0)
                v.memset(esel[64:128, dc * 16 + 2 * dc + 1:dc * 16 + 2 * dc + 2], 1.0)
            v.memset(kmax2[:, :], 0.0)
            v.memset(qmax2[:, :], 0.0)

    with nc.Block() as b1:
        @b1.sync
        def _(s):
            def spill(gp):
                s.wait_ge(s_vK, 8 * gp + 8)
                for hc in range(8):
                    s.dma_start(
                        out=kT[hc * 128:(hc + 1) * 128, gp * 512:(gp + 1) * 512],
                        in_=kTsp[gp % 2][:, hc * 512:(hc + 1) * 512],
                    ).then_inc(s_spK, 16)
                s.wait_ge(s_vV, 4 * gp + 4)
                for mt in range(4):
                    s.dma_start(
                        out=vh[(4 * gp + mt) * 128:(4 * gp + mt + 1) * 128, :],
                        in_=vsp[gp % 2][:, mt * D:(mt + 1) * D],
                    ).then_inc(s_spV, 16)

            for g in range(NGK):
                spill(g)

        @b1.vector
        def _(v):
            for g in range(NGK):
                for tq in range(4):
                    t = 4 * g + tq
                    v.wait_ge(s_x, 16 * (t + 1))
                    x = x0t[t % 3]
                    v.reduce_sum(st_s1[:, :], x[:, :], axis=AX)
                    v.scalar_tensor_tensor(sqv[:, :], x[:, :], 0.0, x[:, :],
                                           ALU.add, ALU.mult,
                                           accum_out=st_s2[:, :])
                    v.tensor_scalar_mul(st_m[:, :], st_s1[:, :], 1.0 / D)
                    v.tensor_scalar_mul(st_t[:, :], st_s2[:, :], 1.0 / D)
                    v.tensor_mul(st_s1[:, :], st_m[:, :], st_m[:, :])
                    v.tensor_sub(st_t[:, :], st_t[:, :], st_s1[:, :])
                    v.tensor_scalar_add(st_t[:, :], st_t[:, :],
                                        EPS).then_inc(s_vS, 1)
                    v.wait_ge(s_sS, t + 1)
                    v.tensor_scalar(ln1t[t % 2][:, :], x[:, :], st_m[:, :],
                                    st_rs[:, :], ALU.subtract,
                                    ALU.mult).then_inc(s_vLN, 1)
                    if tq == 0 and g >= 2:
                        if g - 2 < NGQ:
                            v.wait_ge(s_tQ, 8 * (g - 1))
                        else:
                            v.wait_ge(s_tV, 4 * (g - 1))
                    for dc in range(8):
                        v.wait_ge(s_tTP, 8 * t + dc + 1)
                        v.tensor_copy(
                            ln1Tg[g % 2][:, dc * 512 + tq * 128:dc * 512 + tq * 128 + 128],
                            ptp[dc % 2][:, 0:128]).then_inc(s_vTP, 1)
                if g >= 2:
                    v.wait_ge(s_spK, 16 * 8 * (g - 1))
                    v.wait_ge(s_spV, 16 * 4 * (g - 1))
                for hc in range(8):
                    KI = 8 * g + hc
                    v.wait_ge(s_tK, KI + 1)
                    v.tensor_copy(kTsp[g % 2][:, hc * 512:(hc + 1) * 512],
                                  kps[KI % 2][:, :]).then_inc(s_vK, 1)
                    v.tensor_mul(ksq[g % 2][:, hc * 512:(hc + 1) * 512],
                                 kTsp[g % 2][:, hc * 512:(hc + 1) * 512],
                                 kTsp[g % 2][:, hc * 512:(hc + 1) * 512]
                                 ).then_inc(s_vSQ, 1)
                for mt in range(4):
                    VI = 4 * g + mt
                    v.wait_ge(s_tV, VI + 1)
                    v.tensor_copy(vsp[g % 2][:, mt * D:(mt + 1) * D],
                                  vps[:, :]).then_inc(s_vV, 1)
                if g >= 2:
                    v.wait_ge(s_tCK, g - 1)
                    if debug and g == 2:
                        v.tensor_copy(dbg_ck[:, :], cps[0:16, :])
                    v.reduce_max(kgmax[:, g - 2:g - 1], cps[0:16, :],
                                 axis=AX).then_inc(s_vCK, 1)
                if 2 <= g and g - 2 < NGQ:
                    v.wait_ge(s_tCQ, g - 1)
                    if debug:
                        v.tensor_copy(dbg_cq[:, (g - 2) * 512:(g - 1) * 512],
                                      cpsq[0:16, :])
                    v.reduce_max(qgmax[:, g - 2:g - 1], cpsq[0:16, :],
                                 axis=AX).then_inc(s_vCQ, 1)
            v.wait_ge(s_tCK, NGK - 1)
            v.reduce_max(kgmax[:, NGK - 2:NGK - 1], cps[0:16, :],
                         axis=AX).then_inc(s_vCK, 1)
            v.wait_ge(s_tCK, NGK)
            v.reduce_max(kgmax[:, NGK - 1:NGK], cps[0:16, :],
                         axis=AX).then_inc(s_vCK, 1)

        @b1.scalar
        def _(sc):
            def xload(t):
                sc.dma_start(out=x0t[t % 3][:, :],
                             in_=xb[t * 128:(t + 1) * 128, :]).then_inc(s_x, 16)

            for t in range(3):
                xload(t)
            for t in range(NTK):
                sc.wait_ge(s_vS, t + 1)
                sc.activation(out=st_rv[:, :], in_=st_t[:, :], func=AF.Ln)
                sc.activation(out=st_rs[:, :], in_=st_rv[:, :],
                              func=AF.Exp, scale=-0.5).then_inc(s_sS, 1)
                if t + 3 < NTK:
                    sc.wait_ge(s_vLN, t + 1)
                    xload(t + 3)
                if t % 4 == 0:
                    gq = t // 4 - 1
                    if 0 <= gq < NGQ:
                        for hc in range(8):
                            QI = 8 * gq + hc
                            sc.wait_ge(s_tQ, QI + 1)
                            sc.activation(
                                out=qT[:, hc * NQ + gq * 512:hc * NQ + gq * 512 + 512],
                                in_=kps[QI % 2][:, :], func=AF.Copy,
                                scale=SCALE).then_inc(s_vQ, 1)
                            sc.activation(
                                out=qsq[gq % 2][:, hc * 512:(hc + 1) * 512],
                                in_=kps[QI % 2][:, :], func=AF.Square,
                                scale=SCALE).then_inc(s_vQS, 1)

        @b1.tensor
        def _(tn):
            tn.wait_ge(s_w, 16 * NW1)
            for g in range(NGK):
                for tq in range(4):
                    t = 4 * g + tq
                    tn.wait_ge(s_vLN, t + 1)
                    for dc in range(8):
                        i = 8 * t + dc
                        if i >= 2:
                            tn.wait_ge(s_vTP, i - 1)
                        tn.transpose(ptp[dc % 2][:, 0:128],
                                     ln1t[t % 2][:, dc * 128:(dc + 1) * 128],
                                     identb[:, :]).then_inc(s_tTP, 1)
                tn.wait_ge(s_vTP, 8 * (4 * g + 4))
                for hc in range(8):
                    KI = 8 * g + hc
                    if KI >= 2:
                        tn.wait_ge(s_vK, KI - 1)
                    if g >= 1 and g - 1 < NGQ:
                        tn.wait_ge(s_vQS, 8 * (g - 1) + hc + 1)
                    for dc in range(8):
                        ins = tn.matmul(
                            kps[KI % 2][:, :],
                            Wk_sb[:, dc * D + hc * 128:dc * D + hc * 128 + 128],
                            ln1Tg[g % 2][:, dc * 512:(dc + 1) * 512],
                            start=(dc == 0), stop=(dc == 7))
                    ins.then_inc(s_tK, 1)
                for mt in range(4):
                    VI = 4 * g + mt
                    if VI >= 1:
                        tn.wait_ge(s_vV, VI)
                    for dc in range(8):
                        tn.matmul(vps[:, 0:512],
                                  ln1Tg[g % 2][:, dc * 512 + mt * 128:dc * 512 + mt * 128 + 128],
                                  Wv_sb[:, dc * D:dc * D + 512],
                                  start=(dc == 0), stop=(dc == 7))
                        ins = tn.matmul(
                            vps[:, 512:1024],
                            ln1Tg[g % 2][:, dc * 512 + mt * 128:dc * 512 + mt * 128 + 128],
                            Wv_sb[:, dc * D + 512:dc * D + 1024],
                            start=(dc == 0), stop=(dc == 7))
                    ins.then_inc(s_tV, 1)
                if g < NGQ:
                    tn.wait_ge(s_vK, 8 * (g + 1))
                    for hc in range(8):
                        QI = 8 * g + hc
                        if QI >= 2:
                            tn.wait_ge(s_vQS, QI - 1)
                        for dc in range(8):
                            ins = tn.matmul(
                                kps[QI % 2][:, :],
                                Wq_sb[:, dc * D + hc * 128:dc * D + hc * 128 + 128],
                                ln1Tg[g % 2][:, dc * 512:(dc + 1) * 512],
                                start=(dc == 0), stop=(dc == 7))
                        ins.then_inc(s_tQ, 1)
                def emit_ck(gp):
                    if gp >= 1:
                        tn.wait_ge(s_vCK, gp)
                    for hc in range(8):
                        tn.wait_ge(s_vSQ, 8 * gp + hc + 1)
                        ins = tn.matmul(cps[0:16, :],
                                        esel[:, hc * 16:(hc + 1) * 16],
                                        ksq[gp % 2][:, hc * 512:(hc + 1) * 512],
                                        start=(hc == 0), stop=(hc == 7))
                    ins.then_inc(s_tCK, 1)

                def emit_cq(gp):
                    if gp >= 1:
                        tn.wait_ge(s_vCQ, gp)
                    for hc in range(8):
                        tn.wait_ge(s_vQS, 8 * gp + hc + 1)
                        ins = tn.matmul(cpsq[0:16, :],
                                        esel[:, hc * 16:(hc + 1) * 16],
                                        qsq[gp % 2][:, hc * 512:(hc + 1) * 512],
                                        start=(hc == 0), stop=(hc == 7))
                    ins.then_inc(s_tCQ, 1)

                if g >= 1:
                    emit_ck(g - 1)
                if 1 <= g <= NGQ:
                    emit_cq(g - 1)
                if g == NGK - 1:
                    tn.wait_ge(s_vCK, g)
                    emit_ck(g)

    if phases == 1:
        with nc.Block() as bf:
            @bf.gpsimd
            def _(g):
                for t in range(NTQ):
                    g.dma_start(out=outh.ap()[t * 128:(t + 1) * 128, :],
                                in_=x0t[0][:, :]).then_inc(s_out, 16)
                g.wait_ge(s_out, 16 * NTQ)
                if debug:
                    nd = 0
                    g.dma_start(out=dbgh["d_kT"].ap()[:, 0:512],
                                in_=ln1t[(NTK - 1) % 2][:, 0:512]).then_inc(s_out, 16)
                    nd += 1
                    g.dma_start(out=dbgh["d_v"].ap()[:, 0:512],
                                in_=x0t[(NTK - 1) % 3][:, 0:512]).then_inc(s_out, 16)
                    nd += 1
                    g.dma_start(out=dbgh["d_qT"].ap()[:, 0:512],
                                in_=qT[0:128, 0:512]).then_inc(s_out, 16)
                    nd += 1
                    g.dma_start(out=dbgh["d_attc"].ap()[:, 0:512],
                                in_=ln1Tg[(NGK - 1) % 2][:, 0:512]).then_inc(s_out, 16)
                    nd += 1
                    g.dma_start(out=dbgh["d_crep"].ap()[0:16, 0:16],
                                in_=chist[0:16, 0:16]).then_inc(s_out, 16)
                    nd += 1
                    g.dma_start(out=dbgh["d_attc"].ap()[0:16, 0:512],
                                in_=cpsdump[0:16, 0:512]).then_inc(s_out, 16)
                    nd += 1
                    g.dma_start(out=dbgh["d_kT"].ap()[0:16, 0:20],
                                in_=chist[0:16, 20:40]).then_inc(s_out, 16)
                    nd += 1
                    g.dma_start(out=dbgh["d_crep"].ap()[0:128, 1:2],
                                in_=st_rv[0:128, 0:1],
                                allow_slow_non_contiguous=True).then_inc(s_out, 16)
                    nd += 1
                    g.dma_start(out=dbgh["d_crep"].ap()[0:128, 2:3],
                                in_=st_rs[0:128, 0:1],
                                allow_slow_non_contiguous=True).then_inc(s_out, 16)
                    nd += 1
                    g.dma_start(out=dbgh["d_crep"].ap()[0:128, 3:4],
                                in_=st_s2[0:128, 0:1],
                                allow_slow_non_contiguous=True).then_inc(s_out, 16)
                    nd += 1
                    g.dma_start(out=dbgh["d_kmax"].ap()[0:16, 0:1],
                                in_=kmax2[0:16, 0:1],
                                allow_slow_non_contiguous=True).then_inc(s_out, 16)
                    nd += 1
                    g.dma_start(out=dbgh["d_kmax"].ap()[0:16, 1:2],
                                in_=qmax2[0:16, 0:1],
                                allow_slow_non_contiguous=True).then_inc(s_out, 16)
                    nd += 1
                    g.wait_ge(s_out, 16 * (NTQ + nd))
        p1.close()
        ctx.close()
        return nc

    p1.close()

    attc = sb("attc", [128, 8 * NQ], BF16)
    Wp_sb = sb("Wp_sb", [128, 8 * D], BF16)
    Win_sb = sb("Win_sb", [128, 8 * M], BF16)
    Wout_sb = sb("Wout_sb", [64, D], BF16)
    bp_rep = sb("bp_rep", [128, D], F32)
    bout_rep = sb("bout_rep", [128, D], F32)
    w1_rep = sb("w1_rep", [128, D], F32)
    b1_rep = sb("b1_rep", [128, D], F32)
    w2_rep = sb("w2_rep", [128, D], F32)
    bin_rep = sb("bin_rep", [128, M], F32)
    b2_rep = sb("b2_rep", [128, M], F32)

    p2 = ExitStack()

    def sb2(name, shape, dt):
        return p2.enter_context(nc.sbuf_tensor(name, shape, dt))

    kdup = [sb2(f"kdup{i}", [128, NK], BF16) for i in range(2)]
    VTW = NTK * 65
    vt = [sb2(f"vt{i}", [128, VTW], BF16) for i in range(2)]
    qdup = [sb2(f"qdup{i}", [128, NQ], BF16) for i in range(2)]
    ex = [sb2(f"ex{i}", [128, 1024], BF16) for i in range(3)]
    recr = [sb2(f"recr{i}", [128, 512], F32) for i in range(2)]
    rtmp = sb2("rtmp", [1, 512], F32)
    atst = [sb2(f"atst{i}", [64, 512], BF16) for i in range(2)]
    cneg = sb2("cneg", [16, 1], F32)

    pL = p2.enter_context(nc.psum_tensor("pL", [128, 3072], F32))
    pA = [p2.enter_context(nc.psum_tensor(f"pA{i}", [128, 512], F32))
          for i in range(2)]

    DOUBLES = [(64, 1, 0, True), (0, 1, 1, False), (0, 2, 2, False),
               (0, 4, 4, False), (0, 8, 8, False), (0, 16, 16, False)]
    NRC = len(DOUBLES)

    def emit_rec_chain(eng, hq, semA, semB, idx):
        eng.wait_ge(s_vrec, hq + 1)
        r = recr[hq % 2]
        for (sA, base) in ((semA, 0), (semB, 32)):
            cnt = 16 * NRC * idx
            for (src0, n, dst0, abs_src) in DOUBLES:
                s0 = src0 if abs_src else base + src0
                if cnt:
                    eng.wait_ge(sA, cnt)
                eng.dma_start(out=r[base + dst0:base + dst0 + n, :],
                              in_=r[s0:s0 + n, :]).then_inc(sA, 16)
                cnt += 16

    def av_inc_of(G):
        hq_, g_ = G // NG, G % NG
        if g_ == NG - 1:
            return ("q", hq_ + 1)
        return ("g", (NG - 1) * hq_ + g_ + 1)

    NBG_PLAIN = 16 + 8 + 1
    rep_specs = [(bph, D), (bouth, D), (W1h, D), (b1h, D), (W2h, D),
                 (binh, M), (b2h, M)]
    rep_dst = [None] * 7
    rep_dst[0] = bp_rep; rep_dst[1] = bout_rep; rep_dst[2] = w1_rep
    rep_dst[3] = b1_rep; rep_dst[4] = w2_rep; rep_dst[5] = bin_rep
    rep_dst[6] = b2_rep
    NW2 = NBG_PLAIN + 7 * 8

    with nc.Block() as b2:
        @b2.vector
        def _(v):
            v.memset(crep[:, :], 0.0)
            v.reduce_max(kmax2[:, :], kgmax[:, :], axis=AX)
            v.reduce_max(qmax2[:, :], qgmax[:, :], axis=AX)
            v.memset(crep[:, :], 0.0)
            v.tensor_mul(cneg[:, :], kmax2[:, :], qmax2[:, :]).then_inc(s_c, 1)
            v.wait_ge(s_cs, 1)
            v.tensor_scalar(cneg[:, :], cneg[:, :], -1.0, -1.0,
                            ALU.mult, ALU.add).then_inc(s_c, 1)
            for h in range(P2H):
                for qb in range(NQB):
                    hq = NQB * h + qb
                    v.wait_ge(s_tAVq, hq + 1)
                    v.tensor_copy(recr[hq % 2][64:65, :],
                                  pA[hq % 2][64:65, :]).then_inc(s_vrec0, 1)
                    if hq % 2 == 0:
                        v.wait_ge(s_rrA, 16 * NRC * (hq // 2 + 1))
                        v.wait_ge(s_rrB, 16 * NRC * (hq // 2 + 1))
                    else:
                        v.wait_ge(s_rgA, 16 * NRC * (hq // 2 + 1))
                        v.wait_ge(s_rgB, 16 * NRC * (hq // 2 + 1))
                    if hq >= 2:
                        v.wait_ge(s_at2, 16 * (hq - 1))
                    v.tensor_tensor(
                        atst[hq % 2][:, :],
                        pA[hq % 2][0:64, :], recr[hq % 2][0:64, :],
                        ALU.mult).then_inc(s_vAT, 1)

        @b2.scalar
        def _(sc):
            def emit_recr(hq_):
                sc.wait_ge(s_vrec0, hq_ + 1)
                sc.activation(out=rtmp[0:1, :],
                              in_=recr[hq_ % 2][64:65, :], func=AF.Ln)
                sc.activation(out=recr[hq_ % 2][64:65, :],
                              in_=rtmp[0:1, :], func=AF.Exp,
                              scale=-1.0).then_inc(s_vrec, 1)

            sc.wait_ge(s_c, 1)
            sc.activation(out=cneg[:, :], in_=cneg[:, :], func=AF.Ln)
            sc.activation(out=cneg[:, :], in_=cneg[:, :],
                          func=AF.Exp, scale=0.5).then_inc(s_cs, 1)
            sc.wait_ge(s_crep, 16 * 8)
            for h in range(P2H):
                for qb in range(NQB):
                    hq = NQB * h + qb
                    for g in range(NG):
                        G = NG * hq + g
                        sc.wait_ge(s_tLp, NG * hq + g + 1)
                        if G >= 3:
                            sm, val = av_inc_of(G - 3)
                            sc.wait_ge(s_tAVg if sm == "g" else s_tAVq, val)
                        sc.activation(
                            out=ex[G % 3][:, :],
                            in_=pL[:, (g % 3) * 1024:(g % 3) * 1024 + 1024],
                            func=AF.Exp,
                            bias=crep[:, h:h + 1]).then_inc(s_exp, 1)
                        if g == 2 and hq >= 1:
                            emit_recr(hq - 1)
            emit_recr(P2H * NQB - 1)

        @b2.tensor
        def _(tn):
            for h in range(P2H):
                for qb in range(NQB):
                    hq = NQB * h + qb
                    if qb == 0:
                        tn.wait_ge(s_k, 32 * (h + 1))
                        tn.wait_ge(s_q2, 32 * (h + 1))
                        tn.wait_ge(s_v2, 16 * NVD * (h + 1))

                    def emit_av_group(g, hq=hq, h=h):
                        G = NG * hq + g
                        last_grp = (g == NG - 1)
                        tn.wait_ge(s_exp, G + 1)
                        if g == 0 and hq >= 2:
                            tn.wait_ge(s_vAT, hq - 1)
                        for i, t in enumerate((2 * g, 2 * g + 1)):
                            ins = tn.matmul(pA[hq % 2][0:65, :],
                                            vt[h % 2][:, t * 65:t * 65 + 65],
                                            ex[G % 3][:, i * 512:(i + 1) * 512],
                                            start=(t == 0), stop=(t == NTK - 1))
                        if last_grp:
                            ins.then_inc(s_tAVq, 1)
                        else:
                            ins.then_inc(s_tAVg, 1)

                    for j in range(NG):
                        Gj = NG * hq + j
                        if hq == 0 and j < 3:
                            pass
                        elif j in (0, 1):
                            tn.wait_ge(s_exp, Gj - 1)
                        elif j == 2:
                            tn.wait_ge(s_exp, Gj - 4)
                        else:
                            tn.wait_ge(s_exp, Gj - 2)
                        ta, tb = 2 * j, 2 * j + 1
                        tn.matmul(pL[:, (ta % 6) * 512:(ta % 6) * 512 + 512],
                                  kdup[h % 2][0:64, ta * 128:(ta + 1) * 128],
                                  qdup[h % 2][0:64, qb * 512:(qb + 1) * 512],
                                  start=True, stop=True, tile_position=(0, 0))
                        ins = tn.matmul(
                            pL[:, (tb % 6) * 512:(tb % 6) * 512 + 512],
                            kdup[h % 2][64:128, tb * 128:(tb + 1) * 128],
                            qdup[h % 2][64:128, qb * 512:(qb + 1) * 512],
                            start=True, stop=True, tile_position=(64, 0))
                        ins.then_inc(s_tLp, 1)
                        if j >= 2:
                            emit_av_group(j - 2)
                    emit_av_group(NG - 2)
                    emit_av_group(NG - 1)

        @b2.sync
        def _(s):
            def at2_dma(hq):
                h_, qb_ = hq // NQB, hq % NQB
                s.wait_ge(s_vAT, hq + 1)
                if p2_stub & 8:
                    s.sem_inc(s_at2, 16)
                    return
                s.dma_start(
                    out=attc[(h_ % 2) * 64:(h_ % 2) * 64 + 64,
                             (h_ // 2) * NQ + qb_ * 512:(h_ // 2) * NQ + qb_ * 512 + 512],
                    in_=atst[hq % 2][:, :]).then_inc(s_at2, 16)

            def loadkq(h):
                if p2_stub & 128:
                    s.sem_inc(s_k, 32)
                    s.sem_inc(s_q2, 32)
                    return
                s.dma_start(out=kdup[h % 2][0:64, :],
                            in_=kT[h * 64:h * 64 + 64, :]).then_inc(s_k, 16)
                s.dma_start(out=kdup[h % 2][64:128, :],
                            in_=kT[h * 64:h * 64 + 64, :]).then_inc(s_k, 16)
                s.dma_start(out=qdup[h % 2][0:64, :],
                            in_=qT[(h % 2) * 64:(h % 2) * 64 + 64,
                                   (h // 2) * NQ:(h // 2 + 1) * NQ]).then_inc(s_q2, 16)
                s.dma_start(out=qdup[h % 2][64:128, :],
                            in_=qT[(h % 2) * 64:(h % 2) * 64 + 64,
                                   (h // 2) * NQ:(h // 2 + 1) * NQ]).then_inc(s_q2, 16)

            s.wait_ge(s_spK, 16 * 8 * NGK)
            s.wait_ge(s_spV, 16 * 4 * NGK)
            loadkq(0)
            for h in range(P2H):
                if h + 1 < P2H:
                    if h >= 1:
                        s.wait_ge(s_tLp, NPAIR * NQB * h)
                    loadkq(h + 1)
                for qb in range(NQB):
                    hq = NQB * h + qb
                    if not (p2_stub & 4):
                        if hq % 2 == 0:
                            emit_rec_chain(s, hq, s_rrA, s_rrB, hq // 2)
                        else:
                            emit_rec_chain(s, hq, s_rgA, s_rgB, hq // 2)
                    at2_dma(hq)

        @b2.gpsimd
        def _(g):
            for i in range(2):
                g.memset(bass.AP(vt[i], 64, [[VTW, 128], [65, NTK], [1, 1]]), 1.0)
            g.wait_ge(s_c, 2)
            if p2_stub & 1024:
                g.sem_inc(s_crep, 16 * 8)
            else:
                g.dma_start(out=crep[0:1, 0:16],
                            in_=cneg[0:16, 0:1]).then_inc(s_crep, 16)
                cnt = 16
                for (src0, n, dst0) in [(0, 1, 1), (0, 2, 2), (0, 4, 4), (0, 8, 8),
                                        (0, 16, 16), (0, 32, 32), (0, 64, 64)]:
                    g.wait_ge(s_crep, cnt)
                    g.dma_start(out=crep[dst0:dst0 + n, :],
                                in_=crep[src0:src0 + n, :]).then_inc(s_crep, 16)
                    cnt += 16
            w2 = [0]

            def w2inc(ins):
                ins.then_inc(s_w2, 16)
                w2[0] += 16
                return w2[0]

            bg = []
            for hc in range(8):
                for hl in range(2):
                    h_ = 2 * hc + hl
                    bg.append(lambda g, hc=hc, hl=hl, h_=h_: w2inc(g.dma_start(
                        out=Wp_sb[hl * 64:(hl + 1) * 64, hc * D:(hc + 1) * D],
                        in_=bass.AP(Wph, h_ * D, [[16 * D, 64], [1, D]]))))
            for dc in range(8):
                bg.append(lambda g, dc=dc: w2inc(g.dma_start(
                    out=Win_sb[:, dc * M:(dc + 1) * M],
                    in_=bass.AP(Winh, dc * 128 * M, [[M, 128], [1, M]]))))
            bg.append(lambda g: w2inc(g.dma_start(
                out=Wout_sb[0:64, :], in_=bass.AP(Wouth, 0, [[D, 64], [1, D]]))))
            rep_state = []
            for i, (srch, width) in enumerate(rep_specs):
                rep_state.append({"dst": rep_dst[i], "srch": srch, "w": width,
                                  "k": 0, "cnt": 0})

            def emit_bg_one(g):
                if bg:
                    bg.pop(0)(g)
                    return True
                st = next((x for x in rep_state if x["k"] < 8), None)
                if st is None:
                    return False
                if st["k"] == 0:
                    st["cnt"] = w2inc(g.dma_start(
                        out=st["dst"][0:1, :],
                        in_=bass.AP(st["srch"], 0, [[st["w"], 1], [1, st["w"]]])))
                else:
                    rows = 1 << (st["k"] - 1)
                    g.wait_ge(s_w2, st["cnt"])
                    st["cnt"] = w2inc(g.dma_start(
                        out=st["dst"][rows:2 * rows, :],
                        in_=st["dst"][0:rows, :]))
                st["k"] += 1
                return True

            for h in range(P2H):
                if h >= 2:
                    g.wait_ge(s_tAVq, NQB * (h - 1))
                if p2_stub & 256:
                    g.sem_inc(s_v2, 16 * NVD)
                else:
                    for mo in range(NVD):
                        g.dma_start(
                            out=bass.AP(vt[h % 2], mo * 8 * 65,
                                        [[VTW, 128], [65, 8], [1, E]]),
                            in_=bass.AP(vhh, (mo * 8 * 128) * D + h * E,
                                        [[D, 128], [128 * D, 8], [1, E]]),
                        ).then_inc(s_v2, 16)
                if not (p2_stub & 512):
                    for _ in range(6):
                        emit_bg_one(g)

            if not (p2_stub & 512):
                while emit_bg_one(g):
                    pass
            else:
                g.sem_inc(s_w2, 16 * NW2)

    p2.close()

    if phases == 2:
        with nc.Block() as bf:
            @bf.gpsimd
            def _(g):
                for t in range(NTQ):
                    g.dma_start(out=outh.ap()[t * 128:(t + 1) * 128, :],
                                in_=bp_rep[:, :]).then_inc(s_out, 16)
                g.wait_ge(s_out, 16 * NTQ)
        ctx.close()
        return nc

    p3 = ExitStack()

    def sb3(name, shape, dt):
        return p3.enter_context(nc.sbuf_tensor(name, shape, dt))

    x3t = [sb3(f"x3t{i}", [128, D], F32) for i in range(2)]
    zt = sb3("zt", [128, D], F32)
    zt2 = sb3("zt2", [128, D], F32)
    x1t = [sb3(f"x1t{i}", [128, D], F32) for i in range(3)]
    ln2t = [sb3(f"ln2t{i}", [128, D], BF16) for i in range(2)]
    ln2T = [sb3(f"ln2T{i}", [128, D], BF16) for i in range(2)]
    s_sb = [sb3(f"s_sb{i}", [128, M], F32) for i in range(2)]
    hbuf = sb3("hbuf", [128, D], F32)
    g0b = sb3("g0b", [128, M], F32)
    gsb = [sb3(f"gsb{i}", [128, M], BF16) for i in range(2)]
    gT = [sb3(f"gT{i}", [64, 128], BF16) for i in range(2)]
    outt = [sb3(f"outt{i}", [128, D], F32) for i in range(2)]
    sq3 = sb3("sq3", [128, D], F32)
    sq3b = sb3("sq3b", [128, D], F32)
    st3 = [sb3(f"st3_{i}", [128, 1], F32) for i in range(6)]
    st3g = sb3("st3g", [128, 1], F32)

    pO = p3.enter_context(nc.psum_tensor("pO", [128, 1024], F32))
    pF = p3.enter_context(nc.psum_tensor("pF", [128, 1024], F32))
    pS = p3.enter_context(nc.psum_tensor("pS", [128, 512], F32))
    ptp3 = [p3.enter_context(nc.psum_tensor(f"ptp3{i}", [128, 128], BF16))
            for i in range(2)]
    ptg = p3.enter_context(nc.psum_tensor("ptg", [128, 128], BF16))

    NT3 = NTQ

    V = {}
    cv = 0
    for t in range(NT3):
        if t >= 1:
            cv += 1; V[("sS", t - 1)] = cv
        if t >= 2:
            cv += 1; V[("gsb", t - 2)] = cv
        cv += 1; V[("a", t)] = cv
        cv += 1; V[("b", t)] = cv
        cv += 1; V[("ln2", t)] = cv
        if t >= 2:
            cv += 1; V[("cF", t - 2)] = cv
            cv += 1; V[("out", t - 2)] = cv
    cv += 1; V[("sS", NT3 - 1)] = cv
    cv += 1; V[("gsb", NT3 - 2)] = cv
    cv += 1; V[("cF", NT3 - 2)] = cv
    cv += 1; V[("out", NT3 - 2)] = cv
    cv += 1; V[("gsb", NT3 - 1)] = cv
    cv += 1; V[("cF", NT3 - 1)] = cv
    cv += 1; V[("out", NT3 - 1)] = cv

    Tt = {}
    ct = 0
    for t in range(NT3):
        ct += 1; Tt[("proj", t)] = ct
        if t >= 2:
            ct += 1; Tt[("ptg", t - 2)] = ct
            ct += 1; Tt[("wout", t - 2)] = ct
        for i in range(8):
            ct += 1; Tt[("tp", t, i)] = ct
        ct += 1; Tt[("win", t)] = ct
    ct += 1; Tt[("ptg", NT3 - 2)] = ct
    ct += 1; Tt[("wout", NT3 - 2)] = ct
    ct += 1; Tt[("ptg", NT3 - 1)] = ct
    ct += 1; Tt[("wout", NT3 - 1)] = ct

    Sc = {}
    Cc = {}
    cs3 = 0
    cc3 = 0
    for t in range(NT3):
        if t >= 2:
            cc3 += 1; Cc[("gt", t - 2)] = cc3
        cs3 += 1; Sc[("a", t)] = cs3
        cs3 += 1; Sc[("b", t)] = cs3
        for i in range(8):
            cc3 += 1; Cc[("tp", t, i)] = cc3
        if t >= 2:
            cs3 += 1; Sc[("c", t - 2)] = cs3
    cc3 += 1; Cc[("gt", NT3 - 2)] = cc3
    cs3 += 1; Sc[("c", NT3 - 2)] = cs3
    cc3 += 1; Cc[("gt", NT3 - 1)] = cc3
    cs3 += 1; Sc[("c", NT3 - 1)] = cs3

    Gg = {}
    cg3 = 0
    for t in range(NT3):
        cg3 += 1; Gg[("hb", t)] = cg3

    with nc.Block() as b3:
        @b3.sync
        def _(s):
            for t in range(min(2, NT3)):
                s.dma_start(out=x3t[t % 2][:, :],
                            in_=xb[t * 128:(t + 1) * 128, :]).then_inc(s3_x, 16)
            for t in range(NT3):
                if t + 2 < NT3:
                    s.wait_ge(s3_v, V[("b", t)])
                    s.dma_start(out=x3t[t % 2][:, :],
                                in_=xb[(t + 2) * 128:(t + 3) * 128, :]
                                ).then_inc(s3_x, 16)
                s.wait_ge(s3_v, V[("out", t)])
                s.dma_start(out=outh.ap()[t * 128:(t + 1) * 128, :],
                            in_=outt[t % 2][:, :]).then_inc(s_out, 16)
            s.wait_ge(s_out, 16 * NT3)

        @b3.tensor
        def _(tn):
            tn.wait_ge(s_w2, 16 * NW2)
            tn.wait_ge(s_at2, 16 * H * NQB)

            def emit_pw(tt):
                tn.wait_ge(s3_v, V[("gsb", tt)])
                if tt >= 1:
                    tn.wait_ge(s3_c, Cc[("gt", tt - 1)])
                tn.transpose(ptg[0:64, 0:128], gsb[tt % 2][:, :],
                             identb[:, :]).then_inc(s3_t, 1)
                tn.wait_ge(s3_c, Cc[("gt", tt)])
                if tt >= 1:
                    tn.wait_ge(s3_v, V[("cF", tt - 1)])
                tn.matmul(pF[:, 0:512], gT[tt % 2][:, :],
                          Wout_sb[0:64, 0:512], start=True, stop=True)
                tn.matmul(pF[:, 512:1024], gT[tt % 2][:, :],
                          Wout_sb[0:64, 512:1024],
                          start=True, stop=True).then_inc(s3_t, 1)

            for t in range(NT3):
                qb, tq = t // 4, t % 4
                if t >= 1:
                    tn.wait_ge(s3_v, V[("a", t - 1)])
                for hc in range(8):
                    lhs = attc[:, hc * NQ + qb * 512 + tq * 128:
                               hc * NQ + qb * 512 + tq * 128 + 128]
                    tn.matmul(pO[:, 0:512], lhs,
                              Wp_sb[:, hc * D:hc * D + 512],
                              start=(hc == 0), stop=(hc == 7))
                    ins = tn.matmul(pO[:, 512:1024], lhs,
                                    Wp_sb[:, hc * D + 512:hc * D + 1024],
                                    start=(hc == 0), stop=(hc == 7))
                ins.then_inc(s3_t, 1)
                if t >= 2:
                    emit_pw(t - 2)
                tn.wait_ge(s3_v, V[("ln2", t)])
                for i in range(8):
                    gi = 8 * t + i
                    if gi >= 2:
                        tt_, ii_ = (t, i - 2) if i >= 2 else (t - 1, i + 6)
                        tn.wait_ge(s3_c, Cc[("tp", tt_, ii_)])
                    tn.transpose(ptp3[i % 2][:, 0:128],
                                 ln2t[t % 2][:, i * 128:(i + 1) * 128],
                                 identb[:, :]).then_inc(s3_t, 1)
                tn.wait_ge(s3_c, Cc[("tp", t, 7)])
                if t >= 1:
                    tn.wait_ge(s3_v, V[("sS", t - 1)])
                for dc in range(8):
                    ins = tn.matmul(pS[:, 0:64],
                                    ln2T[t % 2][:, dc * 128:(dc + 1) * 128],
                                    Win_sb[:, dc * M:(dc + 1) * M],
                                    start=(dc == 0), stop=(dc == 7))
                ins.then_inc(s3_t, 1)
            emit_pw(NT3 - 2)
            emit_pw(NT3 - 1)

        @b3.vector
        def _(v):
            def ln_stats(src):
                v.reduce_sum(st3[4][:, :], src[:, :], axis=AX)
                v.scalar_tensor_tensor(sq3[:, :], src[:, :], 0.0, src[:, :],
                                       ALU.add, ALU.mult,
                                       accum_out=st3[5][:, :])
                v.tensor_scalar_mul(st3[1][:, :], st3[4][:, :], 1.0 / D)
                v.tensor_scalar_mul(st3[5][:, :], st3[5][:, :], 1.0 / D)
                v.tensor_mul(st3[4][:, :], st3[1][:, :], st3[1][:, :])
                v.tensor_sub(st3[5][:, :], st3[5][:, :], st3[4][:, :])
                return v.tensor_scalar_add(st3[5][:, :], st3[5][:, :], EPS)

            def emit_sS(tt):
                v.wait_ge(s3_t, Tt[("win", tt)])
                if tt >= 2:
                    v.wait_ge(s3_g, Gg[("hb", tt - 2)])
                v.tensor_add(s_sb[tt % 2][:, :], pS[:, 0:64],
                             bin_rep[:, :]).then_inc(s3_v, 1)

            def emit_R(tt):
                v.wait_ge(s3_g, Gg[("hb", tt)])
                if tt >= 2:
                    v.wait_ge(s3_t, Tt[("ptg", tt - 2)])
                v.scalar_tensor_tensor(hbuf[:, :], hbuf[:, :], 0.0,
                                       w2_rep[:, :], ALU.max, ALU.mult)
                v.reduce_sum(g0b[:, :],
                             bass.AP(hbuf, 0,
                                     [[D, 128], [INNER, M], [1, INNER]]),
                             axis=AX)
                v.tensor_copy(st3g[:, :], b2_rep[:, 0:1])
                v.tensor_add(gsb[tt % 2][:, :], g0b[:, :],
                             b2_rep[:, :]).then_inc(s3_v, 1)

            def emit_A(tt):
                v.wait_ge(s3_t, Tt[("proj", tt)])
                v.tensor_add(zt[:, :], pO[:, :], bp_rep[:, :])
                ln_stats(zt).then_inc(s3_v, 1)
                v.wait_ge(s3_s, Sc[("a", tt)])
                v.wait_ge(s3_x, 16 * (tt + 1))
                v.tensor_scalar(sq3[:, :], zt[:, :], st3[1][:, :], st3[2][:, :],
                                ALU.subtract, ALU.mult)
                v.tensor_add(x1t[tt % 3][:, :], x3t[tt % 2][:, :], sq3[:, :])
                ln_stats(x1t[tt % 3]).then_inc(s3_v, 1)
                v.wait_ge(s3_s, Sc[("b", tt)])
                if tt >= 2:
                    v.wait_ge(s3_t, Tt[("tp", tt - 2, 7)])
                v.tensor_scalar(ln2t[tt % 2][:, :], x1t[tt % 3][:, :],
                                st3[1][:, :], st3[2][:, :],
                                ALU.subtract, ALU.mult).then_inc(s3_v, 1)

            def emit_F(tt):
                v.wait_ge(s3_t, Tt[("wout", tt)])
                v.tensor_add(zt2[:, :], pF[:, :], bout_rep[:, :])
                ln_stats(zt2).then_inc(s3_v, 1)
                v.wait_ge(s3_s, Sc[("c", tt)])
                if tt >= 2:
                    v.wait_ge(s_out, 16 * (tt - 1))
                v.tensor_scalar(sq3b[:, :], zt2[:, :], st3[1][:, :],
                                st3[2][:, :], ALU.subtract, ALU.mult)
                v.tensor_add(outt[tt % 2][:, :], x1t[tt % 3][:, :],
                             sq3b[:, :]).then_inc(s3_v, 1)

            v.wait_ge(s_w2, 16 * NW2)
            emit_A(0)
            for t in range(1, NT3):
                emit_sS(t - 1)
                if t >= 2:
                    emit_R(t - 2)
                emit_A(t)
                if t >= 2:
                    emit_F(t - 2)
            emit_sS(NT3 - 1)
            emit_R(NT3 - 2)
            emit_F(NT3 - 2)
            emit_R(NT3 - 1)
            emit_F(NT3 - 1)

        @b3.scalar
        def _(sc):
            def rsqrt(v_need):
                sc.wait_ge(s3_v, v_need)
                sc.activation(out=st3[0][:, :], in_=st3[5][:, :], func=AF.Ln)
                sc.activation(out=st3[2][:, :], in_=st3[0][:, :],
                              func=AF.Exp, scale=-0.5).then_inc(s3_s, 1)

            def emit_gt(tt):
                sc.wait_ge(s3_t, Tt[("ptg", tt)])
                if tt >= 2:
                    sc.wait_ge(s3_t, Tt[("wout", tt - 2)])
                sc.activation(out=gT[tt % 2][:, :], in_=ptg[0:64, 0:128],
                              func=AF.Copy).then_inc(s3_c, 1)

            for t in range(NT3):
                if t >= 2:
                    emit_gt(t - 2)
                rsqrt(V[("a", t)])
                rsqrt(V[("b", t)])
                for i in range(8):
                    sc.wait_ge(s3_t, Tt[("tp", t, i)])
                    if t >= 2 and i == 0:
                        sc.wait_ge(s3_t, Tt[("win", t - 2)])
                    sc.activation(out=ln2T[t % 2][:, i * 128:(i + 1) * 128],
                                  in_=ptp3[i % 2][:, 0:128],
                                  func=AF.Copy).then_inc(s3_c, 1)
                if t >= 2:
                    rsqrt(V[("cF", t - 2)])
            emit_gt(NT3 - 2)
            rsqrt(V[("cF", NT3 - 2)])
            emit_gt(NT3 - 1)
            rsqrt(V[("cF", NT3 - 1)])

        @b3.gpsimd
        def _(g):
            g.wait_ge(s_w2, 16 * NW2)
            for t in range(NT3):
                g.wait_ge(s3_v, V[("sS", t)])
                if t >= 1:
                    g.wait_ge(s3_v, V[("gsb", t - 1)])
                g.tensor_tensor(hbuf[:, :],
                                bass.AP(s_sb[t % 2], 0,
                                        [[M, 128], [1, M], [0, INNER]]),
                                w1_rep[:, :], ALU.mult)
                g.tensor_add(hbuf[:, :], hbuf[:, :],
                             b1_rep[:, :]).then_inc(s3_g, 1)
            g.wait_ge(s_out, 16 * NT3)
            if debug:
                nd = 0
                for nm, src in (("d_qT", qT[:, 0:512]),
                                ("d_crep", crep[:, :]),
                                ("d_attc", attc[:, 0:512])):
                    g.dma_start(out=x3t[0][:, 0:src.shape[-1]], in_=src).then_inc(s_out, 16)
                    nd += 1
                    g.wait_ge(s_out, 16 * (NT3 + nd))
                    g.dma_start(out=dbgh[nm].ap()[0:src.shape[0], 0:src.shape[-1]],
                                in_=x3t[0][0:src.shape[0], 0:src.shape[-1]]).then_inc(s_out, 16)
                    nd += 1
                    g.wait_ge(s_out, 16 * (NT3 + nd))
                g.dma_start(out=dbgh["d_kT"].ap()[:, 0:512],
                            in_=kT[0:128, 0:512]).then_inc(s_out, 16)
                nd += 1
                g.wait_ge(s_out, 16 * (NT3 + nd))
                g.dma_start(out=dbgh["d_v"].ap()[:, 0:512],
                            in_=vh[0:128, 0:512]).then_inc(s_out, 16)
                nd += 1
                g.wait_ge(s_out, 16 * (NT3 + nd))
                g.dma_start(out=dbgh["d_kmax"].ap()[0:16, 0:1],
                            in_=kmax2[0:16, 0:1],
                            allow_slow_non_contiguous=True).then_inc(s_out, 16)
                nd += 1
                g.dma_start(out=dbgh["d_kmax"].ap()[0:16, 1:2],
                            in_=qmax2[0:16, 0:1],
                            allow_slow_non_contiguous=True).then_inc(s_out, 16)
                nd += 1
                g.dma_start(out=dbgh["d_cq"].ap()[:, :],
                            in_=dbg_cq[:, :]).then_inc(s_out, 16)
                nd += 1
                g.dma_start(out=dbgh["d_ck"].ap()[:, :],
                            in_=dbg_ck[:, :]).then_inc(s_out, 16)
                nd += 1
                g.dma_start(out=dbgh["d_q"].ap()[:, :],
                            in_=dbg_q[:, :]).then_inc(s_out, 16)
                nd += 1
                g.wait_ge(s_out, 16 * (NT3 + nd))

    p3.close()
    ctx.close()
    return nc


def kernel(L0_attn_query, L0_attn_Wq, L0_attn_Wk, L0_attn_Wv, L0_attn_Wp,
           L0_attn_bp, L0_ffn_Win, L0_ffn_bin_, L0_ffn_W1, L0_ffn_b1,
           L0_ffn_W2, L0_ffn_b2, L0_ffn_Wout, L0_ffn_bout):
    x = np.ascontiguousarray(np.asarray(L0_attn_query, dtype=np.float32))
    assert x.shape == (2, 8192, 1024)
    nc = build_nc(NK=8192, NQ=2048)
    weights = {
        "Wq": np.ascontiguousarray(np.asarray(L0_attn_Wq, np.float32)),
        "Wk": np.ascontiguousarray(np.asarray(L0_attn_Wk, np.float32)),
        "Wv": np.ascontiguousarray(np.asarray(L0_attn_Wv, np.float32)),
        "Wp": np.ascontiguousarray(np.asarray(L0_attn_Wp, np.float32)),
        "bp": np.ascontiguousarray(np.asarray(L0_attn_bp, np.float32)),
        "Win": np.ascontiguousarray(np.asarray(L0_ffn_Win, np.float32)),
        "binp": np.ascontiguousarray(np.asarray(L0_ffn_bin_, np.float32)),
        "W1": np.ascontiguousarray(np.asarray(L0_ffn_W1, np.float32)),
        "b1": np.ascontiguousarray(np.asarray(L0_ffn_b1, np.float32)),
        "W2": np.ascontiguousarray(np.asarray(L0_ffn_W2, np.float32)),
        "b2": np.ascontiguousarray(np.asarray(L0_ffn_b2, np.float32)),
        "Wout": np.ascontiguousarray(np.asarray(L0_ffn_Wout, np.float32)),
        "bout": np.ascontiguousarray(np.asarray(L0_ffn_bout, np.float32)),
    }
    in_maps = []
    for c in range(8):
        b, qq = c // 4, c % 4
        xr = np.roll(x[b], -qq * 2048, axis=0)
        m = {"xb": np.ascontiguousarray(xr)}
        m.update(weights)
        in_maps.append(m)
    res = run_bass_kernel_spmd(nc, in_maps, core_ids=list(range(8)))
    out = np.empty((2, 8192, 1024), np.float32)
    for c in range(8):
        b, qq = c // 4, c % 4
        out[b, qq * 2048:(qq + 1) * 2048] = res.results[c]["out"]
    return out


# version 15
# speedup vs baseline: 1.1284x; 1.0212x over previous
import numpy as np
from contextlib import ExitStack

import concourse.bass as bass
import concourse.mybir as mybir
from concourse.bass_utils import run_bass_kernel_spmd

F32 = mybir.dt.float32
BF16 = mybir.dt.bfloat16
AX = mybir.AxisListType.X
ALU = mybir.AluOpType
AF = mybir.ActivationFunctionType

D = 1024
H = 16
E = 64
M = 64
INNER = 16
EPS = 1e-5
SCALE = 0.125


def build_nc(NK=8192, NQ=2048, debug=False, phases=3, p2_heads=None, p2_stub=0):
    NTK = NK // 128
    NGK = NK // 512
    NTQ = NQ // 128
    NGQ = max(1, NQ // 512)
    NQB = max(1, NQ // 512)
    NG = NTK // 2
    NPAIR = NTK // 2
    NVD = max(1, NTK // 8)
    P2H = H if p2_heads is None else p2_heads

    nc = bass.Bass("TRN2", target_bir_lowering=False,
                   detect_race_conditions=False)

    xbh = nc.dram_tensor("xb", [NK, D], F32, kind="ExternalInput")
    Wqh = nc.dram_tensor("Wq", [H, D, E], F32, kind="ExternalInput")
    Wkh = nc.dram_tensor("Wk", [H, D, E], F32, kind="ExternalInput")
    Wvh = nc.dram_tensor("Wv", [H, D, E], F32, kind="ExternalInput")
    Wph = nc.dram_tensor("Wp", [D, D], F32, kind="ExternalInput")
    bph = nc.dram_tensor("bp", [D], F32, kind="ExternalInput")
    Winh = nc.dram_tensor("Win", [D, M], F32, kind="ExternalInput")
    binh = nc.dram_tensor("binp", [M], F32, kind="ExternalInput")
    W1h = nc.dram_tensor("W1", [M, INNER], F32, kind="ExternalInput")
    b1h = nc.dram_tensor("b1", [M, INNER], F32, kind="ExternalInput")
    W2h = nc.dram_tensor("W2", [M, INNER], F32, kind="ExternalInput")
    b2h = nc.dram_tensor("b2", [M], F32, kind="ExternalInput")
    Wouth = nc.dram_tensor("Wout", [M, D], F32, kind="ExternalInput")
    bouth = nc.dram_tensor("bout", [D], F32, kind="ExternalInput")
    outh = nc.dram_tensor("out", [NQ, D], F32, kind="ExternalOutput")

    kTh = nc.dram_tensor("kT_scr", [D, NK], BF16, kind="Internal")
    vhh = nc.dram_tensor("v_scr", [NK, D], BF16, kind="Internal")
    dbgh = {}
    if debug:
        for nm, shp in (("d_kT", [128, 512]), ("d_v", [128, 512]),
                        ("d_qT", [128, 512]), ("d_crep", [128, 16]),
                        ("d_attc", [128, 512]), ("d_kmax", [16, 2]),
                        ("d_cq", [16, 4 * 512]), ("d_ck", [16, 512]),
                        ("d_q", [128, 512])):
            dbgh[nm] = nc.dram_tensor(nm, shp, F32, kind="ExternalOutput")

    xb = xbh.ap()
    kT = kTh.ap()
    vh = vhh.ap()

    ctx = ExitStack()

    def sb(name, shape, dt):
        return ctx.enter_context(nc.sbuf_tensor(name, shape, dt))

    def sem(name):
        return ctx.enter_context(nc.semaphore(name))

    identb = sb("identb", [128, 128], BF16)
    esel = sb("esel", [128, 8 * 16], BF16)
    if debug:
        dbg_cq = sb("dbg_cq", [16, 4 * 512], F32)
        dbg_ck = sb("dbg_ck", [16, 512], F32)
        dbg_q = sb("dbg_q", [128, 512], F32)
    qT = sb("qT", [128, 8 * NQ], BF16)
    kmax2 = sb("kmax2", [16, 1], F32)
    qmax2 = sb("qmax2", [16, 1], F32)
    kgmax = sb("kgmax", [16, NGK], F32)
    qgmax = sb("qgmax", [16, max(1, NGQ)], F32)
    crep = sb("crep", [128, 16], F32)

    s_w = sem("s_w")
    s_x = sem("s_x"); s_vS = sem("s_vS"); s_sS = sem("s_sS")
    s_vLN = sem("s_vLN"); s_tTP = sem("s_tTP"); s_vTP = sem("s_vTP")
    s_tK = sem("s_tK"); s_vK = sem("s_vK"); s_tV = sem("s_tV"); s_vV = sem("s_vV")
    s_tQ = sem("s_tQ"); s_vQ = sem("s_vQ")
    s_vSQ = sem("s_vSQ"); s_tCK = sem("s_tCK"); s_vCK = sem("s_vCK")
    s_vQS = sem("s_vQS"); s_tCQ = sem("s_tCQ"); s_vCQ = sem("s_vCQ")
    s_spK = sem("s_spK"); s_spV = sem("s_spV"); s_tG = sem("s_tG")
    s_c = sem("s_c"); s_cs = sem("s_cs"); s_crep = sem("s_crep")
    s_k = sem("s_k"); s_v2 = sem("s_v2"); s_q2 = sem("s_q2")
    s_tLp = sem("s_tLp"); s_tL = sem("s_tL")
    s_exp = sem("s_exp"); s_tAVg = sem("s_tAVg"); s_tAVq = sem("s_tAVq")
    s_vrec = sem("s_vrec"); s_vrec0 = sem("s_vrec0")
    s_rrA = sem("s_rrA"); s_rrB = sem("s_rrB")
    s_rgA = sem("s_rgA"); s_rgB = sem("s_rgB"); s_at2 = sem("s_at2")
    s_vAT = sem("s_vAT")
    s_w2 = sem("s_w2")
    s3_x = sem("s3_x"); s3_t = sem("s3_t"); s3_v = sem("s3_v"); s3_s = sem("s3_s")
    s3_c = sem("s3_c"); s3_g = sem("s3_g")
    s_out = sem("s_out")

    p1 = ExitStack()

    def sb1(name, shape, dt):
        return p1.enter_context(nc.sbuf_tensor(name, shape, dt))

    def ps1(name, shape, dt):
        return p1.enter_context(nc.psum_tensor(name, shape, dt))

    Wq_sb = sb1("Wq_sb", [128, 8 * D], BF16)
    Wk_sb = sb1("Wk_sb", [128, 8 * D], BF16)
    Wv_sb = sb1("Wv_sb", [128, 8 * D], BF16)
    x0t = [sb1(f"x0t{i}", [128, D], F32) for i in range(3)]
    sqv = sb1("sqv", [128, D], F32)
    ln1t = [sb1(f"ln1t{i}", [128, D], BF16) for i in range(2)]
    ln1Tg = [sb1(f"ln1Tg{i}", [128, 8 * 512], BF16) for i in range(2)]
    kTsp = [sb1(f"kTsp{i}", [128, 8 * 512], BF16) for i in range(2)]
    vsp = [sb1(f"vsp{i}", [128, 4 * D], BF16) for i in range(2)]
    ksq = [sb1(f"ksq{i}", [128, 8 * 512], BF16) for i in range(2)]
    qsq = [sb1(f"qsq{i}", [128, 8 * 512], BF16) for i in range(2)]
    st_s1 = sb1("st_s1", [128, 1], F32)
    st_s2 = sb1("st_s2", [128, 1], F32)
    st_m = sb1("st_m", [128, 1], F32)
    st_t = sb1("st_t", [128, 1], F32)
    st_rv = sb1("st_rv", [128, 1], F32)
    st_rs = sb1("st_rs", [128, 1], F32)
    cmx = sb1("cmx", [16, 1], F32)

    ptp = [ps1(f"ptp{i}", [128, 1024], BF16) for i in range(2)]
    kps = [ps1(f"kps{i}", [128, 512], F32) for i in range(2)]
    vps = ps1("vps", [128, 1024], F32)
    cps = ps1("cps", [128, 512], F32)
    cpsq = ps1("cpsq", [128, 512], F32)

    NW1 = 24

    with nc.Block() as b0:
        @b0.gpsimd
        def _(g):
            for dc in range(8):
                for (Wh, Wsb) in ((Wqh, Wq_sb), (Wkh, Wk_sb), (Wvh, Wv_sb)):
                    g.dma_start(
                        out=Wsb[:, dc * D:(dc + 1) * D],
                        in_=bass.AP(Wh, dc * 128 * E, [[E, 128], [D * E, H], [1, E]]),
                    ).then_inc(s_w, 16)
            g.memset(identb[:, :], 1.0)
            g.affine_select(out=identb[:, :], in_=identb[:, :],
                            compare_op=ALU.is_equal, fill=0.0, base=0,
                            pattern=[[-1, 128]], channel_multiplier=1)

        @b0.vector
        def _(v):
            v.memset(esel[:, :], 0.0)
            for dc in range(8):
                v.memset(esel[0:64, dc * 16 + 2 * dc:dc * 16 + 2 * dc + 1], 1.0)
                v.memset(esel[64:128, dc * 16 + 2 * dc + 1:dc * 16 + 2 * dc + 2], 1.0)
            v.memset(kmax2[:, :], 0.0)
            v.memset(qmax2[:, :], 0.0)

    with nc.Block() as b1:
        @b1.sync
        def _(s):
            def spill(gp):
                s.wait_ge(s_vK, 8 * gp + 8)
                for hc in range(8):
                    s.dma_start(
                        out=kT[hc * 128:(hc + 1) * 128, gp * 512:(gp + 1) * 512],
                        in_=kTsp[gp % 2][:, hc * 512:(hc + 1) * 512],
                    ).then_inc(s_spK, 16)
                s.wait_ge(s_vV, 4 * gp + 4)
                for mt in range(4):
                    s.dma_start(
                        out=vh[(4 * gp + mt) * 128:(4 * gp + mt + 1) * 128, :],
                        in_=vsp[gp % 2][:, mt * D:(mt + 1) * D],
                    ).then_inc(s_spV, 16)

            for g in range(NGK):
                spill(g)

        @b1.vector
        def _(v):
            for g in range(NGK):
                for tq in range(4):
                    t = 4 * g + tq
                    v.wait_ge(s_x, 16 * (t + 1))
                    x = x0t[t % 3]
                    v.reduce_sum(st_s1[:, :], x[:, :], axis=AX)
                    v.scalar_tensor_tensor(sqv[:, :], x[:, :], 0.0, x[:, :],
                                           ALU.add, ALU.mult,
                                           accum_out=st_s2[:, :])
                    v.tensor_scalar_mul(st_m[:, :], st_s1[:, :], 1.0 / D)
                    v.tensor_scalar_mul(st_t[:, :], st_s2[:, :], 1.0 / D)
                    v.tensor_mul(st_s1[:, :], st_m[:, :], st_m[:, :])
                    v.tensor_sub(st_t[:, :], st_t[:, :], st_s1[:, :])
                    v.tensor_scalar_add(st_t[:, :], st_t[:, :],
                                        EPS).then_inc(s_vS, 1)
                    v.wait_ge(s_sS, t + 1)
                    v.tensor_scalar(ln1t[t % 2][:, :], x[:, :], st_m[:, :],
                                    st_rs[:, :], ALU.subtract,
                                    ALU.mult).then_inc(s_vLN, 1)
                    if tq == 0 and g >= 2:
                        if g - 2 < NGQ:
                            v.wait_ge(s_tQ, 8 * (g - 1))
                        else:
                            v.wait_ge(s_tV, 4 * (g - 1))
                    for dc in range(8):
                        v.wait_ge(s_tTP, 8 * t + dc + 1)
                        v.tensor_copy(
                            ln1Tg[g % 2][:, dc * 512 + tq * 128:dc * 512 + tq * 128 + 128],
                            ptp[dc % 2][:, 0:128]).then_inc(s_vTP, 1)
                if g >= 2:
                    v.wait_ge(s_spK, 16 * 8 * (g - 1))
                for hc in range(8):
                    KI = 8 * g + hc
                    v.wait_ge(s_tK, KI + 1)
                    v.tensor_copy(kTsp[g % 2][:, hc * 512:(hc + 1) * 512],
                                  kps[KI % 2][:, :]).then_inc(s_vK, 1)
                    v.tensor_mul(ksq[g % 2][:, hc * 512:(hc + 1) * 512],
                                 kTsp[g % 2][:, hc * 512:(hc + 1) * 512],
                                 kTsp[g % 2][:, hc * 512:(hc + 1) * 512]
                                 ).then_inc(s_vSQ, 1)
                if g >= 2:
                    v.wait_ge(s_tCK, g - 1)
                    if debug and g == 2:
                        v.tensor_copy(dbg_ck[:, :], cps[0:16, :])
                    v.reduce_max(kgmax[:, g - 2:g - 1], cps[0:16, :],
                                 axis=AX).then_inc(s_vCK, 1)
                if 2 <= g and g - 2 < NGQ:
                    v.wait_ge(s_tCQ, g - 1)
                    if debug:
                        v.tensor_copy(dbg_cq[:, (g - 2) * 512:(g - 1) * 512],
                                      cpsq[0:16, :])
                    v.reduce_max(qgmax[:, g - 2:g - 1], cpsq[0:16, :],
                                 axis=AX).then_inc(s_vCQ, 1)
            v.wait_ge(s_tCK, NGK - 1)
            v.reduce_max(kgmax[:, NGK - 2:NGK - 1], cps[0:16, :],
                         axis=AX).then_inc(s_vCK, 1)
            v.wait_ge(s_tCK, NGK)
            v.reduce_max(kgmax[:, NGK - 1:NGK], cps[0:16, :],
                         axis=AX).then_inc(s_vCK, 1)

        @b1.scalar
        def _(sc):
            def xload(t):
                sc.dma_start(out=x0t[t % 3][:, :],
                             in_=xb[t * 128:(t + 1) * 128, :]).then_inc(s_x, 16)

            for t in range(3):
                xload(t)
            for t in range(NTK):
                sc.wait_ge(s_vS, t + 1)
                sc.activation(out=st_rv[:, :], in_=st_t[:, :], func=AF.Ln)
                sc.activation(out=st_rs[:, :], in_=st_rv[:, :],
                              func=AF.Exp, scale=-0.5).then_inc(s_sS, 1)
                if t + 3 < NTK:
                    sc.wait_ge(s_vLN, t + 1)
                    xload(t + 3)
                if t % 4 == 3:
                    if t // 4 >= 2:
                        sc.wait_ge(s_spV, 16 * 4 * (t // 4 - 1))
                    for mt in range(4):
                        VI = 4 * (t // 4) + mt
                        sc.wait_ge(s_tV, VI + 1)
                        sc.activation(out=vsp[(t // 4) % 2][:, mt * D:(mt + 1) * D],
                                      in_=vps[:, :],
                                      func=AF.Copy).then_inc(s_vV, 1)
                if t % 4 == 0:
                    gq = t // 4 - 1
                    if 0 <= gq < NGQ:
                        for hc in range(8):
                            QI = 8 * gq + hc
                            sc.wait_ge(s_tQ, QI + 1)
                            sc.activation(
                                out=qT[:, hc * NQ + gq * 512:hc * NQ + gq * 512 + 512],
                                in_=kps[QI % 2][:, :], func=AF.Copy,
                                scale=SCALE).then_inc(s_vQ, 1)
                            sc.activation(
                                out=qsq[gq % 2][:, hc * 512:(hc + 1) * 512],
                                in_=kps[QI % 2][:, :], func=AF.Square,
                                scale=SCALE).then_inc(s_vQS, 1)

        @b1.tensor
        def _(tn):
            tn.wait_ge(s_w, 16 * NW1)
            for g in range(NGK):
                for tq in range(4):
                    t = 4 * g + tq
                    tn.wait_ge(s_vLN, t + 1)
                    for dc in range(8):
                        i = 8 * t + dc
                        if i >= 2:
                            tn.wait_ge(s_vTP, i - 1)
                        tn.transpose(ptp[dc % 2][:, 0:128],
                                     ln1t[t % 2][:, dc * 128:(dc + 1) * 128],
                                     identb[:, :]).then_inc(s_tTP, 1)
                tn.wait_ge(s_vTP, 8 * (4 * g + 4))
                for hc in range(8):
                    KI = 8 * g + hc
                    if KI >= 2:
                        tn.wait_ge(s_vK, KI - 1)
                    if g >= 1 and g - 1 < NGQ:
                        tn.wait_ge(s_vQS, 8 * (g - 1) + hc + 1)
                    for dc in range(8):
                        ins = tn.matmul(
                            kps[KI % 2][:, :],
                            Wk_sb[:, dc * D + hc * 128:dc * D + hc * 128 + 128],
                            ln1Tg[g % 2][:, dc * 512:(dc + 1) * 512],
                            start=(dc == 0), stop=(dc == 7))
                    ins.then_inc(s_tK, 1)
                for mt in range(4):
                    VI = 4 * g + mt
                    if VI >= 1:
                        tn.wait_ge(s_vV, VI)
                    for dc in range(8):
                        tn.matmul(vps[:, 0:512],
                                  ln1Tg[g % 2][:, dc * 512 + mt * 128:dc * 512 + mt * 128 + 128],
                                  Wv_sb[:, dc * D:dc * D + 512],
                                  start=(dc == 0), stop=(dc == 7))
                        ins = tn.matmul(
                            vps[:, 512:1024],
                            ln1Tg[g % 2][:, dc * 512 + mt * 128:dc * 512 + mt * 128 + 128],
                            Wv_sb[:, dc * D + 512:dc * D + 1024],
                            start=(dc == 0), stop=(dc == 7))
                    ins.then_inc(s_tV, 1)
                if g < NGQ:
                    tn.wait_ge(s_vK, 8 * (g + 1))
                    for hc in range(8):
                        QI = 8 * g + hc
                        if QI >= 2:
                            tn.wait_ge(s_vQS, QI - 1)
                        for dc in range(8):
                            ins = tn.matmul(
                                kps[QI % 2][:, :],
                                Wq_sb[:, dc * D + hc * 128:dc * D + hc * 128 + 128],
                                ln1Tg[g % 2][:, dc * 512:(dc + 1) * 512],
                                start=(dc == 0), stop=(dc == 7))
                        ins.then_inc(s_tQ, 1)
                def emit_ck(gp):
                    if gp >= 1:
                        tn.wait_ge(s_vCK, gp)
                    for hc in range(8):
                        tn.wait_ge(s_vSQ, 8 * gp + hc + 1)
                        ins = tn.matmul(cps[0:16, :],
                                        esel[:, hc * 16:(hc + 1) * 16],
                                        ksq[gp % 2][:, hc * 512:(hc + 1) * 512],
                                        start=(hc == 0), stop=(hc == 7))
                    ins.then_inc(s_tCK, 1)

                def emit_cq(gp):
                    if gp >= 1:
                        tn.wait_ge(s_vCQ, gp)
                    for hc in range(8):
                        tn.wait_ge(s_vQS, 8 * gp + hc + 1)
                        ins = tn.matmul(cpsq[0:16, :],
                                        esel[:, hc * 16:(hc + 1) * 16],
                                        qsq[gp % 2][:, hc * 512:(hc + 1) * 512],
                                        start=(hc == 0), stop=(hc == 7))
                    ins.then_inc(s_tCQ, 1)

                if g >= 1:
                    emit_ck(g - 1)
                if 1 <= g <= NGQ:
                    emit_cq(g - 1)
                if g == NGK - 1:
                    tn.wait_ge(s_vCK, g)
                    emit_ck(g)

    if phases == 1:
        with nc.Block() as bf:
            @bf.gpsimd
            def _(g):
                for t in range(NTQ):
                    g.dma_start(out=outh.ap()[t * 128:(t + 1) * 128, :],
                                in_=x0t[0][:, :]).then_inc(s_out, 16)
                g.wait_ge(s_out, 16 * NTQ)
                if debug:
                    nd = 0
                    g.dma_start(out=dbgh["d_kT"].ap()[:, 0:512],
                                in_=ln1t[(NTK - 1) % 2][:, 0:512]).then_inc(s_out, 16)
                    nd += 1
                    g.dma_start(out=dbgh["d_v"].ap()[:, 0:512],
                                in_=x0t[(NTK - 1) % 3][:, 0:512]).then_inc(s_out, 16)
                    nd += 1
                    g.dma_start(out=dbgh["d_qT"].ap()[:, 0:512],
                                in_=qT[0:128, 0:512]).then_inc(s_out, 16)
                    nd += 1
                    g.dma_start(out=dbgh["d_attc"].ap()[:, 0:512],
                                in_=ln1Tg[(NGK - 1) % 2][:, 0:512]).then_inc(s_out, 16)
                    nd += 1
                    g.dma_start(out=dbgh["d_crep"].ap()[0:16, 0:16],
                                in_=chist[0:16, 0:16]).then_inc(s_out, 16)
                    nd += 1
                    g.dma_start(out=dbgh["d_attc"].ap()[0:16, 0:512],
                                in_=cpsdump[0:16, 0:512]).then_inc(s_out, 16)
                    nd += 1
                    g.dma_start(out=dbgh["d_kT"].ap()[0:16, 0:20],
                                in_=chist[0:16, 20:40]).then_inc(s_out, 16)
                    nd += 1
                    g.dma_start(out=dbgh["d_crep"].ap()[0:128, 1:2],
                                in_=st_rv[0:128, 0:1],
                                allow_slow_non_contiguous=True).then_inc(s_out, 16)
                    nd += 1
                    g.dma_start(out=dbgh["d_crep"].ap()[0:128, 2:3],
                                in_=st_rs[0:128, 0:1],
                                allow_slow_non_contiguous=True).then_inc(s_out, 16)
                    nd += 1
                    g.dma_start(out=dbgh["d_crep"].ap()[0:128, 3:4],
                                in_=st_s2[0:128, 0:1],
                                allow_slow_non_contiguous=True).then_inc(s_out, 16)
                    nd += 1
                    g.dma_start(out=dbgh["d_kmax"].ap()[0:16, 0:1],
                                in_=kmax2[0:16, 0:1],
                                allow_slow_non_contiguous=True).then_inc(s_out, 16)
                    nd += 1
                    g.dma_start(out=dbgh["d_kmax"].ap()[0:16, 1:2],
                                in_=qmax2[0:16, 0:1],
                                allow_slow_non_contiguous=True).then_inc(s_out, 16)
                    nd += 1
                    g.wait_ge(s_out, 16 * (NTQ + nd))
        p1.close()
        ctx.close()
        return nc

    p1.close()

    attc = sb("attc", [128, 8 * NQ], BF16)
    Wp_sb = sb("Wp_sb", [128, 8 * D], BF16)
    Win_sb = sb("Win_sb", [128, 8 * M], BF16)
    Wout_sb = sb("Wout_sb", [64, D], BF16)
    bp_rep = sb("bp_rep", [128, D], F32)
    bout_rep = sb("bout_rep", [128, D], F32)
    w1_rep = sb("w1_rep", [128, D], F32)
    b1_rep = sb("b1_rep", [128, D], F32)
    w2_rep = sb("w2_rep", [128, D], F32)
    bin_rep = sb("bin_rep", [128, M], F32)
    b2_rep = sb("b2_rep", [128, M], F32)

    p2 = ExitStack()

    def sb2(name, shape, dt):
        return p2.enter_context(nc.sbuf_tensor(name, shape, dt))

    kdup = [sb2(f"kdup{i}", [128, NK], BF16) for i in range(2)]
    VTW = NTK * 65
    vt = [sb2(f"vt{i}", [128, VTW], BF16) for i in range(2)]
    qdup = [sb2(f"qdup{i}", [128, NQ], BF16) for i in range(2)]
    ex = [sb2(f"ex{i}", [128, 1024], BF16) for i in range(3)]
    recr = [sb2(f"recr{i}", [128, 512], F32) for i in range(2)]
    rtmp = sb2("rtmp", [1, 512], F32)
    atst = [sb2(f"atst{i}", [64, 512], BF16) for i in range(2)]
    cneg = sb2("cneg", [16, 1], F32)

    pL = p2.enter_context(nc.psum_tensor("pL", [128, 3072], F32))
    pA = [p2.enter_context(nc.psum_tensor(f"pA{i}", [128, 512], F32))
          for i in range(2)]

    DOUBLES = [(64, 1, 0, True), (0, 1, 1, False), (0, 2, 2, False),
               (0, 4, 4, False), (0, 8, 8, False), (0, 16, 16, False)]
    NRC = len(DOUBLES)

    def emit_rec_chain(eng, hq, semA, semB, idx):
        eng.wait_ge(s_vrec, hq + 1)
        r = recr[hq % 2]
        for (sA, base) in ((semA, 0), (semB, 32)):
            cnt = 16 * NRC * idx
            for (src0, n, dst0, abs_src) in DOUBLES:
                s0 = src0 if abs_src else base + src0
                if cnt:
                    eng.wait_ge(sA, cnt)
                eng.dma_start(out=r[base + dst0:base + dst0 + n, :],
                              in_=r[s0:s0 + n, :]).then_inc(sA, 16)
                cnt += 16

    def av_inc_of(G):
        hq_, g_ = G // NG, G % NG
        if g_ == NG - 1:
            return ("q", hq_ + 1)
        return ("g", (NG - 1) * hq_ + g_ + 1)

    NBG_PLAIN = 16 + 8 + 1
    rep_specs = [(bph, D), (bouth, D), (W1h, D), (b1h, D), (W2h, D),
                 (binh, M), (b2h, M)]
    rep_dst = [None] * 7
    rep_dst[0] = bp_rep; rep_dst[1] = bout_rep; rep_dst[2] = w1_rep
    rep_dst[3] = b1_rep; rep_dst[4] = w2_rep; rep_dst[5] = bin_rep
    rep_dst[6] = b2_rep
    NW2 = NBG_PLAIN + 7 * 8

    with nc.Block() as b2:
        @b2.vector
        def _(v):
            v.memset(crep[:, :], 0.0)
            v.reduce_max(kmax2[:, :], kgmax[:, :], axis=AX)
            v.reduce_max(qmax2[:, :], qgmax[:, :], axis=AX)
            v.memset(crep[:, :], 0.0)
            v.tensor_mul(cneg[:, :], kmax2[:, :], qmax2[:, :]).then_inc(s_c, 1)
            v.wait_ge(s_cs, 1)
            v.tensor_scalar(cneg[:, :], cneg[:, :], -1.0, -1.0,
                            ALU.mult, ALU.add).then_inc(s_c, 1)
            for h in range(P2H):
                for qb in range(NQB):
                    hq = NQB * h + qb
                    v.wait_ge(s_tAVq, hq + 1)
                    v.tensor_copy(recr[hq % 2][64:65, :],
                                  pA[hq % 2][64:65, :]).then_inc(s_vrec0, 1)
                    if hq % 2 == 0:
                        v.wait_ge(s_rrA, 16 * NRC * (hq // 2 + 1))
                        v.wait_ge(s_rrB, 16 * NRC * (hq // 2 + 1))
                    else:
                        v.wait_ge(s_rgA, 16 * NRC * (hq // 2 + 1))
                        v.wait_ge(s_rgB, 16 * NRC * (hq // 2 + 1))
                    if hq >= 2:
                        v.wait_ge(s_at2, 16 * (hq - 1))
                    v.tensor_tensor(
                        atst[hq % 2][:, :],
                        pA[hq % 2][0:64, :], recr[hq % 2][0:64, :],
                        ALU.mult).then_inc(s_vAT, 1)

        @b2.scalar
        def _(sc):
            def emit_recr(hq_):
                sc.wait_ge(s_vrec0, hq_ + 1)
                sc.activation(out=rtmp[0:1, :],
                              in_=recr[hq_ % 2][64:65, :], func=AF.Ln)
                sc.activation(out=recr[hq_ % 2][64:65, :],
                              in_=rtmp[0:1, :], func=AF.Exp,
                              scale=-1.0).then_inc(s_vrec, 1)

            sc.wait_ge(s_c, 1)
            sc.activation(out=cneg[:, :], in_=cneg[:, :], func=AF.Ln)
            sc.activation(out=cneg[:, :], in_=cneg[:, :],
                          func=AF.Exp, scale=0.5).then_inc(s_cs, 1)
            sc.wait_ge(s_crep, 16 * 8)
            for h in range(P2H):
                for qb in range(NQB):
                    hq = NQB * h + qb
                    for g in range(NG):
                        G = NG * hq + g
                        sc.wait_ge(s_tLp, NG * hq + g + 1)
                        if G >= 3:
                            sm, val = av_inc_of(G - 3)
                            sc.wait_ge(s_tAVg if sm == "g" else s_tAVq, val)
                        sc.activation(
                            out=ex[G % 3][:, :],
                            in_=pL[:, (g % 3) * 1024:(g % 3) * 1024 + 1024],
                            func=AF.Exp,
                            bias=crep[:, h:h + 1]).then_inc(s_exp, 1)
                        if g == 2 and hq >= 1:
                            emit_recr(hq - 1)
            emit_recr(P2H * NQB - 1)

        @b2.tensor
        def _(tn):
            for h in range(P2H):
                for qb in range(NQB):
                    hq = NQB * h + qb
                    if qb == 0:
                        tn.wait_ge(s_k, 32 * (h + 1))
                        tn.wait_ge(s_q2, 32 * (h + 1))
                        tn.wait_ge(s_v2, 16 * NVD * (h + 1))

                    def emit_av_group(g, hq=hq, h=h):
                        G = NG * hq + g
                        last_grp = (g == NG - 1)
                        tn.wait_ge(s_exp, G + 1)
                        if g == 0 and hq >= 2:
                            tn.wait_ge(s_vAT, hq - 1)
                        for i, t in enumerate((2 * g, 2 * g + 1)):
                            ins = tn.matmul(pA[hq % 2][0:65, :],
                                            vt[h % 2][:, t * 65:t * 65 + 65],
                                            ex[G % 3][:, i * 512:(i + 1) * 512],
                                            start=(t == 0), stop=(t == NTK - 1))
                        if last_grp:
                            ins.then_inc(s_tAVq, 1)
                        else:
                            ins.then_inc(s_tAVg, 1)

                    for j in range(NG):
                        Gj = NG * hq + j
                        if hq == 0 and j < 3:
                            pass
                        elif j in (0, 1):
                            tn.wait_ge(s_exp, Gj - 1)
                        elif j == 2:
                            tn.wait_ge(s_exp, Gj - 4)
                        else:
                            tn.wait_ge(s_exp, Gj - 2)
                        ta, tb = 2 * j, 2 * j + 1
                        tn.matmul(pL[:, (ta % 6) * 512:(ta % 6) * 512 + 512],
                                  kdup[h % 2][0:64, ta * 128:(ta + 1) * 128],
                                  qdup[h % 2][0:64, qb * 512:(qb + 1) * 512],
                                  start=True, stop=True, tile_position=(0, 0))
                        ins = tn.matmul(
                            pL[:, (tb % 6) * 512:(tb % 6) * 512 + 512],
                            kdup[h % 2][64:128, tb * 128:(tb + 1) * 128],
                            qdup[h % 2][64:128, qb * 512:(qb + 1) * 512],
                            start=True, stop=True, tile_position=(64, 0))
                        ins.then_inc(s_tLp, 1)
                        if j >= 2:
                            emit_av_group(j - 2)
                    emit_av_group(NG - 2)
                    emit_av_group(NG - 1)

        @b2.sync
        def _(s):
            def at2_dma(hq):
                h_, qb_ = hq // NQB, hq % NQB
                s.wait_ge(s_vAT, hq + 1)
                if p2_stub & 8:
                    s.sem_inc(s_at2, 16)
                    return
                s.dma_start(
                    out=attc[(h_ % 2) * 64:(h_ % 2) * 64 + 64,
                             (h_ // 2) * NQ + qb_ * 512:(h_ // 2) * NQ + qb_ * 512 + 512],
                    in_=atst[hq % 2][:, :]).then_inc(s_at2, 16)

            def loadkq(h):
                if p2_stub & 128:
                    s.sem_inc(s_k, 32)
                    s.sem_inc(s_q2, 32)
                    return
                s.dma_start(out=kdup[h % 2][0:64, :],
                            in_=kT[h * 64:h * 64 + 64, :]).then_inc(s_k, 16)
                s.dma_start(out=kdup[h % 2][64:128, :],
                            in_=kT[h * 64:h * 64 + 64, :]).then_inc(s_k, 16)
                s.dma_start(out=qdup[h % 2][0:64, :],
                            in_=qT[(h % 2) * 64:(h % 2) * 64 + 64,
                                   (h // 2) * NQ:(h // 2 + 1) * NQ]).then_inc(s_q2, 16)
                s.dma_start(out=qdup[h % 2][64:128, :],
                            in_=qT[(h % 2) * 64:(h % 2) * 64 + 64,
                                   (h // 2) * NQ:(h // 2 + 1) * NQ]).then_inc(s_q2, 16)

            s.wait_ge(s_spK, 16 * 8 * NGK)
            s.wait_ge(s_spV, 16 * 4 * NGK)
            loadkq(0)
            for h in range(P2H):
                if h + 1 < P2H:
                    if h >= 1:
                        s.wait_ge(s_tLp, NPAIR * NQB * h)
                    loadkq(h + 1)
                for qb in range(NQB):
                    hq = NQB * h + qb
                    if not (p2_stub & 4):
                        if hq % 2 == 0:
                            emit_rec_chain(s, hq, s_rrA, s_rrB, hq // 2)
                        else:
                            emit_rec_chain(s, hq, s_rgA, s_rgB, hq // 2)
                    at2_dma(hq)

        @b2.gpsimd
        def _(g):
            for i in range(2):
                g.memset(bass.AP(vt[i], 64, [[VTW, 128], [65, NTK], [1, 1]]), 1.0)
            g.wait_ge(s_c, 2)
            if p2_stub & 1024:
                g.sem_inc(s_crep, 16 * 8)
            else:
                g.dma_start(out=crep[0:1, 0:16],
                            in_=cneg[0:16, 0:1]).then_inc(s_crep, 16)
                cnt = 16
                for (src0, n, dst0) in [(0, 1, 1), (0, 2, 2), (0, 4, 4), (0, 8, 8),
                                        (0, 16, 16), (0, 32, 32), (0, 64, 64)]:
                    g.wait_ge(s_crep, cnt)
                    g.dma_start(out=crep[dst0:dst0 + n, :],
                                in_=crep[src0:src0 + n, :]).then_inc(s_crep, 16)
                    cnt += 16
            w2 = [0]

            def w2inc(ins):
                ins.then_inc(s_w2, 16)
                w2[0] += 16
                return w2[0]

            bg = []
            for hc in range(8):
                for hl in range(2):
                    h_ = 2 * hc + hl
                    bg.append(lambda g, hc=hc, hl=hl, h_=h_: w2inc(g.dma_start(
                        out=Wp_sb[hl * 64:(hl + 1) * 64, hc * D:(hc + 1) * D],
                        in_=bass.AP(Wph, h_ * D, [[16 * D, 64], [1, D]]))))
            for dc in range(8):
                bg.append(lambda g, dc=dc: w2inc(g.dma_start(
                    out=Win_sb[:, dc * M:(dc + 1) * M],
                    in_=bass.AP(Winh, dc * 128 * M, [[M, 128], [1, M]]))))
            bg.append(lambda g: w2inc(g.dma_start(
                out=Wout_sb[0:64, :], in_=bass.AP(Wouth, 0, [[D, 64], [1, D]]))))
            rep_state = []
            for i, (srch, width) in enumerate(rep_specs):
                rep_state.append({"dst": rep_dst[i], "srch": srch, "w": width,
                                  "k": 0, "cnt": 0})

            def emit_bg_one(g):
                if bg:
                    bg.pop(0)(g)
                    return True
                st = next((x for x in rep_state if x["k"] < 8), None)
                if st is None:
                    return False
                if st["k"] == 0:
                    st["cnt"] = w2inc(g.dma_start(
                        out=st["dst"][0:1, :],
                        in_=bass.AP(st["srch"], 0, [[st["w"], 1], [1, st["w"]]])))
                else:
                    rows = 1 << (st["k"] - 1)
                    g.wait_ge(s_w2, st["cnt"])
                    st["cnt"] = w2inc(g.dma_start(
                        out=st["dst"][rows:2 * rows, :],
                        in_=st["dst"][0:rows, :]))
                st["k"] += 1
                return True

            for h in range(P2H):
                if h >= 2:
                    g.wait_ge(s_tAVq, NQB * (h - 1))
                if p2_stub & 256:
                    g.sem_inc(s_v2, 16 * NVD)
                else:
                    for mo in range(NVD):
                        g.dma_start(
                            out=bass.AP(vt[h % 2], mo * 8 * 65,
                                        [[VTW, 128], [65, 8], [1, E]]),
                            in_=bass.AP(vhh, (mo * 8 * 128) * D + h * E,
                                        [[D, 128], [128 * D, 8], [1, E]]),
                        ).then_inc(s_v2, 16)
                if not (p2_stub & 512):
                    for _ in range(6):
                        emit_bg_one(g)

            if not (p2_stub & 512):
                while emit_bg_one(g):
                    pass
            else:
                g.sem_inc(s_w2, 16 * NW2)

    p2.close()

    if phases == 2:
        with nc.Block() as bf:
            @bf.gpsimd
            def _(g):
                for t in range(NTQ):
                    g.dma_start(out=outh.ap()[t * 128:(t + 1) * 128, :],
                                in_=bp_rep[:, :]).then_inc(s_out, 16)
                g.wait_ge(s_out, 16 * NTQ)
        ctx.close()
        return nc

    p3 = ExitStack()

    def sb3(name, shape, dt):
        return p3.enter_context(nc.sbuf_tensor(name, shape, dt))

    x3t = [sb3(f"x3t{i}", [128, D], F32) for i in range(2)]
    zt = sb3("zt", [128, D], F32)
    zt2 = sb3("zt2", [128, D], F32)
    x1t = [sb3(f"x1t{i}", [128, D], F32) for i in range(3)]
    ln2t = [sb3(f"ln2t{i}", [128, D], BF16) for i in range(2)]
    ln2T = [sb3(f"ln2T{i}", [128, D], BF16) for i in range(2)]
    s_sb = [sb3(f"s_sb{i}", [128, M], F32) for i in range(2)]
    hbuf = sb3("hbuf", [128, D], F32)
    g0b = sb3("g0b", [128, M], F32)
    gsb = [sb3(f"gsb{i}", [128, M], BF16) for i in range(2)]
    gT = [sb3(f"gT{i}", [64, 128], BF16) for i in range(2)]
    outt = [sb3(f"outt{i}", [128, D], F32) for i in range(2)]
    sq3 = sb3("sq3", [128, D], F32)
    sq3b = sb3("sq3b", [128, D], F32)
    st3 = [sb3(f"st3_{i}", [128, 1], F32) for i in range(6)]
    st3g = sb3("st3g", [128, 1], F32)

    pO = p3.enter_context(nc.psum_tensor("pO", [128, 1024], F32))
    pF = p3.enter_context(nc.psum_tensor("pF", [128, 1024], F32))
    pS = p3.enter_context(nc.psum_tensor("pS", [128, 512], F32))
    ptp3 = [p3.enter_context(nc.psum_tensor(f"ptp3{i}", [128, 128], BF16))
            for i in range(2)]
    ptg = p3.enter_context(nc.psum_tensor("ptg", [128, 128], BF16))

    NT3 = NTQ

    V = {}
    cv = 0
    for t in range(NT3):
        if t >= 1:
            cv += 1; V[("sS", t - 1)] = cv
        if t >= 2:
            cv += 1; V[("gsb", t - 2)] = cv
        cv += 1; V[("a", t)] = cv
        cv += 1; V[("b", t)] = cv
        cv += 1; V[("ln2", t)] = cv
        if t >= 2:
            cv += 1; V[("cF", t - 2)] = cv
            cv += 1; V[("out", t - 2)] = cv
    cv += 1; V[("sS", NT3 - 1)] = cv
    cv += 1; V[("gsb", NT3 - 2)] = cv
    cv += 1; V[("cF", NT3 - 2)] = cv
    cv += 1; V[("out", NT3 - 2)] = cv
    cv += 1; V[("gsb", NT3 - 1)] = cv
    cv += 1; V[("cF", NT3 - 1)] = cv
    cv += 1; V[("out", NT3 - 1)] = cv

    Tt = {}
    ct = 0
    for t in range(NT3):
        ct += 1; Tt[("proj", t)] = ct
        if t >= 2:
            ct += 1; Tt[("ptg", t - 2)] = ct
            ct += 1; Tt[("wout", t - 2)] = ct
        for i in range(8):
            ct += 1; Tt[("tp", t, i)] = ct
        ct += 1; Tt[("win", t)] = ct
    ct += 1; Tt[("ptg", NT3 - 2)] = ct
    ct += 1; Tt[("wout", NT3 - 2)] = ct
    ct += 1; Tt[("ptg", NT3 - 1)] = ct
    ct += 1; Tt[("wout", NT3 - 1)] = ct

    Sc = {}
    Cc = {}
    cs3 = 0
    cc3 = 0
    for t in range(NT3):
        if t >= 2:
            cc3 += 1; Cc[("gt", t - 2)] = cc3
        cs3 += 1; Sc[("a", t)] = cs3
        cs3 += 1; Sc[("b", t)] = cs3
        for i in range(8):
            cc3 += 1; Cc[("tp", t, i)] = cc3
        if t >= 2:
            cs3 += 1; Sc[("c", t - 2)] = cs3
    cc3 += 1; Cc[("gt", NT3 - 2)] = cc3
    cs3 += 1; Sc[("c", NT3 - 2)] = cs3
    cc3 += 1; Cc[("gt", NT3 - 1)] = cc3
    cs3 += 1; Sc[("c", NT3 - 1)] = cs3

    Gg = {}
    cg3 = 0
    for t in range(NT3):
        cg3 += 1; Gg[("hb", t)] = cg3

    with nc.Block() as b3:
        @b3.sync
        def _(s):
            for t in range(min(2, NT3)):
                s.dma_start(out=x3t[t % 2][:, :],
                            in_=xb[t * 128:(t + 1) * 128, :]).then_inc(s3_x, 16)
            for t in range(NT3):
                if t + 2 < NT3:
                    s.wait_ge(s3_v, V[("b", t)])
                    s.dma_start(out=x3t[t % 2][:, :],
                                in_=xb[(t + 2) * 128:(t + 3) * 128, :]
                                ).then_inc(s3_x, 16)
                s.wait_ge(s3_v, V[("out", t)])
                s.dma_start(out=outh.ap()[t * 128:(t + 1) * 128, :],
                            in_=outt[t % 2][:, :]).then_inc(s_out, 16)
            s.wait_ge(s_out, 16 * NT3)

        @b3.tensor
        def _(tn):
            tn.wait_ge(s_w2, 16 * NW2)
            tn.wait_ge(s_at2, 16 * H * NQB)

            def emit_pw(tt):
                tn.wait_ge(s3_v, V[("gsb", tt)])
                if tt >= 1:
                    tn.wait_ge(s3_c, Cc[("gt", tt - 1)])
                tn.transpose(ptg[0:64, 0:128], gsb[tt % 2][:, :],
                             identb[:, :]).then_inc(s3_t, 1)
                tn.wait_ge(s3_c, Cc[("gt", tt)])
                if tt >= 1:
                    tn.wait_ge(s3_v, V[("cF", tt - 1)])
                tn.matmul(pF[:, 0:512], gT[tt % 2][:, :],
                          Wout_sb[0:64, 0:512], start=True, stop=True)
                tn.matmul(pF[:, 512:1024], gT[tt % 2][:, :],
                          Wout_sb[0:64, 512:1024],
                          start=True, stop=True).then_inc(s3_t, 1)

            for t in range(NT3):
                qb, tq = t // 4, t % 4
                if t >= 1:
                    tn.wait_ge(s3_v, V[("a", t - 1)])
                for hc in range(8):
                    lhs = attc[:, hc * NQ + qb * 512 + tq * 128:
                               hc * NQ + qb * 512 + tq * 128 + 128]
                    tn.matmul(pO[:, 0:512], lhs,
                              Wp_sb[:, hc * D:hc * D + 512],
                              start=(hc == 0), stop=(hc == 7))
                    ins = tn.matmul(pO[:, 512:1024], lhs,
                                    Wp_sb[:, hc * D + 512:hc * D + 1024],
                                    start=(hc == 0), stop=(hc == 7))
                ins.then_inc(s3_t, 1)
                if t >= 2:
                    emit_pw(t - 2)
                tn.wait_ge(s3_v, V[("ln2", t)])
                for i in range(8):
                    gi = 8 * t + i
                    if gi >= 2:
                        tt_, ii_ = (t, i - 2) if i >= 2 else (t - 1, i + 6)
                        tn.wait_ge(s3_c, Cc[("tp", tt_, ii_)])
                    tn.transpose(ptp3[i % 2][:, 0:128],
                                 ln2t[t % 2][:, i * 128:(i + 1) * 128],
                                 identb[:, :]).then_inc(s3_t, 1)
                tn.wait_ge(s3_c, Cc[("tp", t, 7)])
                if t >= 1:
                    tn.wait_ge(s3_v, V[("sS", t - 1)])
                for dc in range(8):
                    ins = tn.matmul(pS[:, 0:64],
                                    ln2T[t % 2][:, dc * 128:(dc + 1) * 128],
                                    Win_sb[:, dc * M:(dc + 1) * M],
                                    start=(dc == 0), stop=(dc == 7))
                ins.then_inc(s3_t, 1)
            emit_pw(NT3 - 2)
            emit_pw(NT3 - 1)

        @b3.vector
        def _(v):
            def ln_stats(src):
                v.reduce_sum(st3[4][:, :], src[:, :], axis=AX)
                v.scalar_tensor_tensor(sq3[:, :], src[:, :], 0.0, src[:, :],
                                       ALU.add, ALU.mult,
                                       accum_out=st3[5][:, :])
                v.tensor_scalar_mul(st3[1][:, :], st3[4][:, :], 1.0 / D)
                v.tensor_scalar_mul(st3[5][:, :], st3[5][:, :], 1.0 / D)
                v.tensor_mul(st3[4][:, :], st3[1][:, :], st3[1][:, :])
                v.tensor_sub(st3[5][:, :], st3[5][:, :], st3[4][:, :])
                return v.tensor_scalar_add(st3[5][:, :], st3[5][:, :], EPS)

            def emit_sS(tt):
                v.wait_ge(s3_t, Tt[("win", tt)])
                if tt >= 2:
                    v.wait_ge(s3_g, Gg[("hb", tt - 2)])
                v.tensor_add(s_sb[tt % 2][:, :], pS[:, 0:64],
                             bin_rep[:, :]).then_inc(s3_v, 1)

            def emit_R(tt):
                v.wait_ge(s3_g, Gg[("hb", tt)])
                if tt >= 2:
                    v.wait_ge(s3_t, Tt[("ptg", tt - 2)])
                v.scalar_tensor_tensor(hbuf[:, :], hbuf[:, :], 0.0,
                                       w2_rep[:, :], ALU.max, ALU.mult)
                v.reduce_sum(g0b[:, :],
                             bass.AP(hbuf, 0,
                                     [[D, 128], [INNER, M], [1, INNER]]),
                             axis=AX)
                v.tensor_copy(st3g[:, :], b2_rep[:, 0:1])
                v.tensor_add(gsb[tt % 2][:, :], g0b[:, :],
                             b2_rep[:, :]).then_inc(s3_v, 1)

            def emit_A(tt):
                v.wait_ge(s3_t, Tt[("proj", tt)])
                v.tensor_add(zt[:, :], pO[:, :], bp_rep[:, :])
                ln_stats(zt).then_inc(s3_v, 1)
                v.wait_ge(s3_s, Sc[("a", tt)])
                v.wait_ge(s3_x, 16 * (tt + 1))
                v.tensor_scalar(sq3[:, :], zt[:, :], st3[1][:, :], st3[2][:, :],
                                ALU.subtract, ALU.mult)
                v.tensor_add(x1t[tt % 3][:, :], x3t[tt % 2][:, :], sq3[:, :])
                ln_stats(x1t[tt % 3]).then_inc(s3_v, 1)
                v.wait_ge(s3_s, Sc[("b", tt)])
                if tt >= 2:
                    v.wait_ge(s3_t, Tt[("tp", tt - 2, 7)])
                v.tensor_scalar(ln2t[tt % 2][:, :], x1t[tt % 3][:, :],
                                st3[1][:, :], st3[2][:, :],
                                ALU.subtract, ALU.mult).then_inc(s3_v, 1)

            def emit_F(tt):
                v.wait_ge(s3_t, Tt[("wout", tt)])
                v.tensor_add(zt2[:, :], pF[:, :], bout_rep[:, :])
                ln_stats(zt2).then_inc(s3_v, 1)
                v.wait_ge(s3_s, Sc[("c", tt)])
                if tt >= 2:
                    v.wait_ge(s_out, 16 * (tt - 1))
                v.tensor_scalar(sq3b[:, :], zt2[:, :], st3[1][:, :],
                                st3[2][:, :], ALU.subtract, ALU.mult)
                v.tensor_add(outt[tt % 2][:, :], x1t[tt % 3][:, :],
                             sq3b[:, :]).then_inc(s3_v, 1)

            v.wait_ge(s_w2, 16 * NW2)
            emit_A(0)
            for t in range(1, NT3):
                emit_sS(t - 1)
                if t >= 2:
                    emit_R(t - 2)
                emit_A(t)
                if t >= 2:
                    emit_F(t - 2)
            emit_sS(NT3 - 1)
            emit_R(NT3 - 2)
            emit_F(NT3 - 2)
            emit_R(NT3 - 1)
            emit_F(NT3 - 1)

        @b3.scalar
        def _(sc):
            def rsqrt(v_need):
                sc.wait_ge(s3_v, v_need)
                sc.activation(out=st3[0][:, :], in_=st3[5][:, :], func=AF.Ln)
                sc.activation(out=st3[2][:, :], in_=st3[0][:, :],
                              func=AF.Exp, scale=-0.5).then_inc(s3_s, 1)

            def emit_gt(tt):
                sc.wait_ge(s3_t, Tt[("ptg", tt)])
                if tt >= 2:
                    sc.wait_ge(s3_t, Tt[("wout", tt - 2)])
                sc.activation(out=gT[tt % 2][:, :], in_=ptg[0:64, 0:128],
                              func=AF.Copy).then_inc(s3_c, 1)

            for t in range(NT3):
                if t >= 2:
                    emit_gt(t - 2)
                rsqrt(V[("a", t)])
                rsqrt(V[("b", t)])
                for i in range(8):
                    sc.wait_ge(s3_t, Tt[("tp", t, i)])
                    if t >= 2 and i == 0:
                        sc.wait_ge(s3_t, Tt[("win", t - 2)])
                    sc.activation(out=ln2T[t % 2][:, i * 128:(i + 1) * 128],
                                  in_=ptp3[i % 2][:, 0:128],
                                  func=AF.Copy).then_inc(s3_c, 1)
                if t >= 2:
                    rsqrt(V[("cF", t - 2)])
            emit_gt(NT3 - 2)
            rsqrt(V[("cF", NT3 - 2)])
            emit_gt(NT3 - 1)
            rsqrt(V[("cF", NT3 - 1)])

        @b3.gpsimd
        def _(g):
            g.wait_ge(s_w2, 16 * NW2)
            for t in range(NT3):
                g.wait_ge(s3_v, V[("sS", t)])
                if t >= 1:
                    g.wait_ge(s3_v, V[("gsb", t - 1)])
                g.tensor_tensor(hbuf[:, :],
                                bass.AP(s_sb[t % 2], 0,
                                        [[M, 128], [1, M], [0, INNER]]),
                                w1_rep[:, :], ALU.mult)
                g.tensor_add(hbuf[:, :], hbuf[:, :],
                             b1_rep[:, :]).then_inc(s3_g, 1)
            g.wait_ge(s_out, 16 * NT3)
            if debug:
                nd = 0
                for nm, src in (("d_qT", qT[:, 0:512]),
                                ("d_crep", crep[:, :]),
                                ("d_attc", attc[:, 0:512])):
                    g.dma_start(out=x3t[0][:, 0:src.shape[-1]], in_=src).then_inc(s_out, 16)
                    nd += 1
                    g.wait_ge(s_out, 16 * (NT3 + nd))
                    g.dma_start(out=dbgh[nm].ap()[0:src.shape[0], 0:src.shape[-1]],
                                in_=x3t[0][0:src.shape[0], 0:src.shape[-1]]).then_inc(s_out, 16)
                    nd += 1
                    g.wait_ge(s_out, 16 * (NT3 + nd))
                g.dma_start(out=dbgh["d_kT"].ap()[:, 0:512],
                            in_=kT[0:128, 0:512]).then_inc(s_out, 16)
                nd += 1
                g.wait_ge(s_out, 16 * (NT3 + nd))
                g.dma_start(out=dbgh["d_v"].ap()[:, 0:512],
                            in_=vh[0:128, 0:512]).then_inc(s_out, 16)
                nd += 1
                g.wait_ge(s_out, 16 * (NT3 + nd))
                g.dma_start(out=dbgh["d_kmax"].ap()[0:16, 0:1],
                            in_=kmax2[0:16, 0:1],
                            allow_slow_non_contiguous=True).then_inc(s_out, 16)
                nd += 1
                g.dma_start(out=dbgh["d_kmax"].ap()[0:16, 1:2],
                            in_=qmax2[0:16, 0:1],
                            allow_slow_non_contiguous=True).then_inc(s_out, 16)
                nd += 1
                g.dma_start(out=dbgh["d_cq"].ap()[:, :],
                            in_=dbg_cq[:, :]).then_inc(s_out, 16)
                nd += 1
                g.dma_start(out=dbgh["d_ck"].ap()[:, :],
                            in_=dbg_ck[:, :]).then_inc(s_out, 16)
                nd += 1
                g.dma_start(out=dbgh["d_q"].ap()[:, :],
                            in_=dbg_q[:, :]).then_inc(s_out, 16)
                nd += 1
                g.wait_ge(s_out, 16 * (NT3 + nd))

    p3.close()
    ctx.close()
    return nc


def kernel(L0_attn_query, L0_attn_Wq, L0_attn_Wk, L0_attn_Wv, L0_attn_Wp,
           L0_attn_bp, L0_ffn_Win, L0_ffn_bin_, L0_ffn_W1, L0_ffn_b1,
           L0_ffn_W2, L0_ffn_b2, L0_ffn_Wout, L0_ffn_bout):
    x = np.ascontiguousarray(np.asarray(L0_attn_query, dtype=np.float32))
    assert x.shape == (2, 8192, 1024)
    nc = build_nc(NK=8192, NQ=2048)
    weights = {
        "Wq": np.ascontiguousarray(np.asarray(L0_attn_Wq, np.float32)),
        "Wk": np.ascontiguousarray(np.asarray(L0_attn_Wk, np.float32)),
        "Wv": np.ascontiguousarray(np.asarray(L0_attn_Wv, np.float32)),
        "Wp": np.ascontiguousarray(np.asarray(L0_attn_Wp, np.float32)),
        "bp": np.ascontiguousarray(np.asarray(L0_attn_bp, np.float32)),
        "Win": np.ascontiguousarray(np.asarray(L0_ffn_Win, np.float32)),
        "binp": np.ascontiguousarray(np.asarray(L0_ffn_bin_, np.float32)),
        "W1": np.ascontiguousarray(np.asarray(L0_ffn_W1, np.float32)),
        "b1": np.ascontiguousarray(np.asarray(L0_ffn_b1, np.float32)),
        "W2": np.ascontiguousarray(np.asarray(L0_ffn_W2, np.float32)),
        "b2": np.ascontiguousarray(np.asarray(L0_ffn_b2, np.float32)),
        "Wout": np.ascontiguousarray(np.asarray(L0_ffn_Wout, np.float32)),
        "bout": np.ascontiguousarray(np.asarray(L0_ffn_bout, np.float32)),
    }
    in_maps = []
    for c in range(8):
        b, qq = c // 4, c % 4
        xr = np.roll(x[b], -qq * 2048, axis=0)
        m = {"xb": np.ascontiguousarray(xr)}
        m.update(weights)
        in_maps.append(m)
    res = run_bass_kernel_spmd(nc, in_maps, core_ids=list(range(8)))
    out = np.empty((2, 8192, 1024), np.float32)
    for c in range(8):
        b, qq = c // 4, c % 4
        out[b, qq * 2048:(qq + 1) * 2048] = res.results[c]["out"]
    return out


# version 16
# speedup vs baseline: 1.1290x; 1.0006x over previous
import numpy as np
from contextlib import ExitStack

import concourse.bass as bass
import concourse.mybir as mybir
from concourse.bass_utils import run_bass_kernel_spmd

F32 = mybir.dt.float32
BF16 = mybir.dt.bfloat16
AX = mybir.AxisListType.X
ALU = mybir.AluOpType
AF = mybir.ActivationFunctionType

D = 1024
H = 16
E = 64
M = 64
INNER = 16
EPS = 1e-5
SCALE = 0.125


def build_nc(NK=8192, NQ=2048, debug=False, phases=3, p2_heads=None, p2_stub=0):
    NTK = NK // 128
    NGK = NK // 512
    NTQ = NQ // 128
    NGQ = max(1, NQ // 512)
    NQB = max(1, NQ // 512)
    NG = NTK // 2
    NPAIR = NTK // 2
    NVD = max(1, NTK // 8)
    P2H = H if p2_heads is None else p2_heads

    nc = bass.Bass("TRN2", target_bir_lowering=False,
                   detect_race_conditions=False)

    xbh = nc.dram_tensor("xb", [NK, D], F32, kind="ExternalInput")
    Wqh = nc.dram_tensor("Wq", [H, D, E], F32, kind="ExternalInput")
    Wkh = nc.dram_tensor("Wk", [H, D, E], F32, kind="ExternalInput")
    Wvh = nc.dram_tensor("Wv", [H, D, E], F32, kind="ExternalInput")
    Wph = nc.dram_tensor("Wp", [D, D], F32, kind="ExternalInput")
    bph = nc.dram_tensor("bp", [D], F32, kind="ExternalInput")
    Winh = nc.dram_tensor("Win", [D, M], F32, kind="ExternalInput")
    binh = nc.dram_tensor("binp", [M], F32, kind="ExternalInput")
    W1h = nc.dram_tensor("W1", [M, INNER], F32, kind="ExternalInput")
    b1h = nc.dram_tensor("b1", [M, INNER], F32, kind="ExternalInput")
    W2h = nc.dram_tensor("W2", [M, INNER], F32, kind="ExternalInput")
    b2h = nc.dram_tensor("b2", [M], F32, kind="ExternalInput")
    Wouth = nc.dram_tensor("Wout", [M, D], F32, kind="ExternalInput")
    bouth = nc.dram_tensor("bout", [D], F32, kind="ExternalInput")
    outh = nc.dram_tensor("out", [NQ, D], F32, kind="ExternalOutput")

    kTh = nc.dram_tensor("kT_scr", [D, NK], BF16, kind="Internal")
    vhh = nc.dram_tensor("v_scr", [NK, D], BF16, kind="Internal")
    dbgh = {}
    if debug:
        for nm, shp in (("d_kT", [128, 512]), ("d_v", [128, 512]),
                        ("d_qT", [128, 512]), ("d_crep", [128, 16]),
                        ("d_attc", [128, 512]), ("d_kmax", [16, 2]),
                        ("d_cq", [16, 4 * 512]), ("d_ck", [16, 512]),
                        ("d_q", [128, 512])):
            dbgh[nm] = nc.dram_tensor(nm, shp, F32, kind="ExternalOutput")

    xb = xbh.ap()
    kT = kTh.ap()
    vh = vhh.ap()

    ctx = ExitStack()

    def sb(name, shape, dt):
        return ctx.enter_context(nc.sbuf_tensor(name, shape, dt))

    def sem(name):
        return ctx.enter_context(nc.semaphore(name))

    identb = sb("identb", [128, 128], BF16)
    esel = sb("esel", [128, 8 * 16], BF16)
    if debug:
        dbg_cq = sb("dbg_cq", [16, 4 * 512], F32)
        dbg_ck = sb("dbg_ck", [16, 512], F32)
        dbg_q = sb("dbg_q", [128, 512], F32)
    qT = sb("qT", [128, 8 * NQ], BF16)
    kmax2 = sb("kmax2", [16, 1], F32)
    qmax2 = sb("qmax2", [16, 1], F32)
    kgmax = sb("kgmax", [16, NGK], F32)
    qgmax = sb("qgmax", [16, max(1, NGQ)], F32)
    crep = sb("crep", [128, 16], F32)

    s_w = sem("s_w")
    s_x = sem("s_x"); s_vS = sem("s_vS"); s_sS = sem("s_sS")
    s_vLN = sem("s_vLN"); s_tTP = sem("s_tTP"); s_vTP = sem("s_vTP")
    s_tK = sem("s_tK"); s_vK = sem("s_vK"); s_tV = sem("s_tV"); s_vV = sem("s_vV")
    s_tQ = sem("s_tQ"); s_vQ = sem("s_vQ")
    s_vSQ = sem("s_vSQ"); s_tCK = sem("s_tCK"); s_vCK = sem("s_vCK")
    s_vQS = sem("s_vQS"); s_tCQ = sem("s_tCQ"); s_vCQ = sem("s_vCQ")
    s_spK = sem("s_spK"); s_spV = sem("s_spV"); s_tG = sem("s_tG")
    s_c = sem("s_c"); s_cs = sem("s_cs"); s_crep = sem("s_crep")
    s_k = sem("s_k"); s_v2 = sem("s_v2"); s_q2 = sem("s_q2")
    s_tLp = sem("s_tLp"); s_tL = sem("s_tL")
    s_exp = sem("s_exp"); s_tAVg = sem("s_tAVg"); s_tAVq = sem("s_tAVq")
    s_vrec = sem("s_vrec"); s_vrec0 = sem("s_vrec0")
    s_rrA = sem("s_rrA"); s_rrB = sem("s_rrB")
    s_rgA = sem("s_rgA"); s_rgB = sem("s_rgB"); s_at2 = sem("s_at2")
    s_vAT = sem("s_vAT")
    s_w2 = sem("s_w2")
    s3_x = sem("s3_x"); s3_t = sem("s3_t"); s3_v = sem("s3_v"); s3_s = sem("s3_s")
    s3_c = sem("s3_c"); s3_g = sem("s3_g")
    s_out = sem("s_out")

    p1 = ExitStack()

    def sb1(name, shape, dt):
        return p1.enter_context(nc.sbuf_tensor(name, shape, dt))

    def ps1(name, shape, dt):
        return p1.enter_context(nc.psum_tensor(name, shape, dt))

    Wq_sb = sb1("Wq_sb", [128, 8 * D], BF16)
    Wk_sb = sb1("Wk_sb", [128, 8 * D], BF16)
    Wv_sb = sb1("Wv_sb", [128, 8 * D], BF16)
    x0t = [sb1(f"x0t{i}", [128, D], F32) for i in range(3)]
    sqv = sb1("sqv", [128, D], F32)
    ln1t = [sb1(f"ln1t{i}", [128, D], BF16) for i in range(2)]
    ln1Tg = [sb1(f"ln1Tg{i}", [128, 8 * 512], BF16) for i in range(2)]
    kTsp = [sb1(f"kTsp{i}", [128, 8 * 512], BF16) for i in range(2)]
    vsp = [sb1(f"vsp{i}", [128, 4 * D], BF16) for i in range(2)]
    ksq = [sb1(f"ksq{i}", [128, 8 * 512], BF16) for i in range(2)]
    qsq = [sb1(f"qsq{i}", [128, 8 * 512], BF16) for i in range(2)]
    st_s1 = sb1("st_s1", [128, 1], F32)
    st_s2 = sb1("st_s2", [128, 1], F32)
    st_m = sb1("st_m", [128, 1], F32)
    st_t = sb1("st_t", [128, 1], F32)
    st_rv = sb1("st_rv", [128, 1], F32)
    st_rs = sb1("st_rs", [128, 1], F32)
    cmx = sb1("cmx", [16, 1], F32)

    ptp = [ps1(f"ptp{i}", [128, 1024], BF16) for i in range(2)]
    kps = [ps1(f"kps{i}", [128, 512], F32) for i in range(2)]
    vps = ps1("vps", [128, 1024], F32)
    cps = ps1("cps", [128, 512], F32)
    cpsq = ps1("cpsq", [128, 512], F32)

    NW1 = 24

    with nc.Block() as b0:
        @b0.gpsimd
        def _(g):
            for dc in range(8):
                for (Wh, Wsb) in ((Wqh, Wq_sb), (Wkh, Wk_sb), (Wvh, Wv_sb)):
                    g.dma_start(
                        out=Wsb[:, dc * D:(dc + 1) * D],
                        in_=bass.AP(Wh, dc * 128 * E, [[E, 128], [D * E, H], [1, E]]),
                    ).then_inc(s_w, 16)
            g.memset(identb[:, :], 1.0)
            g.affine_select(out=identb[:, :], in_=identb[:, :],
                            compare_op=ALU.is_equal, fill=0.0, base=0,
                            pattern=[[-1, 128]], channel_multiplier=1)

        @b0.vector
        def _(v):
            v.memset(esel[:, :], 0.0)
            for dc in range(8):
                v.memset(esel[0:64, dc * 16 + 2 * dc:dc * 16 + 2 * dc + 1], 1.0)
                v.memset(esel[64:128, dc * 16 + 2 * dc + 1:dc * 16 + 2 * dc + 2], 1.0)
            v.memset(kmax2[:, :], 0.0)
            v.memset(qmax2[:, :], 0.0)

    with nc.Block() as b1:
        @b1.sync
        def _(s):
            def spill(gp):
                s.wait_ge(s_vK, 8 * gp + 8)
                for hc in range(8):
                    s.dma_start(
                        out=kT[hc * 128:(hc + 1) * 128, gp * 512:(gp + 1) * 512],
                        in_=kTsp[gp % 2][:, hc * 512:(hc + 1) * 512],
                    ).then_inc(s_spK, 16)
                s.wait_ge(s_vV, 4 * gp + 4)
                for mt in range(4):
                    s.dma_start(
                        out=vh[(4 * gp + mt) * 128:(4 * gp + mt + 1) * 128, :],
                        in_=vsp[gp % 2][:, mt * D:(mt + 1) * D],
                    ).then_inc(s_spV, 16)

            for g in range(NGK):
                spill(g)

        @b1.vector
        def _(v):
            for g in range(NGK):
                for tq in range(4):
                    t = 4 * g + tq
                    v.wait_ge(s_x, 16 * (t + 1))
                    x = x0t[t % 3]
                    v.reduce_sum(st_s1[:, :], x[:, :], axis=AX)
                    v.scalar_tensor_tensor(sqv[:, :], x[:, :], 0.0, x[:, :],
                                           ALU.add, ALU.mult,
                                           accum_out=st_s2[:, :])
                    v.tensor_scalar_mul(st_m[:, :], st_s1[:, :], 1.0 / D)
                    v.tensor_scalar_mul(st_t[:, :], st_s2[:, :], 1.0 / D)
                    v.tensor_mul(st_s1[:, :], st_m[:, :], st_m[:, :])
                    v.tensor_sub(st_t[:, :], st_t[:, :], st_s1[:, :])
                    v.tensor_scalar_add(st_t[:, :], st_t[:, :],
                                        EPS).then_inc(s_vS, 1)
                    v.tensor_scalar_sub(sqv[:, :], x[:, :], st_m[:, :])
                    v.wait_ge(s_sS, t + 1)
                    v.tensor_scalar_mul(ln1t[t % 2][:, :], sqv[:, :],
                                        st_rs[:, :]).then_inc(s_vLN, 1)
                    if tq == 0 and g >= 2:
                        if g - 2 < NGQ:
                            v.wait_ge(s_tQ, 8 * (g - 1))
                        else:
                            v.wait_ge(s_tV, 4 * (g - 1))
                    for dc in range(8):
                        v.wait_ge(s_tTP, 8 * t + dc + 1)
                        v.tensor_copy(
                            ln1Tg[g % 2][:, dc * 512 + tq * 128:dc * 512 + tq * 128 + 128],
                            ptp[dc % 2][:, 0:128]).then_inc(s_vTP, 1)
                if g >= 2:
                    v.wait_ge(s_spK, 16 * 8 * (g - 1))
                for hc in range(8):
                    KI = 8 * g + hc
                    v.wait_ge(s_tK, KI + 1)
                    v.tensor_copy(kTsp[g % 2][:, hc * 512:(hc + 1) * 512],
                                  kps[KI % 2][:, :]).then_inc(s_vK, 1)
                    v.tensor_mul(ksq[g % 2][:, hc * 512:(hc + 1) * 512],
                                 kTsp[g % 2][:, hc * 512:(hc + 1) * 512],
                                 kTsp[g % 2][:, hc * 512:(hc + 1) * 512]
                                 ).then_inc(s_vSQ, 1)
                if g >= 2:
                    v.wait_ge(s_tCK, g - 1)
                    if debug and g == 2:
                        v.tensor_copy(dbg_ck[:, :], cps[0:16, :])
                    v.reduce_max(kgmax[:, g - 2:g - 1], cps[0:16, :],
                                 axis=AX).then_inc(s_vCK, 1)
                if 2 <= g and g - 2 < NGQ:
                    v.wait_ge(s_tCQ, g - 1)
                    if debug:
                        v.tensor_copy(dbg_cq[:, (g - 2) * 512:(g - 1) * 512],
                                      cpsq[0:16, :])
                    v.reduce_max(qgmax[:, g - 2:g - 1], cpsq[0:16, :],
                                 axis=AX).then_inc(s_vCQ, 1)
            v.wait_ge(s_tCK, NGK - 1)
            v.reduce_max(kgmax[:, NGK - 2:NGK - 1], cps[0:16, :],
                         axis=AX).then_inc(s_vCK, 1)
            v.wait_ge(s_tCK, NGK)
            v.reduce_max(kgmax[:, NGK - 1:NGK], cps[0:16, :],
                         axis=AX).then_inc(s_vCK, 1)

        @b1.scalar
        def _(sc):
            def xload(t):
                sc.dma_start(out=x0t[t % 3][:, :],
                             in_=xb[t * 128:(t + 1) * 128, :]).then_inc(s_x, 16)

            for t in range(3):
                xload(t)
            for t in range(NTK):
                if t % 4 == 0:
                    gq = t // 4 - 1
                    if 0 <= gq < NGQ:
                        for hc in range(8):
                            QI = 8 * gq + hc
                            sc.wait_ge(s_tQ, QI + 1)
                            sc.activation(
                                out=qT[:, hc * NQ + gq * 512:hc * NQ + gq * 512 + 512],
                                in_=kps[QI % 2][:, :], func=AF.Copy,
                                scale=SCALE).then_inc(s_vQ, 1)
                            sc.activation(
                                out=qsq[gq % 2][:, hc * 512:(hc + 1) * 512],
                                in_=kps[QI % 2][:, :], func=AF.Square,
                                scale=SCALE).then_inc(s_vQS, 1)
                sc.wait_ge(s_vS, t + 1)
                sc.activation(out=st_rv[:, :], in_=st_t[:, :], func=AF.Ln)
                sc.activation(out=st_rs[:, :], in_=st_rv[:, :],
                              func=AF.Exp, scale=-0.5).then_inc(s_sS, 1)
                if t + 3 < NTK:
                    sc.wait_ge(s_vLN, t + 1)
                    xload(t + 3)
                if t % 4 == 3:
                    if t // 4 >= 2:
                        sc.wait_ge(s_spV, 16 * 4 * (t // 4 - 1))
                    for mt in range(4):
                        VI = 4 * (t // 4) + mt
                        sc.wait_ge(s_tV, VI + 1)
                        sc.activation(out=vsp[(t // 4) % 2][:, mt * D:(mt + 1) * D],
                                      in_=vps[:, :],
                                      func=AF.Copy).then_inc(s_vV, 1)

        @b1.tensor
        def _(tn):
            tn.wait_ge(s_w, 16 * NW1)
            for g in range(NGK):
                for tq in range(4):
                    t = 4 * g + tq
                    tn.wait_ge(s_vLN, t + 1)
                    for dc in range(8):
                        i = 8 * t + dc
                        if i >= 2:
                            tn.wait_ge(s_vTP, i - 1)
                        tn.transpose(ptp[dc % 2][:, 0:128],
                                     ln1t[t % 2][:, dc * 128:(dc + 1) * 128],
                                     identb[:, :]).then_inc(s_tTP, 1)
                tn.wait_ge(s_vTP, 8 * (4 * g + 4))
                for hc in range(8):
                    KI = 8 * g + hc
                    if KI >= 2:
                        tn.wait_ge(s_vK, KI - 1)
                    if g >= 1 and g - 1 < NGQ:
                        tn.wait_ge(s_vQS, 8 * (g - 1) + hc + 1)
                    for dc in range(8):
                        ins = tn.matmul(
                            kps[KI % 2][:, :],
                            Wk_sb[:, dc * D + hc * 128:dc * D + hc * 128 + 128],
                            ln1Tg[g % 2][:, dc * 512:(dc + 1) * 512],
                            start=(dc == 0), stop=(dc == 7))
                    ins.then_inc(s_tK, 1)
                for mt in range(4):
                    VI = 4 * g + mt
                    if VI >= 1:
                        tn.wait_ge(s_vV, VI)
                    for dc in range(8):
                        tn.matmul(vps[:, 0:512],
                                  ln1Tg[g % 2][:, dc * 512 + mt * 128:dc * 512 + mt * 128 + 128],
                                  Wv_sb[:, dc * D:dc * D + 512],
                                  start=(dc == 0), stop=(dc == 7))
                        ins = tn.matmul(
                            vps[:, 512:1024],
                            ln1Tg[g % 2][:, dc * 512 + mt * 128:dc * 512 + mt * 128 + 128],
                            Wv_sb[:, dc * D + 512:dc * D + 1024],
                            start=(dc == 0), stop=(dc == 7))
                    ins.then_inc(s_tV, 1)
                if g < NGQ:
                    tn.wait_ge(s_vK, 8 * (g + 1))
                    for hc in range(8):
                        QI = 8 * g + hc
                        if QI >= 2:
                            tn.wait_ge(s_vQS, QI - 1)
                        for dc in range(8):
                            ins = tn.matmul(
                                kps[QI % 2][:, :],
                                Wq_sb[:, dc * D + hc * 128:dc * D + hc * 128 + 128],
                                ln1Tg[g % 2][:, dc * 512:(dc + 1) * 512],
                                start=(dc == 0), stop=(dc == 7))
                        ins.then_inc(s_tQ, 1)
                def emit_ck(gp):
                    if gp >= 1:
                        tn.wait_ge(s_vCK, gp)
                    for hc in range(8):
                        tn.wait_ge(s_vSQ, 8 * gp + hc + 1)
                        ins = tn.matmul(cps[0:16, :],
                                        esel[:, hc * 16:(hc + 1) * 16],
                                        ksq[gp % 2][:, hc * 512:(hc + 1) * 512],
                                        start=(hc == 0), stop=(hc == 7))
                    ins.then_inc(s_tCK, 1)

                def emit_cq(gp):
                    if gp >= 1:
                        tn.wait_ge(s_vCQ, gp)
                    for hc in range(8):
                        tn.wait_ge(s_vQS, 8 * gp + hc + 1)
                        ins = tn.matmul(cpsq[0:16, :],
                                        esel[:, hc * 16:(hc + 1) * 16],
                                        qsq[gp % 2][:, hc * 512:(hc + 1) * 512],
                                        start=(hc == 0), stop=(hc == 7))
                    ins.then_inc(s_tCQ, 1)

                if g >= 1:
                    emit_ck(g - 1)
                if 1 <= g <= NGQ:
                    emit_cq(g - 1)
                if g == NGK - 1:
                    tn.wait_ge(s_vCK, g)
                    emit_ck(g)

    if phases == 1:
        with nc.Block() as bf:
            @bf.gpsimd
            def _(g):
                for t in range(NTQ):
                    g.dma_start(out=outh.ap()[t * 128:(t + 1) * 128, :],
                                in_=x0t[0][:, :]).then_inc(s_out, 16)
                g.wait_ge(s_out, 16 * NTQ)
                if debug:
                    nd = 0
                    g.dma_start(out=dbgh["d_kT"].ap()[:, 0:512],
                                in_=ln1t[(NTK - 1) % 2][:, 0:512]).then_inc(s_out, 16)
                    nd += 1
                    g.dma_start(out=dbgh["d_v"].ap()[:, 0:512],
                                in_=x0t[(NTK - 1) % 3][:, 0:512]).then_inc(s_out, 16)
                    nd += 1
                    g.dma_start(out=dbgh["d_qT"].ap()[:, 0:512],
                                in_=qT[0:128, 0:512]).then_inc(s_out, 16)
                    nd += 1
                    g.dma_start(out=dbgh["d_attc"].ap()[:, 0:512],
                                in_=ln1Tg[(NGK - 1) % 2][:, 0:512]).then_inc(s_out, 16)
                    nd += 1
                    g.dma_start(out=dbgh["d_crep"].ap()[0:16, 0:16],
                                in_=chist[0:16, 0:16]).then_inc(s_out, 16)
                    nd += 1
                    g.dma_start(out=dbgh["d_attc"].ap()[0:16, 0:512],
                                in_=cpsdump[0:16, 0:512]).then_inc(s_out, 16)
                    nd += 1
                    g.dma_start(out=dbgh["d_kT"].ap()[0:16, 0:20],
                                in_=chist[0:16, 20:40]).then_inc(s_out, 16)
                    nd += 1
                    g.dma_start(out=dbgh["d_crep"].ap()[0:128, 1:2],
                                in_=st_rv[0:128, 0:1],
                                allow_slow_non_contiguous=True).then_inc(s_out, 16)
                    nd += 1
                    g.dma_start(out=dbgh["d_crep"].ap()[0:128, 2:3],
                                in_=st_rs[0:128, 0:1],
                                allow_slow_non_contiguous=True).then_inc(s_out, 16)
                    nd += 1
                    g.dma_start(out=dbgh["d_crep"].ap()[0:128, 3:4],
                                in_=st_s2[0:128, 0:1],
                                allow_slow_non_contiguous=True).then_inc(s_out, 16)
                    nd += 1
                    g.dma_start(out=dbgh["d_kmax"].ap()[0:16, 0:1],
                                in_=kmax2[0:16, 0:1],
                                allow_slow_non_contiguous=True).then_inc(s_out, 16)
                    nd += 1
                    g.dma_start(out=dbgh["d_kmax"].ap()[0:16, 1:2],
                                in_=qmax2[0:16, 0:1],
                                allow_slow_non_contiguous=True).then_inc(s_out, 16)
                    nd += 1
                    g.wait_ge(s_out, 16 * (NTQ + nd))
        p1.close()
        ctx.close()
        return nc

    p1.close()

    attc = sb("attc", [128, 8 * NQ], BF16)
    Wp_sb = sb("Wp_sb", [128, 8 * D], BF16)
    Win_sb = sb("Win_sb", [128, 8 * M], BF16)
    Wout_sb = sb("Wout_sb", [64, D], BF16)
    bp_rep = sb("bp_rep", [128, D], F32)
    bout_rep = sb("bout_rep", [128, D], F32)
    w1_rep = sb("w1_rep", [128, D], F32)
    b1_rep = sb("b1_rep", [128, D], F32)
    w2_rep = sb("w2_rep", [128, D], F32)
    bin_rep = sb("bin_rep", [128, M], F32)
    b2_rep = sb("b2_rep", [128, M], F32)

    p2 = ExitStack()

    def sb2(name, shape, dt):
        return p2.enter_context(nc.sbuf_tensor(name, shape, dt))

    kdup = [sb2(f"kdup{i}", [128, NK], BF16) for i in range(2)]
    VTW = NTK * 65
    vt = [sb2(f"vt{i}", [128, VTW], BF16) for i in range(2)]
    qdup = [sb2(f"qdup{i}", [128, NQ], BF16) for i in range(2)]
    ex = [sb2(f"ex{i}", [128, 1024], BF16) for i in range(3)]
    recr = [sb2(f"recr{i}", [128, 512], F32) for i in range(2)]
    rtmp = sb2("rtmp", [1, 512], F32)
    atst = [sb2(f"atst{i}", [64, 512], BF16) for i in range(2)]
    cneg = sb2("cneg", [16, 1], F32)

    pL = p2.enter_context(nc.psum_tensor("pL", [128, 3072], F32))
    pA = [p2.enter_context(nc.psum_tensor(f"pA{i}", [128, 512], F32))
          for i in range(2)]

    DOUBLES = [(64, 1, 0, True), (0, 1, 1, False), (0, 2, 2, False),
               (0, 4, 4, False), (0, 8, 8, False), (0, 16, 16, False)]
    NRC = len(DOUBLES)

    def emit_rec_chain(eng, hq, semA, semB, idx):
        eng.wait_ge(s_vrec, hq + 1)
        r = recr[hq % 2]
        for (sA, base) in ((semA, 0), (semB, 32)):
            cnt = 16 * NRC * idx
            for (src0, n, dst0, abs_src) in DOUBLES:
                s0 = src0 if abs_src else base + src0
                if cnt:
                    eng.wait_ge(sA, cnt)
                eng.dma_start(out=r[base + dst0:base + dst0 + n, :],
                              in_=r[s0:s0 + n, :]).then_inc(sA, 16)
                cnt += 16

    def av_inc_of(G):
        hq_, g_ = G // NG, G % NG
        if g_ == NG - 1:
            return ("q", hq_ + 1)
        return ("g", (NG - 1) * hq_ + g_ + 1)

    NBG_PLAIN = 16 + 8 + 1
    rep_specs = [(bph, D), (bouth, D), (W1h, D), (b1h, D), (W2h, D),
                 (binh, M), (b2h, M)]
    rep_dst = [None] * 7
    rep_dst[0] = bp_rep; rep_dst[1] = bout_rep; rep_dst[2] = w1_rep
    rep_dst[3] = b1_rep; rep_dst[4] = w2_rep; rep_dst[5] = bin_rep
    rep_dst[6] = b2_rep
    NW2 = NBG_PLAIN + 7 * 8

    with nc.Block() as b2:
        @b2.vector
        def _(v):
            v.memset(crep[:, :], 0.0)
            v.reduce_max(kmax2[:, :], kgmax[:, :], axis=AX)
            v.reduce_max(qmax2[:, :], qgmax[:, :], axis=AX)
            v.memset(crep[:, :], 0.0)
            v.tensor_mul(cneg[:, :], kmax2[:, :], qmax2[:, :]).then_inc(s_c, 1)
            v.wait_ge(s_cs, 1)
            v.tensor_scalar(cneg[:, :], cneg[:, :], -1.0, -1.0,
                            ALU.mult, ALU.add).then_inc(s_c, 1)
            for h in range(P2H):
                for qb in range(NQB):
                    hq = NQB * h + qb
                    v.wait_ge(s_tAVq, hq + 1)
                    v.tensor_copy(recr[hq % 2][64:65, :],
                                  pA[hq % 2][64:65, :]).then_inc(s_vrec0, 1)
                    if hq % 2 == 0:
                        v.wait_ge(s_rrA, 16 * NRC * (hq // 2 + 1))
                        v.wait_ge(s_rrB, 16 * NRC * (hq // 2 + 1))
                    else:
                        v.wait_ge(s_rgA, 16 * NRC * (hq // 2 + 1))
                        v.wait_ge(s_rgB, 16 * NRC * (hq // 2 + 1))
                    if hq >= 2:
                        v.wait_ge(s_at2, 16 * (hq - 1))
                    v.tensor_tensor(
                        atst[hq % 2][:, :],
                        pA[hq % 2][0:64, :], recr[hq % 2][0:64, :],
                        ALU.mult).then_inc(s_vAT, 1)

        @b2.scalar
        def _(sc):
            def emit_recr(hq_):
                sc.wait_ge(s_vrec0, hq_ + 1)
                sc.activation(out=rtmp[0:1, :],
                              in_=recr[hq_ % 2][64:65, :], func=AF.Ln)
                sc.activation(out=recr[hq_ % 2][64:65, :],
                              in_=rtmp[0:1, :], func=AF.Exp,
                              scale=-1.0).then_inc(s_vrec, 1)

            sc.wait_ge(s_c, 1)
            sc.activation(out=cneg[:, :], in_=cneg[:, :], func=AF.Ln)
            sc.activation(out=cneg[:, :], in_=cneg[:, :],
                          func=AF.Exp, scale=0.5).then_inc(s_cs, 1)
            sc.wait_ge(s_crep, 16 * 8)
            for h in range(P2H):
                for qb in range(NQB):
                    hq = NQB * h + qb
                    for g in range(NG):
                        G = NG * hq + g
                        sc.wait_ge(s_tLp, NG * hq + g + 1)
                        if G >= 3:
                            sm, val = av_inc_of(G - 3)
                            sc.wait_ge(s_tAVg if sm == "g" else s_tAVq, val)
                        sc.activation(
                            out=ex[G % 3][:, :],
                            in_=pL[:, (g % 3) * 1024:(g % 3) * 1024 + 1024],
                            func=AF.Exp,
                            bias=crep[:, h:h + 1]).then_inc(s_exp, 1)
                        if g == 2 and hq >= 1:
                            emit_recr(hq - 1)
            emit_recr(P2H * NQB - 1)

        @b2.tensor
        def _(tn):
            for h in range(P2H):
                for qb in range(NQB):
                    hq = NQB * h + qb
                    if qb == 0:
                        tn.wait_ge(s_k, 32 * (h + 1))
                        tn.wait_ge(s_q2, 32 * (h + 1))
                        tn.wait_ge(s_v2, 16 * NVD * (h + 1))

                    def emit_av_group(g, hq=hq, h=h):
                        G = NG * hq + g
                        last_grp = (g == NG - 1)
                        tn.wait_ge(s_exp, G + 1)
                        if g == 0 and hq >= 2:
                            tn.wait_ge(s_vAT, hq - 1)
                        for i, t in enumerate((2 * g, 2 * g + 1)):
                            ins = tn.matmul(pA[hq % 2][0:65, :],
                                            vt[h % 2][:, t * 65:t * 65 + 65],
                                            ex[G % 3][:, i * 512:(i + 1) * 512],
                                            start=(t == 0), stop=(t == NTK - 1))
                        if last_grp:
                            ins.then_inc(s_tAVq, 1)
                        else:
                            ins.then_inc(s_tAVg, 1)

                    for j in range(NG):
                        Gj = NG * hq + j
                        if hq == 0 and j < 3:
                            pass
                        elif j in (0, 1):
                            tn.wait_ge(s_exp, Gj - 1)
                        elif j == 2:
                            tn.wait_ge(s_exp, Gj - 4)
                        else:
                            tn.wait_ge(s_exp, Gj - 2)
                        ta, tb = 2 * j, 2 * j + 1
                        tn.matmul(pL[:, (ta % 6) * 512:(ta % 6) * 512 + 512],
                                  kdup[h % 2][0:64, ta * 128:(ta + 1) * 128],
                                  qdup[h % 2][0:64, qb * 512:(qb + 1) * 512],
                                  start=True, stop=True, tile_position=(0, 0))
                        ins = tn.matmul(
                            pL[:, (tb % 6) * 512:(tb % 6) * 512 + 512],
                            kdup[h % 2][64:128, tb * 128:(tb + 1) * 128],
                            qdup[h % 2][64:128, qb * 512:(qb + 1) * 512],
                            start=True, stop=True, tile_position=(64, 0))
                        ins.then_inc(s_tLp, 1)
                        if j >= 2:
                            emit_av_group(j - 2)
                    emit_av_group(NG - 2)
                    emit_av_group(NG - 1)

        @b2.sync
        def _(s):
            def at2_dma(hq):
                h_, qb_ = hq // NQB, hq % NQB
                s.wait_ge(s_vAT, hq + 1)
                if p2_stub & 8:
                    s.sem_inc(s_at2, 16)
                    return
                s.dma_start(
                    out=attc[(h_ % 2) * 64:(h_ % 2) * 64 + 64,
                             (h_ // 2) * NQ + qb_ * 512:(h_ // 2) * NQ + qb_ * 512 + 512],
                    in_=atst[hq % 2][:, :]).then_inc(s_at2, 16)

            def loadkq(h):
                if p2_stub & 128:
                    s.sem_inc(s_k, 32)
                    s.sem_inc(s_q2, 32)
                    return
                s.dma_start(out=kdup[h % 2][0:64, :],
                            in_=kT[h * 64:h * 64 + 64, :]).then_inc(s_k, 16)
                s.dma_start(out=kdup[h % 2][64:128, :],
                            in_=kT[h * 64:h * 64 + 64, :]).then_inc(s_k, 16)
                s.dma_start(out=qdup[h % 2][0:64, :],
                            in_=qT[(h % 2) * 64:(h % 2) * 64 + 64,
                                   (h // 2) * NQ:(h // 2 + 1) * NQ]).then_inc(s_q2, 16)
                s.dma_start(out=qdup[h % 2][64:128, :],
                            in_=qT[(h % 2) * 64:(h % 2) * 64 + 64,
                                   (h // 2) * NQ:(h // 2 + 1) * NQ]).then_inc(s_q2, 16)

            s.wait_ge(s_spK, 16 * 8 * NGK)
            s.wait_ge(s_spV, 16 * 4 * NGK)
            loadkq(0)
            for h in range(P2H):
                if h + 1 < P2H:
                    if h >= 1:
                        s.wait_ge(s_tLp, NPAIR * NQB * h)
                    loadkq(h + 1)
                for qb in range(NQB):
                    hq = NQB * h + qb
                    if not (p2_stub & 4):
                        if hq % 2 == 0:
                            emit_rec_chain(s, hq, s_rrA, s_rrB, hq // 2)
                        else:
                            emit_rec_chain(s, hq, s_rgA, s_rgB, hq // 2)
                    at2_dma(hq)

        @b2.gpsimd
        def _(g):
            for i in range(2):
                g.memset(bass.AP(vt[i], 64, [[VTW, 128], [65, NTK], [1, 1]]), 1.0)
            g.wait_ge(s_c, 2)
            if p2_stub & 1024:
                g.sem_inc(s_crep, 16 * 8)
            else:
                g.dma_start(out=crep[0:1, 0:16],
                            in_=cneg[0:16, 0:1]).then_inc(s_crep, 16)
                cnt = 16
                for (src0, n, dst0) in [(0, 1, 1), (0, 2, 2), (0, 4, 4), (0, 8, 8),
                                        (0, 16, 16), (0, 32, 32), (0, 64, 64)]:
                    g.wait_ge(s_crep, cnt)
                    g.dma_start(out=crep[dst0:dst0 + n, :],
                                in_=crep[src0:src0 + n, :]).then_inc(s_crep, 16)
                    cnt += 16
            w2 = [0]

            def w2inc(ins):
                ins.then_inc(s_w2, 16)
                w2[0] += 16
                return w2[0]

            bg = []
            for hc in range(8):
                for hl in range(2):
                    h_ = 2 * hc + hl
                    bg.append(lambda g, hc=hc, hl=hl, h_=h_: w2inc(g.dma_start(
                        out=Wp_sb[hl * 64:(hl + 1) * 64, hc * D:(hc + 1) * D],
                        in_=bass.AP(Wph, h_ * D, [[16 * D, 64], [1, D]]))))
            for dc in range(8):
                bg.append(lambda g, dc=dc: w2inc(g.dma_start(
                    out=Win_sb[:, dc * M:(dc + 1) * M],
                    in_=bass.AP(Winh, dc * 128 * M, [[M, 128], [1, M]]))))
            bg.append(lambda g: w2inc(g.dma_start(
                out=Wout_sb[0:64, :], in_=bass.AP(Wouth, 0, [[D, 64], [1, D]]))))
            rep_state = []
            for i, (srch, width) in enumerate(rep_specs):
                rep_state.append({"dst": rep_dst[i], "srch": srch, "w": width,
                                  "k": 0, "cnt": 0})

            def emit_bg_one(g):
                if bg:
                    bg.pop(0)(g)
                    return True
                st = next((x for x in rep_state if x["k"] < 8), None)
                if st is None:
                    return False
                if st["k"] == 0:
                    st["cnt"] = w2inc(g.dma_start(
                        out=st["dst"][0:1, :],
                        in_=bass.AP(st["srch"], 0, [[st["w"], 1], [1, st["w"]]])))
                else:
                    rows = 1 << (st["k"] - 1)
                    g.wait_ge(s_w2, st["cnt"])
                    st["cnt"] = w2inc(g.dma_start(
                        out=st["dst"][rows:2 * rows, :],
                        in_=st["dst"][0:rows, :]))
                st["k"] += 1
                return True

            for h in range(P2H):
                if h >= 2:
                    g.wait_ge(s_tAVq, NQB * (h - 1))
                if p2_stub & 256:
                    g.sem_inc(s_v2, 16 * NVD)
                else:
                    for mo in range(NVD):
                        g.dma_start(
                            out=bass.AP(vt[h % 2], mo * 8 * 65,
                                        [[VTW, 128], [65, 8], [1, E]]),
                            in_=bass.AP(vhh, (mo * 8 * 128) * D + h * E,
                                        [[D, 128], [128 * D, 8], [1, E]]),
                        ).then_inc(s_v2, 16)
                if not (p2_stub & 512):
                    for _ in range(6):
                        emit_bg_one(g)

            if not (p2_stub & 512):
                while emit_bg_one(g):
                    pass
            else:
                g.sem_inc(s_w2, 16 * NW2)

    p2.close()

    if phases == 2:
        with nc.Block() as bf:
            @bf.gpsimd
            def _(g):
                for t in range(NTQ):
                    g.dma_start(out=outh.ap()[t * 128:(t + 1) * 128, :],
                                in_=bp_rep[:, :]).then_inc(s_out, 16)
                g.wait_ge(s_out, 16 * NTQ)
        ctx.close()
        return nc

    p3 = ExitStack()

    def sb3(name, shape, dt):
        return p3.enter_context(nc.sbuf_tensor(name, shape, dt))

    x3t = [sb3(f"x3t{i}", [128, D], F32) for i in range(2)]
    zt = sb3("zt", [128, D], F32)
    zt2 = sb3("zt2", [128, D], F32)
    x1t = [sb3(f"x1t{i}", [128, D], F32) for i in range(3)]
    ln2t = [sb3(f"ln2t{i}", [128, D], BF16) for i in range(2)]
    ln2T = [sb3(f"ln2T{i}", [128, D], BF16) for i in range(2)]
    s_sb = [sb3(f"s_sb{i}", [128, M], F32) for i in range(2)]
    hbuf = sb3("hbuf", [128, D], F32)
    g0b = sb3("g0b", [128, M], F32)
    gsb = [sb3(f"gsb{i}", [128, M], BF16) for i in range(2)]
    gT = [sb3(f"gT{i}", [64, 128], BF16) for i in range(2)]
    outt = [sb3(f"outt{i}", [128, D], F32) for i in range(2)]
    sq3 = sb3("sq3", [128, D], F32)
    sq3b = sb3("sq3b", [128, D], F32)
    st3 = [sb3(f"st3_{i}", [128, 1], F32) for i in range(6)]
    st3g = sb3("st3g", [128, 1], F32)

    pO = p3.enter_context(nc.psum_tensor("pO", [128, 1024], F32))
    pF = p3.enter_context(nc.psum_tensor("pF", [128, 1024], F32))
    pS = p3.enter_context(nc.psum_tensor("pS", [128, 512], F32))
    ptp3 = [p3.enter_context(nc.psum_tensor(f"ptp3{i}", [128, 128], BF16))
            for i in range(2)]
    ptg = p3.enter_context(nc.psum_tensor("ptg", [128, 128], BF16))

    NT3 = NTQ

    V = {}
    cv = 0
    for t in range(NT3):
        if t >= 1:
            cv += 1; V[("sS", t - 1)] = cv
        if t >= 2:
            cv += 1; V[("gsb", t - 2)] = cv
        cv += 1; V[("a", t)] = cv
        cv += 1; V[("b", t)] = cv
        cv += 1; V[("ln2", t)] = cv
        if t >= 2:
            cv += 1; V[("cF", t - 2)] = cv
            cv += 1; V[("out", t - 2)] = cv
    cv += 1; V[("sS", NT3 - 1)] = cv
    cv += 1; V[("gsb", NT3 - 2)] = cv
    cv += 1; V[("cF", NT3 - 2)] = cv
    cv += 1; V[("out", NT3 - 2)] = cv
    cv += 1; V[("gsb", NT3 - 1)] = cv
    cv += 1; V[("cF", NT3 - 1)] = cv
    cv += 1; V[("out", NT3 - 1)] = cv

    Tt = {}
    ct = 0
    for t in range(NT3):
        ct += 1; Tt[("proj", t)] = ct
        if t >= 2:
            ct += 1; Tt[("ptg", t - 2)] = ct
            ct += 1; Tt[("wout", t - 2)] = ct
        for i in range(8):
            ct += 1; Tt[("tp", t, i)] = ct
        ct += 1; Tt[("win", t)] = ct
    ct += 1; Tt[("ptg", NT3 - 2)] = ct
    ct += 1; Tt[("wout", NT3 - 2)] = ct
    ct += 1; Tt[("ptg", NT3 - 1)] = ct
    ct += 1; Tt[("wout", NT3 - 1)] = ct

    Sc = {}
    Cc = {}
    cs3 = 0
    cc3 = 0
    for t in range(NT3):
        if t >= 2:
            cc3 += 1; Cc[("gt", t - 2)] = cc3
        cs3 += 1; Sc[("a", t)] = cs3
        cs3 += 1; Sc[("b", t)] = cs3
        for i in range(8):
            cc3 += 1; Cc[("tp", t, i)] = cc3
        if t >= 2:
            cs3 += 1; Sc[("c", t - 2)] = cs3
    cc3 += 1; Cc[("gt", NT3 - 2)] = cc3
    cs3 += 1; Sc[("c", NT3 - 2)] = cs3
    cc3 += 1; Cc[("gt", NT3 - 1)] = cc3
    cs3 += 1; Sc[("c", NT3 - 1)] = cs3

    Gg = {}
    cg3 = 0
    for t in range(NT3):
        cg3 += 1; Gg[("hb", t)] = cg3

    with nc.Block() as b3:
        @b3.sync
        def _(s):
            for t in range(min(2, NT3)):
                s.dma_start(out=x3t[t % 2][:, :],
                            in_=xb[t * 128:(t + 1) * 128, :]).then_inc(s3_x, 16)
            for t in range(NT3):
                if t + 2 < NT3:
                    s.wait_ge(s3_v, V[("b", t)])
                    s.dma_start(out=x3t[t % 2][:, :],
                                in_=xb[(t + 2) * 128:(t + 3) * 128, :]
                                ).then_inc(s3_x, 16)
                s.wait_ge(s3_v, V[("out", t)])
                s.dma_start(out=outh.ap()[t * 128:(t + 1) * 128, :],
                            in_=outt[t % 2][:, :]).then_inc(s_out, 16)
            s.wait_ge(s_out, 16 * NT3)

        @b3.tensor
        def _(tn):
            tn.wait_ge(s_w2, 16 * NW2)
            tn.wait_ge(s_at2, 16 * H * NQB)

            def emit_pw(tt):
                tn.wait_ge(s3_v, V[("gsb", tt)])
                if tt >= 1:
                    tn.wait_ge(s3_c, Cc[("gt", tt - 1)])
                tn.transpose(ptg[0:64, 0:128], gsb[tt % 2][:, :],
                             identb[:, :]).then_inc(s3_t, 1)
                tn.wait_ge(s3_c, Cc[("gt", tt)])
                if tt >= 1:
                    tn.wait_ge(s3_v, V[("cF", tt - 1)])
                tn.matmul(pF[:, 0:512], gT[tt % 2][:, :],
                          Wout_sb[0:64, 0:512], start=True, stop=True)
                tn.matmul(pF[:, 512:1024], gT[tt % 2][:, :],
                          Wout_sb[0:64, 512:1024],
                          start=True, stop=True).then_inc(s3_t, 1)

            for t in range(NT3):
                qb, tq = t // 4, t % 4
                if t >= 1:
                    tn.wait_ge(s3_v, V[("a", t - 1)])
                for hc in range(8):
                    lhs = attc[:, hc * NQ + qb * 512 + tq * 128:
                               hc * NQ + qb * 512 + tq * 128 + 128]
                    tn.matmul(pO[:, 0:512], lhs,
                              Wp_sb[:, hc * D:hc * D + 512],
                              start=(hc == 0), stop=(hc == 7))
                    ins = tn.matmul(pO[:, 512:1024], lhs,
                                    Wp_sb[:, hc * D + 512:hc * D + 1024],
                                    start=(hc == 0), stop=(hc == 7))
                ins.then_inc(s3_t, 1)
                if t >= 2:
                    emit_pw(t - 2)
                tn.wait_ge(s3_v, V[("ln2", t)])
                for i in range(8):
                    gi = 8 * t + i
                    if gi >= 2:
                        tt_, ii_ = (t, i - 2) if i >= 2 else (t - 1, i + 6)
                        tn.wait_ge(s3_c, Cc[("tp", tt_, ii_)])
                    tn.transpose(ptp3[i % 2][:, 0:128],
                                 ln2t[t % 2][:, i * 128:(i + 1) * 128],
                                 identb[:, :]).then_inc(s3_t, 1)
                tn.wait_ge(s3_c, Cc[("tp", t, 7)])
                if t >= 1:
                    tn.wait_ge(s3_v, V[("sS", t - 1)])
                for dc in range(8):
                    ins = tn.matmul(pS[:, 0:64],
                                    ln2T[t % 2][:, dc * 128:(dc + 1) * 128],
                                    Win_sb[:, dc * M:(dc + 1) * M],
                                    start=(dc == 0), stop=(dc == 7))
                ins.then_inc(s3_t, 1)
            emit_pw(NT3 - 2)
            emit_pw(NT3 - 1)

        @b3.vector
        def _(v):
            def ln_stats(src):
                v.reduce_sum(st3[4][:, :], src[:, :], axis=AX)
                v.scalar_tensor_tensor(sq3[:, :], src[:, :], 0.0, src[:, :],
                                       ALU.add, ALU.mult,
                                       accum_out=st3[5][:, :])
                v.tensor_scalar_mul(st3[1][:, :], st3[4][:, :], 1.0 / D)
                v.tensor_scalar_mul(st3[5][:, :], st3[5][:, :], 1.0 / D)
                v.tensor_mul(st3[4][:, :], st3[1][:, :], st3[1][:, :])
                v.tensor_sub(st3[5][:, :], st3[5][:, :], st3[4][:, :])
                return v.tensor_scalar_add(st3[5][:, :], st3[5][:, :], EPS)

            def emit_sS(tt):
                v.wait_ge(s3_t, Tt[("win", tt)])
                if tt >= 2:
                    v.wait_ge(s3_g, Gg[("hb", tt - 2)])
                v.tensor_add(s_sb[tt % 2][:, :], pS[:, 0:64],
                             bin_rep[:, :]).then_inc(s3_v, 1)

            def emit_R(tt):
                v.wait_ge(s3_g, Gg[("hb", tt)])
                if tt >= 2:
                    v.wait_ge(s3_t, Tt[("ptg", tt - 2)])
                v.scalar_tensor_tensor(hbuf[:, :], hbuf[:, :], 0.0,
                                       w2_rep[:, :], ALU.max, ALU.mult)
                v.reduce_sum(g0b[:, :],
                             bass.AP(hbuf, 0,
                                     [[D, 128], [INNER, M], [1, INNER]]),
                             axis=AX)
                v.tensor_copy(st3g[:, :], b2_rep[:, 0:1])
                v.tensor_add(gsb[tt % 2][:, :], g0b[:, :],
                             b2_rep[:, :]).then_inc(s3_v, 1)

            def emit_A(tt):
                v.wait_ge(s3_t, Tt[("proj", tt)])
                v.tensor_add(zt[:, :], pO[:, :], bp_rep[:, :])
                ln_stats(zt).then_inc(s3_v, 1)
                v.tensor_scalar_sub(sq3[:, :], zt[:, :], st3[1][:, :])
                v.wait_ge(s3_s, Sc[("a", tt)])
                v.wait_ge(s3_x, 16 * (tt + 1))
                v.tensor_scalar_mul(sq3[:, :], sq3[:, :], st3[2][:, :])
                v.tensor_add(x1t[tt % 3][:, :], x3t[tt % 2][:, :], sq3[:, :])
                ln_stats(x1t[tt % 3]).then_inc(s3_v, 1)
                v.tensor_scalar_sub(sq3[:, :], x1t[tt % 3][:, :], st3[1][:, :])
                v.wait_ge(s3_s, Sc[("b", tt)])
                if tt >= 2:
                    v.wait_ge(s3_t, Tt[("tp", tt - 2, 7)])
                v.tensor_scalar_mul(ln2t[tt % 2][:, :],
                                    sq3[:, :],
                                    st3[2][:, :]).then_inc(s3_v, 1)

            def emit_F(tt):
                v.wait_ge(s3_t, Tt[("wout", tt)])
                v.tensor_add(zt2[:, :], pF[:, :], bout_rep[:, :])
                ln_stats(zt2).then_inc(s3_v, 1)
                v.tensor_scalar_sub(sq3b[:, :], zt2[:, :], st3[1][:, :])
                v.wait_ge(s3_s, Sc[("c", tt)])
                if tt >= 2:
                    v.wait_ge(s_out, 16 * (tt - 1))
                v.tensor_scalar_mul(sq3b[:, :], sq3b[:, :], st3[2][:, :])
                v.tensor_add(outt[tt % 2][:, :], x1t[tt % 3][:, :],
                             sq3b[:, :]).then_inc(s3_v, 1)

            v.wait_ge(s_w2, 16 * NW2)
            emit_A(0)
            for t in range(1, NT3):
                emit_sS(t - 1)
                if t >= 2:
                    emit_R(t - 2)
                emit_A(t)
                if t >= 2:
                    emit_F(t - 2)
            emit_sS(NT3 - 1)
            emit_R(NT3 - 2)
            emit_F(NT3 - 2)
            emit_R(NT3 - 1)
            emit_F(NT3 - 1)

        @b3.scalar
        def _(sc):
            def rsqrt(v_need):
                sc.wait_ge(s3_v, v_need)
                sc.activation(out=st3[0][:, :], in_=st3[5][:, :], func=AF.Ln)
                sc.activation(out=st3[2][:, :], in_=st3[0][:, :],
                              func=AF.Exp, scale=-0.5).then_inc(s3_s, 1)

            def emit_gt(tt):
                sc.wait_ge(s3_t, Tt[("ptg", tt)])
                if tt >= 2:
                    sc.wait_ge(s3_t, Tt[("wout", tt - 2)])
                sc.activation(out=gT[tt % 2][:, :], in_=ptg[0:64, 0:128],
                              func=AF.Copy).then_inc(s3_c, 1)

            for t in range(NT3):
                if t >= 2:
                    emit_gt(t - 2)
                rsqrt(V[("a", t)])
                rsqrt(V[("b", t)])
                for i in range(8):
                    sc.wait_ge(s3_t, Tt[("tp", t, i)])
                    if t >= 2 and i == 0:
                        sc.wait_ge(s3_t, Tt[("win", t - 2)])
                    sc.activation(out=ln2T[t % 2][:, i * 128:(i + 1) * 128],
                                  in_=ptp3[i % 2][:, 0:128],
                                  func=AF.Copy).then_inc(s3_c, 1)
                if t >= 2:
                    rsqrt(V[("cF", t - 2)])
            emit_gt(NT3 - 2)
            rsqrt(V[("cF", NT3 - 2)])
            emit_gt(NT3 - 1)
            rsqrt(V[("cF", NT3 - 1)])

        @b3.gpsimd
        def _(g):
            g.wait_ge(s_w2, 16 * NW2)
            for t in range(NT3):
                g.wait_ge(s3_v, V[("sS", t)])
                if t >= 1:
                    g.wait_ge(s3_v, V[("gsb", t - 1)])
                g.tensor_tensor(hbuf[:, :],
                                bass.AP(s_sb[t % 2], 0,
                                        [[M, 128], [1, M], [0, INNER]]),
                                w1_rep[:, :], ALU.mult)
                g.tensor_add(hbuf[:, :], hbuf[:, :],
                             b1_rep[:, :]).then_inc(s3_g, 1)
            g.wait_ge(s_out, 16 * NT3)
            if debug:
                nd = 0
                for nm, src in (("d_qT", qT[:, 0:512]),
                                ("d_crep", crep[:, :]),
                                ("d_attc", attc[:, 0:512])):
                    g.dma_start(out=x3t[0][:, 0:src.shape[-1]], in_=src).then_inc(s_out, 16)
                    nd += 1
                    g.wait_ge(s_out, 16 * (NT3 + nd))
                    g.dma_start(out=dbgh[nm].ap()[0:src.shape[0], 0:src.shape[-1]],
                                in_=x3t[0][0:src.shape[0], 0:src.shape[-1]]).then_inc(s_out, 16)
                    nd += 1
                    g.wait_ge(s_out, 16 * (NT3 + nd))
                g.dma_start(out=dbgh["d_kT"].ap()[:, 0:512],
                            in_=kT[0:128, 0:512]).then_inc(s_out, 16)
                nd += 1
                g.wait_ge(s_out, 16 * (NT3 + nd))
                g.dma_start(out=dbgh["d_v"].ap()[:, 0:512],
                            in_=vh[0:128, 0:512]).then_inc(s_out, 16)
                nd += 1
                g.wait_ge(s_out, 16 * (NT3 + nd))
                g.dma_start(out=dbgh["d_kmax"].ap()[0:16, 0:1],
                            in_=kmax2[0:16, 0:1],
                            allow_slow_non_contiguous=True).then_inc(s_out, 16)
                nd += 1
                g.dma_start(out=dbgh["d_kmax"].ap()[0:16, 1:2],
                            in_=qmax2[0:16, 0:1],
                            allow_slow_non_contiguous=True).then_inc(s_out, 16)
                nd += 1
                g.dma_start(out=dbgh["d_cq"].ap()[:, :],
                            in_=dbg_cq[:, :]).then_inc(s_out, 16)
                nd += 1
                g.dma_start(out=dbgh["d_ck"].ap()[:, :],
                            in_=dbg_ck[:, :]).then_inc(s_out, 16)
                nd += 1
                g.dma_start(out=dbgh["d_q"].ap()[:, :],
                            in_=dbg_q[:, :]).then_inc(s_out, 16)
                nd += 1
                g.wait_ge(s_out, 16 * (NT3 + nd))

    p3.close()
    ctx.close()
    return nc


def kernel(L0_attn_query, L0_attn_Wq, L0_attn_Wk, L0_attn_Wv, L0_attn_Wp,
           L0_attn_bp, L0_ffn_Win, L0_ffn_bin_, L0_ffn_W1, L0_ffn_b1,
           L0_ffn_W2, L0_ffn_b2, L0_ffn_Wout, L0_ffn_bout):
    x = np.ascontiguousarray(np.asarray(L0_attn_query, dtype=np.float32))
    assert x.shape == (2, 8192, 1024)
    nc = build_nc(NK=8192, NQ=2048)
    weights = {
        "Wq": np.ascontiguousarray(np.asarray(L0_attn_Wq, np.float32)),
        "Wk": np.ascontiguousarray(np.asarray(L0_attn_Wk, np.float32)),
        "Wv": np.ascontiguousarray(np.asarray(L0_attn_Wv, np.float32)),
        "Wp": np.ascontiguousarray(np.asarray(L0_attn_Wp, np.float32)),
        "bp": np.ascontiguousarray(np.asarray(L0_attn_bp, np.float32)),
        "Win": np.ascontiguousarray(np.asarray(L0_ffn_Win, np.float32)),
        "binp": np.ascontiguousarray(np.asarray(L0_ffn_bin_, np.float32)),
        "W1": np.ascontiguousarray(np.asarray(L0_ffn_W1, np.float32)),
        "b1": np.ascontiguousarray(np.asarray(L0_ffn_b1, np.float32)),
        "W2": np.ascontiguousarray(np.asarray(L0_ffn_W2, np.float32)),
        "b2": np.ascontiguousarray(np.asarray(L0_ffn_b2, np.float32)),
        "Wout": np.ascontiguousarray(np.asarray(L0_ffn_Wout, np.float32)),
        "bout": np.ascontiguousarray(np.asarray(L0_ffn_bout, np.float32)),
    }
    in_maps = []
    for c in range(8):
        b, qq = c // 4, c % 4
        xr = np.roll(x[b], -qq * 2048, axis=0)
        m = {"xb": np.ascontiguousarray(xr)}
        m.update(weights)
        in_maps.append(m)
    res = run_bass_kernel_spmd(nc, in_maps, core_ids=list(range(8)))
    out = np.empty((2, 8192, 1024), np.float32)
    for c in range(8):
        b, qq = c // 4, c % 4
        out[b, qq * 2048:(qq + 1) * 2048] = res.results[c]["out"]
    return out


# version 17 (final)
# speedup vs baseline: 1.1560x; 1.0239x over previous
import numpy as np
from contextlib import ExitStack

import concourse.bass as bass
import concourse.mybir as mybir
from concourse.bass_utils import run_bass_kernel_spmd

F32 = mybir.dt.float32
BF16 = mybir.dt.bfloat16
AX = mybir.AxisListType.X
ALU = mybir.AluOpType
AF = mybir.ActivationFunctionType

D = 1024
H = 16
E = 64
M = 64
INNER = 16
EPS = 1e-5
SCALE = 0.125


def build_nc(NK=8192, NQ=2048, debug=False, phases=3, p2_heads=None, p2_stub=0):
    NTK = NK // 128
    NGK = NK // 512
    NTQ = NQ // 128
    NGQ = max(1, NQ // 512)
    NQB = max(1, NQ // 512)
    NG = NTK // 2
    NPAIR = NTK // 2
    NVD = max(1, NTK // 8)
    P2H = H if p2_heads is None else p2_heads

    nc = bass.Bass("TRN2", target_bir_lowering=False,
                   detect_race_conditions=False)

    xbh = nc.dram_tensor("xb", [NK, D], F32, kind="ExternalInput")
    Wqh = nc.dram_tensor("Wq", [H, D, E], F32, kind="ExternalInput")
    Wkh = nc.dram_tensor("Wk", [H, D, E], F32, kind="ExternalInput")
    Wvh = nc.dram_tensor("Wv", [H, D, E], F32, kind="ExternalInput")
    Wph = nc.dram_tensor("Wp", [D, D], F32, kind="ExternalInput")
    bph = nc.dram_tensor("bp", [D], F32, kind="ExternalInput")
    Winh = nc.dram_tensor("Win", [D, M], F32, kind="ExternalInput")
    binh = nc.dram_tensor("binp", [M], F32, kind="ExternalInput")
    W1h = nc.dram_tensor("W1", [M, INNER], F32, kind="ExternalInput")
    b1h = nc.dram_tensor("b1", [M, INNER], F32, kind="ExternalInput")
    W2h = nc.dram_tensor("W2", [M, INNER], F32, kind="ExternalInput")
    b2h = nc.dram_tensor("b2", [M], F32, kind="ExternalInput")
    Wouth = nc.dram_tensor("Wout", [M, D], F32, kind="ExternalInput")
    bouth = nc.dram_tensor("bout", [D], F32, kind="ExternalInput")
    outh = nc.dram_tensor("out", [NQ, D], F32, kind="ExternalOutput")

    kTh = nc.dram_tensor("kT_scr", [D, NK], BF16, kind="Internal")
    vhh = nc.dram_tensor("v_scr", [NK, D], BF16, kind="Internal")
    dbgh = {}
    if debug:
        for nm, shp in (("d_kT", [128, 512]), ("d_v", [128, 512]),
                        ("d_qT", [128, 512]), ("d_crep", [128, 16]),
                        ("d_attc", [128, 512]), ("d_kmax", [16, 2]),
                        ("d_cq", [16, 4 * 512]), ("d_ck", [16, 512]),
                        ("d_q", [128, 512])):
            dbgh[nm] = nc.dram_tensor(nm, shp, F32, kind="ExternalOutput")

    xb = xbh.ap()
    kT = kTh.ap()
    vh = vhh.ap()

    ctx = ExitStack()

    def sb(name, shape, dt):
        return ctx.enter_context(nc.sbuf_tensor(name, shape, dt))

    def sem(name):
        return ctx.enter_context(nc.semaphore(name))

    identb = sb("identb", [128, 128], BF16)
    esel = sb("esel", [128, 8 * 16], BF16)
    if debug:
        dbg_cq = sb("dbg_cq", [16, 4 * 512], F32)
        dbg_ck = sb("dbg_ck", [16, 512], F32)
        dbg_q = sb("dbg_q", [128, 512], F32)
    qT = sb("qT", [128, 8 * NQ], BF16)
    kmax2 = sb("kmax2", [16, 1], F32)
    qmax2 = sb("qmax2", [16, 1], F32)
    kgmax = sb("kgmax", [16, NGK], F32)
    qgmax = sb("qgmax", [16, max(1, NGQ)], F32)
    crep = sb("crep", [128, 16], F32)

    s_w = sem("s_w")
    s_x = sem("s_x"); s_vS = sem("s_vS"); s_sS = sem("s_sS")
    s_vLN = sem("s_vLN"); s_tTP = sem("s_tTP"); s_vTP = sem("s_vTP")
    s_tK = sem("s_tK"); s_vK = sem("s_vK"); s_tV = sem("s_tV"); s_vV = sem("s_vV")
    s_tQ = sem("s_tQ"); s_vQ = sem("s_vQ")
    s_vSQ = sem("s_vSQ"); s_tCK = sem("s_tCK"); s_vCK = sem("s_vCK")
    s_vQS = sem("s_vQS"); s_tCQ = sem("s_tCQ"); s_vCQ = sem("s_vCQ")
    s_spK = sem("s_spK"); s_spV = sem("s_spV"); s_tG = sem("s_tG")
    s_c = sem("s_c"); s_cs = sem("s_cs"); s_crep = sem("s_crep")
    s_k = sem("s_k"); s_v2 = sem("s_v2"); s_q2 = sem("s_q2")
    s_tLp = sem("s_tLp"); s_tL = sem("s_tL")
    s_exp = sem("s_exp"); s_tAVg = sem("s_tAVg"); s_tAVq = sem("s_tAVq")
    s_vrec = sem("s_vrec"); s_vrec0 = sem("s_vrec0")
    s_rrA = sem("s_rrA"); s_rrB = sem("s_rrB")
    s_rgA = sem("s_rgA"); s_rgB = sem("s_rgB"); s_at2 = sem("s_at2")
    s_vAT = sem("s_vAT")
    s_w2 = sem("s_w2")
    s3_x = sem("s3_x"); s3_t = sem("s3_t"); s3_v = sem("s3_v"); s3_s = sem("s3_s")
    s3_c = sem("s3_c"); s3_g = sem("s3_g")
    s_out = sem("s_out")

    p1 = ExitStack()

    def sb1(name, shape, dt):
        return p1.enter_context(nc.sbuf_tensor(name, shape, dt))

    def ps1(name, shape, dt):
        return p1.enter_context(nc.psum_tensor(name, shape, dt))

    Wq_sb = sb1("Wq_sb", [128, 8 * D], BF16)
    Wk_sb = sb1("Wk_sb", [128, 8 * D], BF16)
    Wv_sb = sb1("Wv_sb", [128, 8 * D], BF16)
    x0t = [sb1(f"x0t{i}", [128, D], F32) for i in range(3)]
    sqv = sb1("sqv", [128, D], F32)
    ln1t = [sb1(f"ln1t{i}", [128, D], BF16) for i in range(2)]
    ln1Tg = [sb1(f"ln1Tg{i}", [128, 8 * 512], BF16) for i in range(2)]
    kTsp = [sb1(f"kTsp{i}", [128, 8 * 512], BF16) for i in range(2)]
    vsp = [sb1(f"vsp{i}", [128, 4 * D], BF16) for i in range(2)]
    ksq = [sb1(f"ksq{i}", [128, 8 * 512], BF16) for i in range(2)]
    qsq = [sb1(f"qsq{i}", [128, 8 * 512], BF16) for i in range(2)]
    st_s1 = sb1("st_s1", [128, 1], F32)
    st_s2 = sb1("st_s2", [128, 1], F32)
    st_m = sb1("st_m", [128, 1], F32)
    st_t = sb1("st_t", [128, 1], F32)
    st_rv = sb1("st_rv", [128, 1], F32)
    st_rs = sb1("st_rs", [128, 1], F32)
    cmx = sb1("cmx", [16, 1], F32)

    ptp = [ps1(f"ptp{i}", [128, 1024], BF16) for i in range(2)]
    kps = [ps1(f"kps{i}", [128, 512], F32) for i in range(2)]
    vps = ps1("vps", [128, 1024], F32)
    cps = ps1("cps", [128, 512], F32)
    cpsq = ps1("cpsq", [128, 512], F32)

    NW1 = 24

    with nc.Block() as b0:
        @b0.gpsimd
        def _(g):
            for dc in range(8):
                for (Wh, Wsb) in ((Wqh, Wq_sb), (Wkh, Wk_sb), (Wvh, Wv_sb)):
                    g.dma_start(
                        out=Wsb[:, dc * D:(dc + 1) * D],
                        in_=bass.AP(Wh, dc * 128 * E, [[E, 128], [D * E, H], [1, E]]),
                    ).then_inc(s_w, 16)
            g.memset(identb[:, :], 1.0)
            g.affine_select(out=identb[:, :], in_=identb[:, :],
                            compare_op=ALU.is_equal, fill=0.0, base=0,
                            pattern=[[-1, 128]], channel_multiplier=1)

        @b0.vector
        def _(v):
            v.memset(esel[:, :], 0.0)
            for dc in range(8):
                v.memset(esel[0:64, dc * 16 + 2 * dc:dc * 16 + 2 * dc + 1], 1.0)
                v.memset(esel[64:128, dc * 16 + 2 * dc + 1:dc * 16 + 2 * dc + 2], 1.0)
            v.memset(kmax2[:, :], 0.0)
            v.memset(qmax2[:, :], 0.0)

    with nc.Block() as b1:
        @b1.sync
        def _(s):
            def spill(gp):
                s.wait_ge(s_vK, 8 * gp + 8)
                for hc in range(8):
                    s.dma_start(
                        out=kT[hc * 128:(hc + 1) * 128, gp * 512:(gp + 1) * 512],
                        in_=kTsp[gp % 2][:, hc * 512:(hc + 1) * 512],
                    ).then_inc(s_spK, 16)
                s.wait_ge(s_vV, 4 * gp + 4)
                for mt in range(4):
                    s.dma_start(
                        out=vh[(4 * gp + mt) * 128:(4 * gp + mt + 1) * 128, :],
                        in_=vsp[gp % 2][:, mt * D:(mt + 1) * D],
                    ).then_inc(s_spV, 16)

            for g in range(NGK):
                spill(g)

        @b1.vector
        def _(v):
            for g in range(NGK):
                for tq in range(4):
                    t = 4 * g + tq
                    v.wait_ge(s_x, 16 * (t + 1))
                    x = x0t[t % 3]
                    v.reduce_sum(st_s1[:, :], x[:, :], axis=AX)
                    v.scalar_tensor_tensor(sqv[:, :], x[:, :], 0.0, x[:, :],
                                           ALU.add, ALU.mult,
                                           accum_out=st_s2[:, :])
                    v.tensor_scalar_mul(st_m[:, :], st_s1[:, :], 1.0 / D)
                    v.tensor_scalar_mul(st_t[:, :], st_s2[:, :], 1.0 / D)
                    v.tensor_mul(st_s1[:, :], st_m[:, :], st_m[:, :])
                    v.tensor_sub(st_t[:, :], st_t[:, :], st_s1[:, :])
                    v.tensor_scalar_add(st_t[:, :], st_t[:, :],
                                        EPS).then_inc(s_vS, 1)
                    v.tensor_scalar_sub(sqv[:, :], x[:, :], st_m[:, :])
                    v.wait_ge(s_sS, t + 1)
                    if t >= 2:
                        v.wait_ge(s_tTP, 8 * (t - 2) + 8)
                    v.tensor_scalar_mul(ln1t[t % 2][:, :], sqv[:, :],
                                        st_rs[:, :]).then_inc(s_vLN, 1)
                if g >= 2:
                    v.wait_ge(s_spK, 16 * 8 * (g - 1))
                for hc in range(8):
                    KI = 8 * g + hc
                    v.wait_ge(s_tK, KI + 1)
                    v.tensor_copy(kTsp[g % 2][:, hc * 512:(hc + 1) * 512],
                                  kps[KI % 2][:, :]).then_inc(s_vK, 1)
                    v.tensor_mul(ksq[g % 2][:, hc * 512:(hc + 1) * 512],
                                 kTsp[g % 2][:, hc * 512:(hc + 1) * 512],
                                 kTsp[g % 2][:, hc * 512:(hc + 1) * 512]
                                 ).then_inc(s_vSQ, 1)
                if g >= 2:
                    v.wait_ge(s_tCK, g - 1)
                    if debug and g == 2:
                        v.tensor_copy(dbg_ck[:, :], cps[0:16, :])
                    v.reduce_max(kgmax[:, g - 2:g - 1], cps[0:16, :],
                                 axis=AX).then_inc(s_vCK, 1)
                if 2 <= g and g - 2 < NGQ:
                    v.wait_ge(s_tCQ, g - 1)
                    if debug:
                        v.tensor_copy(dbg_cq[:, (g - 2) * 512:(g - 1) * 512],
                                      cpsq[0:16, :])
                    v.reduce_max(qgmax[:, g - 2:g - 1], cpsq[0:16, :],
                                 axis=AX).then_inc(s_vCQ, 1)
            v.wait_ge(s_tCK, NGK - 1)
            v.reduce_max(kgmax[:, NGK - 2:NGK - 1], cps[0:16, :],
                         axis=AX).then_inc(s_vCK, 1)
            v.wait_ge(s_tCK, NGK)
            v.reduce_max(kgmax[:, NGK - 1:NGK], cps[0:16, :],
                         axis=AX).then_inc(s_vCK, 1)

        @b1.scalar
        def _(sc):
            def xload(t):
                sc.dma_start(out=x0t[t % 3][:, :],
                             in_=xb[t * 128:(t + 1) * 128, :]).then_inc(s_x, 16)

            for t in range(3):
                xload(t)
            for t in range(NTK):
                if t % 4 == 0:
                    gq = t // 4 - 1
                    if 0 <= gq < NGQ:
                        for hc in range(8):
                            QI = 8 * gq + hc
                            sc.wait_ge(s_tQ, QI + 1)
                            sc.activation(
                                out=qT[:, hc * NQ + gq * 512:hc * NQ + gq * 512 + 512],
                                in_=kps[QI % 2][:, :], func=AF.Copy,
                                scale=SCALE).then_inc(s_vQ, 1)
                            sc.activation(
                                out=qsq[gq % 2][:, hc * 512:(hc + 1) * 512],
                                in_=kps[QI % 2][:, :], func=AF.Square,
                                scale=SCALE).then_inc(s_vQS, 1)
                sc.wait_ge(s_vS, t + 1)
                sc.activation(out=st_rv[:, :], in_=st_t[:, :], func=AF.Ln)
                sc.activation(out=st_rs[:, :], in_=st_rv[:, :],
                              func=AF.Exp, scale=-0.5).then_inc(s_sS, 1)
                if t + 3 < NTK:
                    sc.wait_ge(s_vLN, t + 1)
                    xload(t + 3)
                if t % 4 == 0 and t // 4 >= 2:
                    if t // 4 - 2 < NGQ:
                        sc.wait_ge(s_tQ, 8 * (t // 4 - 1))
                    else:
                        sc.wait_ge(s_tV, 4 * (t // 4 - 1))
                for dc in range(8):
                    sc.wait_ge(s_tTP, 8 * t + dc + 1)
                    sc.activation(
                        out=ln1Tg[(t // 4) % 2][:, dc * 512 + (t % 4) * 128:
                                                dc * 512 + (t % 4) * 128 + 128],
                        in_=ptp[dc % 2][:, 0:128],
                        func=AF.Copy).then_inc(s_vTP, 1)
                if t % 4 == 3:
                    if t // 4 >= 2:
                        sc.wait_ge(s_spV, 16 * 4 * (t // 4 - 1))
                    for mt in range(4):
                        VI = 4 * (t // 4) + mt
                        sc.wait_ge(s_tV, VI + 1)
                        sc.activation(out=vsp[(t // 4) % 2][:, mt * D:(mt + 1) * D],
                                      in_=vps[:, :],
                                      func=AF.Copy).then_inc(s_vV, 1)

        @b1.tensor
        def _(tn):
            tn.wait_ge(s_w, 16 * NW1)
            for g in range(NGK):
                for tq in range(4):
                    t = 4 * g + tq
                    tn.wait_ge(s_vLN, t + 1)
                    for dc in range(8):
                        i = 8 * t + dc
                        if i >= 2:
                            tn.wait_ge(s_vTP, i - 1)
                        tn.transpose(ptp[dc % 2][:, 0:128],
                                     ln1t[t % 2][:, dc * 128:(dc + 1) * 128],
                                     identb[:, :]).then_inc(s_tTP, 1)
                tn.wait_ge(s_vTP, 8 * (4 * g + 4))
                for hc in range(8):
                    KI = 8 * g + hc
                    if KI >= 2:
                        tn.wait_ge(s_vK, KI - 1)
                    if g >= 1 and g - 1 < NGQ:
                        tn.wait_ge(s_vQS, 8 * (g - 1) + hc + 1)
                    for dc in range(8):
                        ins = tn.matmul(
                            kps[KI % 2][:, :],
                            Wk_sb[:, dc * D + hc * 128:dc * D + hc * 128 + 128],
                            ln1Tg[g % 2][:, dc * 512:(dc + 1) * 512],
                            start=(dc == 0), stop=(dc == 7))
                    ins.then_inc(s_tK, 1)
                for mt in range(4):
                    VI = 4 * g + mt
                    if VI >= 1:
                        tn.wait_ge(s_vV, VI)
                    for dc in range(8):
                        tn.matmul(vps[:, 0:512],
                                  ln1Tg[g % 2][:, dc * 512 + mt * 128:dc * 512 + mt * 128 + 128],
                                  Wv_sb[:, dc * D:dc * D + 512],
                                  start=(dc == 0), stop=(dc == 7))
                        ins = tn.matmul(
                            vps[:, 512:1024],
                            ln1Tg[g % 2][:, dc * 512 + mt * 128:dc * 512 + mt * 128 + 128],
                            Wv_sb[:, dc * D + 512:dc * D + 1024],
                            start=(dc == 0), stop=(dc == 7))
                    ins.then_inc(s_tV, 1)
                if g < NGQ:
                    tn.wait_ge(s_vK, 8 * (g + 1))
                    for hc in range(8):
                        QI = 8 * g + hc
                        if QI >= 2:
                            tn.wait_ge(s_vQS, QI - 1)
                        for dc in range(8):
                            ins = tn.matmul(
                                kps[QI % 2][:, :],
                                Wq_sb[:, dc * D + hc * 128:dc * D + hc * 128 + 128],
                                ln1Tg[g % 2][:, dc * 512:(dc + 1) * 512],
                                start=(dc == 0), stop=(dc == 7))
                        ins.then_inc(s_tQ, 1)
                def emit_ck(gp):
                    if gp >= 1:
                        tn.wait_ge(s_vCK, gp)
                    for hc in range(8):
                        tn.wait_ge(s_vSQ, 8 * gp + hc + 1)
                        ins = tn.matmul(cps[0:16, :],
                                        esel[:, hc * 16:(hc + 1) * 16],
                                        ksq[gp % 2][:, hc * 512:(hc + 1) * 512],
                                        start=(hc == 0), stop=(hc == 7))
                    ins.then_inc(s_tCK, 1)

                def emit_cq(gp):
                    if gp >= 1:
                        tn.wait_ge(s_vCQ, gp)
                    for hc in range(8):
                        tn.wait_ge(s_vQS, 8 * gp + hc + 1)
                        ins = tn.matmul(cpsq[0:16, :],
                                        esel[:, hc * 16:(hc + 1) * 16],
                                        qsq[gp % 2][:, hc * 512:(hc + 1) * 512],
                                        start=(hc == 0), stop=(hc == 7))
                    ins.then_inc(s_tCQ, 1)

                if g >= 1:
                    emit_ck(g - 1)
                if 1 <= g <= NGQ:
                    emit_cq(g - 1)
                if g == NGK - 1:
                    tn.wait_ge(s_vCK, g)
                    emit_ck(g)

    if phases == 1:
        with nc.Block() as bf:
            @bf.gpsimd
            def _(g):
                for t in range(NTQ):
                    g.dma_start(out=outh.ap()[t * 128:(t + 1) * 128, :],
                                in_=x0t[0][:, :]).then_inc(s_out, 16)
                g.wait_ge(s_out, 16 * NTQ)
                if debug:
                    nd = 0
                    g.dma_start(out=dbgh["d_kT"].ap()[:, 0:512],
                                in_=ln1t[(NTK - 1) % 2][:, 0:512]).then_inc(s_out, 16)
                    nd += 1
                    g.dma_start(out=dbgh["d_v"].ap()[:, 0:512],
                                in_=x0t[(NTK - 1) % 3][:, 0:512]).then_inc(s_out, 16)
                    nd += 1
                    g.dma_start(out=dbgh["d_qT"].ap()[:, 0:512],
                                in_=qT[0:128, 0:512]).then_inc(s_out, 16)
                    nd += 1
                    g.dma_start(out=dbgh["d_attc"].ap()[:, 0:512],
                                in_=ln1Tg[(NGK - 1) % 2][:, 0:512]).then_inc(s_out, 16)
                    nd += 1
                    g.dma_start(out=dbgh["d_crep"].ap()[0:16, 0:16],
                                in_=chist[0:16, 0:16]).then_inc(s_out, 16)
                    nd += 1
                    g.dma_start(out=dbgh["d_attc"].ap()[0:16, 0:512],
                                in_=cpsdump[0:16, 0:512]).then_inc(s_out, 16)
                    nd += 1
                    g.dma_start(out=dbgh["d_kT"].ap()[0:16, 0:20],
                                in_=chist[0:16, 20:40]).then_inc(s_out, 16)
                    nd += 1
                    g.dma_start(out=dbgh["d_crep"].ap()[0:128, 1:2],
                                in_=st_rv[0:128, 0:1],
                                allow_slow_non_contiguous=True).then_inc(s_out, 16)
                    nd += 1
                    g.dma_start(out=dbgh["d_crep"].ap()[0:128, 2:3],
                                in_=st_rs[0:128, 0:1],
                                allow_slow_non_contiguous=True).then_inc(s_out, 16)
                    nd += 1
                    g.dma_start(out=dbgh["d_crep"].ap()[0:128, 3:4],
                                in_=st_s2[0:128, 0:1],
                                allow_slow_non_contiguous=True).then_inc(s_out, 16)
                    nd += 1
                    g.dma_start(out=dbgh["d_kmax"].ap()[0:16, 0:1],
                                in_=kmax2[0:16, 0:1],
                                allow_slow_non_contiguous=True).then_inc(s_out, 16)
                    nd += 1
                    g.dma_start(out=dbgh["d_kmax"].ap()[0:16, 1:2],
                                in_=qmax2[0:16, 0:1],
                                allow_slow_non_contiguous=True).then_inc(s_out, 16)
                    nd += 1
                    g.wait_ge(s_out, 16 * (NTQ + nd))
        p1.close()
        ctx.close()
        return nc

    p1.close()

    attc = sb("attc", [128, 8 * NQ], BF16)
    Wp_sb = sb("Wp_sb", [128, 8 * D], BF16)
    Win_sb = sb("Win_sb", [128, 8 * M], BF16)
    Wout_sb = sb("Wout_sb", [64, D], BF16)
    bp_rep = sb("bp_rep", [128, D], F32)
    bout_rep = sb("bout_rep", [128, D], F32)
    w1_rep = sb("w1_rep", [128, D], F32)
    b1_rep = sb("b1_rep", [128, D], F32)
    w2_rep = sb("w2_rep", [128, D], F32)
    bin_rep = sb("bin_rep", [128, M], F32)
    b2_rep = sb("b2_rep", [128, M], F32)

    p2 = ExitStack()

    def sb2(name, shape, dt):
        return p2.enter_context(nc.sbuf_tensor(name, shape, dt))

    kdup = [sb2(f"kdup{i}", [128, NK], BF16) for i in range(2)]
    VTW = NTK * 65
    vt = [sb2(f"vt{i}", [128, VTW], BF16) for i in range(2)]
    qdup = [sb2(f"qdup{i}", [128, NQ], BF16) for i in range(2)]
    ex = [sb2(f"ex{i}", [128, 1024], BF16) for i in range(3)]
    recr = [sb2(f"recr{i}", [128, 512], F32) for i in range(2)]
    rtmp = sb2("rtmp", [1, 512], F32)
    atst = [sb2(f"atst{i}", [64, 512], BF16) for i in range(2)]
    cneg = sb2("cneg", [16, 1], F32)

    pL = p2.enter_context(nc.psum_tensor("pL", [128, 3072], F32))
    pA = [p2.enter_context(nc.psum_tensor(f"pA{i}", [128, 512], F32))
          for i in range(2)]

    DOUBLES = [(64, 1, 0, True), (0, 1, 1, False), (0, 2, 2, False),
               (0, 4, 4, False), (0, 8, 8, False), (0, 16, 16, False)]
    NRC = len(DOUBLES)

    def emit_rec_chain(eng, hq, semA, semB, idx):
        eng.wait_ge(s_vrec, hq + 1)
        r = recr[hq % 2]
        for (sA, base) in ((semA, 0), (semB, 32)):
            cnt = 16 * NRC * idx
            for (src0, n, dst0, abs_src) in DOUBLES:
                s0 = src0 if abs_src else base + src0
                if cnt:
                    eng.wait_ge(sA, cnt)
                eng.dma_start(out=r[base + dst0:base + dst0 + n, :],
                              in_=r[s0:s0 + n, :]).then_inc(sA, 16)
                cnt += 16

    def av_inc_of(G):
        hq_, g_ = G // NG, G % NG
        if g_ == NG - 1:
            return ("q", hq_ + 1)
        return ("g", (NG - 1) * hq_ + g_ + 1)

    NBG_PLAIN = 16 + 8 + 1
    rep_specs = [(bph, D), (bouth, D), (W1h, D), (b1h, D), (W2h, D),
                 (binh, M), (b2h, M)]
    rep_dst = [None] * 7
    rep_dst[0] = bp_rep; rep_dst[1] = bout_rep; rep_dst[2] = w1_rep
    rep_dst[3] = b1_rep; rep_dst[4] = w2_rep; rep_dst[5] = bin_rep
    rep_dst[6] = b2_rep
    NW2 = NBG_PLAIN + 7 * 8

    with nc.Block() as b2:
        @b2.vector
        def _(v):
            v.memset(crep[:, :], 0.0)
            v.reduce_max(kmax2[:, :], kgmax[:, :], axis=AX)
            v.reduce_max(qmax2[:, :], qgmax[:, :], axis=AX)
            v.memset(crep[:, :], 0.0)
            v.tensor_mul(cneg[:, :], kmax2[:, :], qmax2[:, :]).then_inc(s_c, 1)
            v.wait_ge(s_cs, 1)
            v.tensor_scalar(cneg[:, :], cneg[:, :], -1.0, -1.0,
                            ALU.mult, ALU.add).then_inc(s_c, 1)
            for h in range(P2H):
                for qb in range(NQB):
                    hq = NQB * h + qb
                    v.wait_ge(s_tAVq, hq + 1)
                    v.tensor_copy(recr[hq % 2][64:65, :],
                                  pA[hq % 2][64:65, :]).then_inc(s_vrec0, 1)
                    if hq % 2 == 0:
                        v.wait_ge(s_rrA, 16 * NRC * (hq // 2 + 1))
                        v.wait_ge(s_rrB, 16 * NRC * (hq // 2 + 1))
                    else:
                        v.wait_ge(s_rgA, 16 * NRC * (hq // 2 + 1))
                        v.wait_ge(s_rgB, 16 * NRC * (hq // 2 + 1))
                    if hq >= 2:
                        v.wait_ge(s_at2, 16 * (hq - 1))
                    v.tensor_tensor(
                        atst[hq % 2][:, :],
                        pA[hq % 2][0:64, :], recr[hq % 2][0:64, :],
                        ALU.mult).then_inc(s_vAT, 1)

        @b2.scalar
        def _(sc):
            def emit_recr(hq_):
                sc.wait_ge(s_vrec0, hq_ + 1)
                sc.activation(out=rtmp[0:1, :],
                              in_=recr[hq_ % 2][64:65, :], func=AF.Ln)
                sc.activation(out=recr[hq_ % 2][64:65, :],
                              in_=rtmp[0:1, :], func=AF.Exp,
                              scale=-1.0).then_inc(s_vrec, 1)

            sc.wait_ge(s_c, 1)
            sc.activation(out=cneg[:, :], in_=cneg[:, :], func=AF.Ln)
            sc.activation(out=cneg[:, :], in_=cneg[:, :],
                          func=AF.Exp, scale=0.5).then_inc(s_cs, 1)
            sc.wait_ge(s_crep, 16 * 8)
            for h in range(P2H):
                for qb in range(NQB):
                    hq = NQB * h + qb
                    for g in range(NG):
                        G = NG * hq + g
                        sc.wait_ge(s_tLp, NG * hq + g + 1)
                        if G >= 3:
                            sm, val = av_inc_of(G - 3)
                            sc.wait_ge(s_tAVg if sm == "g" else s_tAVq, val)
                        sc.activation(
                            out=ex[G % 3][:, :],
                            in_=pL[:, (g % 3) * 1024:(g % 3) * 1024 + 1024],
                            func=AF.Exp,
                            bias=crep[:, h:h + 1]).then_inc(s_exp, 1)
                        if g == 2 and hq >= 1:
                            emit_recr(hq - 1)
            emit_recr(P2H * NQB - 1)

        @b2.tensor
        def _(tn):
            for h in range(P2H):
                for qb in range(NQB):
                    hq = NQB * h + qb
                    if qb == 0:
                        tn.wait_ge(s_k, 32 * (h + 1))
                        tn.wait_ge(s_q2, 32 * (h + 1))

                    def emit_av_group(g, hq=hq, h=h, qb=qb):
                        G = NG * hq + g
                        last_grp = (g == NG - 1)
                        if g == 0 and qb == 0:
                            tn.wait_ge(s_v2, 16 * NVD * (h + 1))
                        tn.wait_ge(s_exp, G + 1)
                        if g == 0 and hq >= 2:
                            tn.wait_ge(s_vAT, hq - 1)
                        for i, t in enumerate((2 * g, 2 * g + 1)):
                            ins = tn.matmul(pA[hq % 2][0:65, :],
                                            vt[h % 2][:, t * 65:t * 65 + 65],
                                            ex[G % 3][:, i * 512:(i + 1) * 512],
                                            start=(t == 0), stop=(t == NTK - 1))
                        if last_grp:
                            ins.then_inc(s_tAVq, 1)
                        else:
                            ins.then_inc(s_tAVg, 1)

                    for j in range(NG):
                        Gj = NG * hq + j
                        if hq == 0 and j < 3:
                            pass
                        elif j in (0, 1):
                            tn.wait_ge(s_exp, Gj - 1)
                        elif j == 2:
                            tn.wait_ge(s_exp, Gj - 4)
                        else:
                            tn.wait_ge(s_exp, Gj - 2)
                        ta, tb = 2 * j, 2 * j + 1
                        tn.matmul(pL[:, (ta % 6) * 512:(ta % 6) * 512 + 512],
                                  kdup[h % 2][0:64, ta * 128:(ta + 1) * 128],
                                  qdup[h % 2][0:64, qb * 512:(qb + 1) * 512],
                                  start=True, stop=True, tile_position=(0, 0))
                        ins = tn.matmul(
                            pL[:, (tb % 6) * 512:(tb % 6) * 512 + 512],
                            kdup[h % 2][64:128, tb * 128:(tb + 1) * 128],
                            qdup[h % 2][64:128, qb * 512:(qb + 1) * 512],
                            start=True, stop=True, tile_position=(64, 0))
                        ins.then_inc(s_tLp, 1)
                        if j >= 2:
                            emit_av_group(j - 2)
                    emit_av_group(NG - 2)
                    emit_av_group(NG - 1)

        @b2.sync
        def _(s):
            def at2_dma(hq):
                h_, qb_ = hq // NQB, hq % NQB
                s.wait_ge(s_vAT, hq + 1)
                if p2_stub & 8:
                    s.sem_inc(s_at2, 16)
                    return
                s.dma_start(
                    out=attc[(h_ % 2) * 64:(h_ % 2) * 64 + 64,
                             (h_ // 2) * NQ + qb_ * 512:(h_ // 2) * NQ + qb_ * 512 + 512],
                    in_=atst[hq % 2][:, :]).then_inc(s_at2, 16)

            def loadkq(h):
                if p2_stub & 128:
                    s.sem_inc(s_k, 32)
                    s.sem_inc(s_q2, 32)
                    return
                s.dma_start(out=kdup[h % 2][0:64, :],
                            in_=kT[h * 64:h * 64 + 64, :]).then_inc(s_k, 16)
                s.dma_start(out=kdup[h % 2][64:128, :],
                            in_=kT[h * 64:h * 64 + 64, :]).then_inc(s_k, 16)
                s.dma_start(out=qdup[h % 2][0:64, :],
                            in_=qT[(h % 2) * 64:(h % 2) * 64 + 64,
                                   (h // 2) * NQ:(h // 2 + 1) * NQ]).then_inc(s_q2, 16)
                s.dma_start(out=qdup[h % 2][64:128, :],
                            in_=qT[(h % 2) * 64:(h % 2) * 64 + 64,
                                   (h // 2) * NQ:(h // 2 + 1) * NQ]).then_inc(s_q2, 16)

            s.wait_ge(s_spK, 16 * 8 * NGK)
            s.wait_ge(s_spV, 16 * 4 * NGK)
            loadkq(0)
            for h in range(P2H):
                if h + 1 < P2H:
                    if h >= 1:
                        s.wait_ge(s_tLp, NPAIR * NQB * h)
                    loadkq(h + 1)
                for qb in range(NQB):
                    hq = NQB * h + qb
                    if not (p2_stub & 4):
                        if hq % 2 == 0:
                            emit_rec_chain(s, hq, s_rrA, s_rrB, hq // 2)
                        else:
                            emit_rec_chain(s, hq, s_rgA, s_rgB, hq // 2)
                    at2_dma(hq)

        @b2.gpsimd
        def _(g):
            for i in range(2):
                g.memset(bass.AP(vt[i], 64, [[VTW, 128], [65, NTK], [1, 1]]), 1.0)
            g.wait_ge(s_c, 2)
            if p2_stub & 1024:
                g.sem_inc(s_crep, 16 * 8)
            else:
                g.dma_start(out=crep[0:1, 0:16],
                            in_=cneg[0:16, 0:1]).then_inc(s_crep, 16)
                cnt = 16
                for (src0, n, dst0) in [(0, 1, 1), (0, 2, 2), (0, 4, 4), (0, 8, 8),
                                        (0, 16, 16), (0, 32, 32), (0, 64, 64)]:
                    g.wait_ge(s_crep, cnt)
                    g.dma_start(out=crep[dst0:dst0 + n, :],
                                in_=crep[src0:src0 + n, :]).then_inc(s_crep, 16)
                    cnt += 16
            w2 = [0]

            def w2inc(ins):
                ins.then_inc(s_w2, 16)
                w2[0] += 16
                return w2[0]

            bg = []
            for hc in range(8):
                for hl in range(2):
                    h_ = 2 * hc + hl
                    bg.append(lambda g, hc=hc, hl=hl, h_=h_: w2inc(g.dma_start(
                        out=Wp_sb[hl * 64:(hl + 1) * 64, hc * D:(hc + 1) * D],
                        in_=bass.AP(Wph, h_ * D, [[16 * D, 64], [1, D]]))))
            for dc in range(8):
                bg.append(lambda g, dc=dc: w2inc(g.dma_start(
                    out=Win_sb[:, dc * M:(dc + 1) * M],
                    in_=bass.AP(Winh, dc * 128 * M, [[M, 128], [1, M]]))))
            bg.append(lambda g: w2inc(g.dma_start(
                out=Wout_sb[0:64, :], in_=bass.AP(Wouth, 0, [[D, 64], [1, D]]))))
            rep_state = []
            for i, (srch, width) in enumerate(rep_specs):
                rep_state.append({"dst": rep_dst[i], "srch": srch, "w": width,
                                  "k": 0, "cnt": 0})

            def emit_bg_one(g):
                if bg:
                    bg.pop(0)(g)
                    return True
                st = next((x for x in rep_state if x["k"] < 8), None)
                if st is None:
                    return False
                if st["k"] == 0:
                    st["cnt"] = w2inc(g.dma_start(
                        out=st["dst"][0:1, :],
                        in_=bass.AP(st["srch"], 0, [[st["w"], 1], [1, st["w"]]])))
                else:
                    rows = 1 << (st["k"] - 1)
                    g.wait_ge(s_w2, st["cnt"])
                    st["cnt"] = w2inc(g.dma_start(
                        out=st["dst"][rows:2 * rows, :],
                        in_=st["dst"][0:rows, :]))
                st["k"] += 1
                return True

            for h in range(P2H):
                if h >= 2:
                    g.wait_ge(s_tAVq, NQB * (h - 1))
                if p2_stub & 256:
                    g.sem_inc(s_v2, 16 * NVD)
                else:
                    for mo in range(NVD):
                        g.dma_start(
                            out=bass.AP(vt[h % 2], mo * 8 * 65,
                                        [[VTW, 128], [65, 8], [1, E]]),
                            in_=bass.AP(vhh, (mo * 8 * 128) * D + h * E,
                                        [[D, 128], [128 * D, 8], [1, E]]),
                        ).then_inc(s_v2, 16)
                if not (p2_stub & 512):
                    for _ in range(6):
                        emit_bg_one(g)

            if not (p2_stub & 512):
                while emit_bg_one(g):
                    pass
            else:
                g.sem_inc(s_w2, 16 * NW2)

    p2.close()

    if phases == 2:
        with nc.Block() as bf:
            @bf.gpsimd
            def _(g):
                for t in range(NTQ):
                    g.dma_start(out=outh.ap()[t * 128:(t + 1) * 128, :],
                                in_=bp_rep[:, :]).then_inc(s_out, 16)
                g.wait_ge(s_out, 16 * NTQ)
        ctx.close()
        return nc

    p3 = ExitStack()

    def sb3(name, shape, dt):
        return p3.enter_context(nc.sbuf_tensor(name, shape, dt))

    x3t = [sb3(f"x3t{i}", [128, D], F32) for i in range(2)]
    zt = sb3("zt", [128, D], F32)
    zt2 = sb3("zt2", [128, D], F32)
    x1t = [sb3(f"x1t{i}", [128, D], F32) for i in range(3)]
    ln2t = [sb3(f"ln2t{i}", [128, D], BF16) for i in range(2)]
    ln2T = [sb3(f"ln2T{i}", [128, D], BF16) for i in range(2)]
    s_sb = [sb3(f"s_sb{i}", [128, M], F32) for i in range(2)]
    hbuf = sb3("hbuf", [128, D], F32)
    g0b = sb3("g0b", [128, M], F32)
    gsb = [sb3(f"gsb{i}", [128, M], BF16) for i in range(2)]
    gT = [sb3(f"gT{i}", [64, 128], BF16) for i in range(2)]
    outt = [sb3(f"outt{i}", [128, D], F32) for i in range(2)]
    sq3 = sb3("sq3", [128, D], F32)
    sq3b = sb3("sq3b", [128, D], F32)
    st3 = [sb3(f"st3_{i}", [128, 1], F32) for i in range(6)]
    st3g = sb3("st3g", [128, 1], F32)

    pO = p3.enter_context(nc.psum_tensor("pO", [128, 1024], F32))
    pF = p3.enter_context(nc.psum_tensor("pF", [128, 1024], F32))
    pS = p3.enter_context(nc.psum_tensor("pS", [128, 512], F32))
    ptp3 = [p3.enter_context(nc.psum_tensor(f"ptp3{i}", [128, 128], BF16))
            for i in range(2)]
    ptg = p3.enter_context(nc.psum_tensor("ptg", [128, 128], BF16))

    NT3 = NTQ

    V = {}
    cv = 0
    for t in range(NT3):
        if t >= 1:
            cv += 1; V[("sS", t - 1)] = cv
        if t >= 2:
            cv += 1; V[("gsb", t - 2)] = cv
        cv += 1; V[("a", t)] = cv
        cv += 1; V[("b", t)] = cv
        cv += 1; V[("ln2", t)] = cv
        if t >= 2:
            cv += 1; V[("cF", t - 2)] = cv
            cv += 1; V[("out", t - 2)] = cv
    cv += 1; V[("sS", NT3 - 1)] = cv
    cv += 1; V[("gsb", NT3 - 2)] = cv
    cv += 1; V[("cF", NT3 - 2)] = cv
    cv += 1; V[("out", NT3 - 2)] = cv
    cv += 1; V[("gsb", NT3 - 1)] = cv
    cv += 1; V[("cF", NT3 - 1)] = cv
    cv += 1; V[("out", NT3 - 1)] = cv

    Tt = {}
    ct = 0
    for t in range(NT3):
        ct += 1; Tt[("proj", t)] = ct
        if t >= 2:
            ct += 1; Tt[("ptg", t - 2)] = ct
            ct += 1; Tt[("wout", t - 2)] = ct
        for i in range(8):
            ct += 1; Tt[("tp", t, i)] = ct
        ct += 1; Tt[("win", t)] = ct
    ct += 1; Tt[("ptg", NT3 - 2)] = ct
    ct += 1; Tt[("wout", NT3 - 2)] = ct
    ct += 1; Tt[("ptg", NT3 - 1)] = ct
    ct += 1; Tt[("wout", NT3 - 1)] = ct

    Sc = {}
    Cc = {}
    cs3 = 0
    cc3 = 0
    for t in range(NT3):
        if t >= 2:
            cc3 += 1; Cc[("gt", t - 2)] = cc3
        cs3 += 1; Sc[("a", t)] = cs3
        cs3 += 1; Sc[("b", t)] = cs3
        for i in range(8):
            cc3 += 1; Cc[("tp", t, i)] = cc3
        if t >= 2:
            cs3 += 1; Sc[("c", t - 2)] = cs3
    cc3 += 1; Cc[("gt", NT3 - 2)] = cc3
    cs3 += 1; Sc[("c", NT3 - 2)] = cs3
    cc3 += 1; Cc[("gt", NT3 - 1)] = cc3
    cs3 += 1; Sc[("c", NT3 - 1)] = cs3

    Gg = {}
    cg3 = 0
    for t in range(NT3):
        cg3 += 1; Gg[("hb", t)] = cg3

    with nc.Block() as b3:
        @b3.sync
        def _(s):
            for t in range(min(2, NT3)):
                s.dma_start(out=x3t[t % 2][:, :],
                            in_=xb[t * 128:(t + 1) * 128, :]).then_inc(s3_x, 16)
            for t in range(NT3):
                if t + 2 < NT3:
                    s.wait_ge(s3_v, V[("b", t)])
                    s.dma_start(out=x3t[t % 2][:, :],
                                in_=xb[(t + 2) * 128:(t + 3) * 128, :]
                                ).then_inc(s3_x, 16)
                s.wait_ge(s3_v, V[("out", t)])
                s.dma_start(out=outh.ap()[t * 128:(t + 1) * 128, :],
                            in_=outt[t % 2][:, :]).then_inc(s_out, 16)
            s.wait_ge(s_out, 16 * NT3)

        @b3.tensor
        def _(tn):
            tn.wait_ge(s_w2, 16 * NW2)
            tn.wait_ge(s_at2, 16 * H * NQB)

            def emit_pw(tt):
                tn.wait_ge(s3_v, V[("gsb", tt)])
                if tt >= 1:
                    tn.wait_ge(s3_c, Cc[("gt", tt - 1)])
                tn.transpose(ptg[0:64, 0:128], gsb[tt % 2][:, :],
                             identb[:, :]).then_inc(s3_t, 1)
                tn.wait_ge(s3_c, Cc[("gt", tt)])
                if tt >= 1:
                    tn.wait_ge(s3_v, V[("cF", tt - 1)])
                tn.matmul(pF[:, 0:512], gT[tt % 2][:, :],
                          Wout_sb[0:64, 0:512], start=True, stop=True)
                tn.matmul(pF[:, 512:1024], gT[tt % 2][:, :],
                          Wout_sb[0:64, 512:1024],
                          start=True, stop=True).then_inc(s3_t, 1)

            for t in range(NT3):
                qb, tq = t // 4, t % 4
                if t >= 1:
                    tn.wait_ge(s3_v, V[("a", t - 1)])
                for hc in range(8):
                    lhs = attc[:, hc * NQ + qb * 512 + tq * 128:
                               hc * NQ + qb * 512 + tq * 128 + 128]
                    tn.matmul(pO[:, 0:512], lhs,
                              Wp_sb[:, hc * D:hc * D + 512],
                              start=(hc == 0), stop=(hc == 7))
                    ins = tn.matmul(pO[:, 512:1024], lhs,
                                    Wp_sb[:, hc * D + 512:hc * D + 1024],
                                    start=(hc == 0), stop=(hc == 7))
                ins.then_inc(s3_t, 1)
                if t >= 2:
                    emit_pw(t - 2)
                tn.wait_ge(s3_v, V[("ln2", t)])
                for i in range(8):
                    gi = 8 * t + i
                    if gi >= 2:
                        tt_, ii_ = (t, i - 2) if i >= 2 else (t - 1, i + 6)
                        tn.wait_ge(s3_c, Cc[("tp", tt_, ii_)])
                    tn.transpose(ptp3[i % 2][:, 0:128],
                                 ln2t[t % 2][:, i * 128:(i + 1) * 128],
                                 identb[:, :]).then_inc(s3_t, 1)
                tn.wait_ge(s3_c, Cc[("tp", t, 7)])
                if t >= 1:
                    tn.wait_ge(s3_v, V[("sS", t - 1)])
                for dc in range(8):
                    ins = tn.matmul(pS[:, 0:64],
                                    ln2T[t % 2][:, dc * 128:(dc + 1) * 128],
                                    Win_sb[:, dc * M:(dc + 1) * M],
                                    start=(dc == 0), stop=(dc == 7))
                ins.then_inc(s3_t, 1)
            emit_pw(NT3 - 2)
            emit_pw(NT3 - 1)

        @b3.vector
        def _(v):
            def ln_stats(src):
                v.reduce_sum(st3[4][:, :], src[:, :], axis=AX)
                v.scalar_tensor_tensor(sq3[:, :], src[:, :], 0.0, src[:, :],
                                       ALU.add, ALU.mult,
                                       accum_out=st3[5][:, :])
                v.tensor_scalar_mul(st3[1][:, :], st3[4][:, :], 1.0 / D)
                v.tensor_scalar_mul(st3[5][:, :], st3[5][:, :], 1.0 / D)
                v.tensor_mul(st3[4][:, :], st3[1][:, :], st3[1][:, :])
                v.tensor_sub(st3[5][:, :], st3[5][:, :], st3[4][:, :])
                return v.tensor_scalar_add(st3[5][:, :], st3[5][:, :], EPS)

            def emit_sS(tt):
                v.wait_ge(s3_t, Tt[("win", tt)])
                if tt >= 2:
                    v.wait_ge(s3_g, Gg[("hb", tt - 2)])
                v.tensor_add(s_sb[tt % 2][:, :], pS[:, 0:64],
                             bin_rep[:, :]).then_inc(s3_v, 1)

            def emit_R(tt):
                v.wait_ge(s3_g, Gg[("hb", tt)])
                if tt >= 2:
                    v.wait_ge(s3_t, Tt[("ptg", tt - 2)])
                v.scalar_tensor_tensor(hbuf[:, :], hbuf[:, :], 0.0,
                                       w2_rep[:, :], ALU.max, ALU.mult)
                v.reduce_sum(g0b[:, :],
                             bass.AP(hbuf, 0,
                                     [[D, 128], [INNER, M], [1, INNER]]),
                             axis=AX)
                v.tensor_copy(st3g[:, :], b2_rep[:, 0:1])
                v.tensor_add(gsb[tt % 2][:, :], g0b[:, :],
                             b2_rep[:, :]).then_inc(s3_v, 1)

            def emit_A(tt):
                v.wait_ge(s3_t, Tt[("proj", tt)])
                v.tensor_add(zt[:, :], pO[:, :], bp_rep[:, :])
                ln_stats(zt).then_inc(s3_v, 1)
                v.tensor_scalar_sub(sq3[:, :], zt[:, :], st3[1][:, :])
                v.wait_ge(s3_s, Sc[("a", tt)])
                v.wait_ge(s3_x, 16 * (tt + 1))
                v.tensor_scalar_mul(sq3[:, :], sq3[:, :], st3[2][:, :])
                v.tensor_add(x1t[tt % 3][:, :], x3t[tt % 2][:, :], sq3[:, :])
                ln_stats(x1t[tt % 3]).then_inc(s3_v, 1)
                v.tensor_scalar_sub(sq3[:, :], x1t[tt % 3][:, :], st3[1][:, :])
                v.wait_ge(s3_s, Sc[("b", tt)])
                if tt >= 2:
                    v.wait_ge(s3_t, Tt[("tp", tt - 2, 7)])
                v.tensor_scalar_mul(ln2t[tt % 2][:, :],
                                    sq3[:, :],
                                    st3[2][:, :]).then_inc(s3_v, 1)

            def emit_F(tt):
                v.wait_ge(s3_t, Tt[("wout", tt)])
                v.tensor_add(zt2[:, :], pF[:, :], bout_rep[:, :])
                ln_stats(zt2).then_inc(s3_v, 1)
                v.tensor_scalar_sub(sq3b[:, :], zt2[:, :], st3[1][:, :])
                v.wait_ge(s3_s, Sc[("c", tt)])
                if tt >= 2:
                    v.wait_ge(s_out, 16 * (tt - 1))
                v.tensor_scalar_mul(sq3b[:, :], sq3b[:, :], st3[2][:, :])
                v.tensor_add(outt[tt % 2][:, :], x1t[tt % 3][:, :],
                             sq3b[:, :]).then_inc(s3_v, 1)

            v.wait_ge(s_w2, 16 * NW2)
            emit_A(0)
            for t in range(1, NT3):
                emit_sS(t - 1)
                if t >= 2:
                    emit_R(t - 2)
                emit_A(t)
                if t >= 2:
                    emit_F(t - 2)
            emit_sS(NT3 - 1)
            emit_R(NT3 - 2)
            emit_F(NT3 - 2)
            emit_R(NT3 - 1)
            emit_F(NT3 - 1)

        @b3.scalar
        def _(sc):
            def rsqrt(v_need):
                sc.wait_ge(s3_v, v_need)
                sc.activation(out=st3[0][:, :], in_=st3[5][:, :], func=AF.Ln)
                sc.activation(out=st3[2][:, :], in_=st3[0][:, :],
                              func=AF.Exp, scale=-0.5).then_inc(s3_s, 1)

            def emit_gt(tt):
                sc.wait_ge(s3_t, Tt[("ptg", tt)])
                if tt >= 2:
                    sc.wait_ge(s3_t, Tt[("wout", tt - 2)])
                sc.activation(out=gT[tt % 2][:, :], in_=ptg[0:64, 0:128],
                              func=AF.Copy).then_inc(s3_c, 1)

            for t in range(NT3):
                if t >= 2:
                    emit_gt(t - 2)
                rsqrt(V[("a", t)])
                rsqrt(V[("b", t)])
                for i in range(8):
                    sc.wait_ge(s3_t, Tt[("tp", t, i)])
                    if t >= 2 and i == 0:
                        sc.wait_ge(s3_t, Tt[("win", t - 2)])
                    sc.activation(out=ln2T[t % 2][:, i * 128:(i + 1) * 128],
                                  in_=ptp3[i % 2][:, 0:128],
                                  func=AF.Copy).then_inc(s3_c, 1)
                if t >= 2:
                    rsqrt(V[("cF", t - 2)])
            emit_gt(NT3 - 2)
            rsqrt(V[("cF", NT3 - 2)])
            emit_gt(NT3 - 1)
            rsqrt(V[("cF", NT3 - 1)])

        @b3.gpsimd
        def _(g):
            g.wait_ge(s_w2, 16 * NW2)
            for t in range(NT3):
                g.wait_ge(s3_v, V[("sS", t)])
                if t >= 1:
                    g.wait_ge(s3_v, V[("gsb", t - 1)])
                g.tensor_tensor(hbuf[:, :],
                                bass.AP(s_sb[t % 2], 0,
                                        [[M, 128], [1, M], [0, INNER]]),
                                w1_rep[:, :], ALU.mult)
                g.tensor_add(hbuf[:, :], hbuf[:, :],
                             b1_rep[:, :]).then_inc(s3_g, 1)
            g.wait_ge(s_out, 16 * NT3)
            if debug:
                nd = 0
                for nm, src in (("d_qT", qT[:, 0:512]),
                                ("d_crep", crep[:, :]),
                                ("d_attc", attc[:, 0:512])):
                    g.dma_start(out=x3t[0][:, 0:src.shape[-1]], in_=src).then_inc(s_out, 16)
                    nd += 1
                    g.wait_ge(s_out, 16 * (NT3 + nd))
                    g.dma_start(out=dbgh[nm].ap()[0:src.shape[0], 0:src.shape[-1]],
                                in_=x3t[0][0:src.shape[0], 0:src.shape[-1]]).then_inc(s_out, 16)
                    nd += 1
                    g.wait_ge(s_out, 16 * (NT3 + nd))
                g.dma_start(out=dbgh["d_kT"].ap()[:, 0:512],
                            in_=kT[0:128, 0:512]).then_inc(s_out, 16)
                nd += 1
                g.wait_ge(s_out, 16 * (NT3 + nd))
                g.dma_start(out=dbgh["d_v"].ap()[:, 0:512],
                            in_=vh[0:128, 0:512]).then_inc(s_out, 16)
                nd += 1
                g.wait_ge(s_out, 16 * (NT3 + nd))
                g.dma_start(out=dbgh["d_kmax"].ap()[0:16, 0:1],
                            in_=kmax2[0:16, 0:1],
                            allow_slow_non_contiguous=True).then_inc(s_out, 16)
                nd += 1
                g.dma_start(out=dbgh["d_kmax"].ap()[0:16, 1:2],
                            in_=qmax2[0:16, 0:1],
                            allow_slow_non_contiguous=True).then_inc(s_out, 16)
                nd += 1
                g.dma_start(out=dbgh["d_cq"].ap()[:, :],
                            in_=dbg_cq[:, :]).then_inc(s_out, 16)
                nd += 1
                g.dma_start(out=dbgh["d_ck"].ap()[:, :],
                            in_=dbg_ck[:, :]).then_inc(s_out, 16)
                nd += 1
                g.dma_start(out=dbgh["d_q"].ap()[:, :],
                            in_=dbg_q[:, :]).then_inc(s_out, 16)
                nd += 1
                g.wait_ge(s_out, 16 * (NT3 + nd))

    p3.close()
    ctx.close()
    return nc


def kernel(L0_attn_query, L0_attn_Wq, L0_attn_Wk, L0_attn_Wv, L0_attn_Wp,
           L0_attn_bp, L0_ffn_Win, L0_ffn_bin_, L0_ffn_W1, L0_ffn_b1,
           L0_ffn_W2, L0_ffn_b2, L0_ffn_Wout, L0_ffn_bout):
    x = np.ascontiguousarray(np.asarray(L0_attn_query, dtype=np.float32))
    assert x.shape == (2, 8192, 1024)
    nc = build_nc(NK=8192, NQ=2048)
    weights = {
        "Wq": np.ascontiguousarray(np.asarray(L0_attn_Wq, np.float32)),
        "Wk": np.ascontiguousarray(np.asarray(L0_attn_Wk, np.float32)),
        "Wv": np.ascontiguousarray(np.asarray(L0_attn_Wv, np.float32)),
        "Wp": np.ascontiguousarray(np.asarray(L0_attn_Wp, np.float32)),
        "bp": np.ascontiguousarray(np.asarray(L0_attn_bp, np.float32)),
        "Win": np.ascontiguousarray(np.asarray(L0_ffn_Win, np.float32)),
        "binp": np.ascontiguousarray(np.asarray(L0_ffn_bin_, np.float32)),
        "W1": np.ascontiguousarray(np.asarray(L0_ffn_W1, np.float32)),
        "b1": np.ascontiguousarray(np.asarray(L0_ffn_b1, np.float32)),
        "W2": np.ascontiguousarray(np.asarray(L0_ffn_W2, np.float32)),
        "b2": np.ascontiguousarray(np.asarray(L0_ffn_b2, np.float32)),
        "Wout": np.ascontiguousarray(np.asarray(L0_ffn_Wout, np.float32)),
        "bout": np.ascontiguousarray(np.asarray(L0_ffn_bout, np.float32)),
    }
    in_maps = []
    for c in range(8):
        b, qq = c // 4, c % 4
        xr = np.roll(x[b], -qq * 2048, axis=0)
        m = {"xb": np.ascontiguousarray(xr)}
        m.update(weights)
        in_maps.append(m)
    res = run_bass_kernel_spmd(nc, in_maps, core_ids=list(range(8)))
    out = np.empty((2, 8192, 1024), np.float32)
    for c in range(8):
        b, qq = c // 4, c % 4
        out[b, qq * 2048:(qq + 1) * 2048] = res.results[c]["out"]
    return out
